# version 04
# baseline (speedup 1.0000x reference)
from collections import defaultdict
from contextlib import ExitStack

import numpy as np
import ml_dtypes
import concourse.bass as bass
import concourse.mybir as mybir
from concourse.bass_utils import run_bass_kernel_spmd
from concourse.replica_groups import maybe_share_collective_output_space

F32 = mybir.dt.float32
BF16 = mybir.dt.bfloat16
AF = mybir.ActivationFunctionType
ALU = mybir.AluOpType

P = 128
H = 4096
DSSM = 8192
NH = 64
CONV_DIM = 8448
PROJ_DIM = 16704
PROJ_PAD = 16768
NPT = PROJ_PAD // P
KC = H // P
T = 2048
TH = T + 3
NCH = 16
EPS = 1e-5
ARENA_BYTES = 196000

TAPS = False
PHASES = 99


class Sched:

    def __init__(self):
        self.q = defaultdict(list)
        self.no = defaultdict(int)
        self.nd = defaultdict(int)
        self.nde = {}
        self.waited = {}

    def op(self, st, fn):
        self.no[st] += 1
        self.q[st].append(("o", fn))
        return (st, "o", self.no[st])

    def dma(self, st, fn):
        ep = self.nde.get(st, 0)
        if self.nd[(st, ep)] + 16 > 59968:
            ep += 1
            self.nde[st] = ep
        self.nd[(st, ep)] += 16
        self.q[st].append(("d", fn, ep))
        return (st, "d", (ep, self.nd[(st, ep)]))

    def wait(self, st, tok):
        if tok is None:
            return
        tst, kind, cnt = tok
        if kind == "d":
            ep, c = cnt
            key = (st, tst, "d", ep)
            if self.waited.get(key, -1) >= c:
                return
            self.waited[key] = c
            self.q[st].append(("w", (tst, "d", ep, c)))
            return
        key = (st, tst, kind)
        if self.waited.get(key, -1) >= cnt:
            return
        self.waited[key] = cnt
        self.q[st].append(("w", (tst, kind, cnt)))

    def waits(self, st, toks):
        for tk in toks:
            self.wait(st, tk)

    def phase(self):
        self.marks = getattr(self, "marks", [])
        self.marks.append(({st: len(q) for st, q in self.q.items()},
                           dict(self.no), dict(self.nd), dict(self.nde),
                           dict(self.waited)))

    def rollback(self, k):
        lens, no, nd, nde, waited = self.marks[k]
        for st in list(self.q):
            del self.q[st][lens.get(st, 0):]
        self.no = defaultdict(int, no)
        self.nd = defaultdict(int, nd)
        self.nde = dict(nde)
        self.waited = dict(waited)

    def now(self, st, kind="o"):
        if kind == "d":
            ep = self.nde.get(st, 0)
            return (st, "d", (ep, self.nd[(st, ep)]))
        return (st, kind, self.no[st])

    def replay(self, block, engines, osem, dsem):
        for st, eng_attr in engines.items():
            items = self.q[st]
            if not items:
                continue

            def make(items=items, st=st):
                def run(eng):
                    ndma = defaultdict(int)
                    nop_ = 0
                    for item in items:
                        kind = item[0]
                        if kind == "w":
                            payload = item[1]
                            if payload[1] == "d":
                                tst, _, ep, cnt = payload
                                eng.wait_ge(dsem[(tst, ep)], cnt)
                            else:
                                tst, tkind, cnt = payload
                                eng.wait_ge(osem[tst], cnt)
                        elif kind == "o":
                            if nop_:
                                eng.wait_ge(osem[st], nop_)
                            item[1](eng).then_inc(osem[st], 1)
                            nop_ += 1
                        else:
                            _, payload, ep = item
                            payload(eng).then_inc(dsem[(st, ep)], 16)
                            ndma[ep] += 16
                            eng.wait_ge(dsem[(st, ep)], ndma[ep])
                return run

            getattr(block, eng_attr)(make())


def build():
    nc = bass.Bass("TRN2", target_bir_lowering=False)

    xh = nc.dram_tensor("xh", [H, TH], F32, kind="ExternalInput").ap()
    w_in = nc.dram_tensor("w_in", [NPT, P, H], F32, kind="ExternalInput").ap()
    w_out = nc.dram_tensor("w_out", [32, P, DSSM], F32, kind="ExternalInput").ap()
    w_base = nc.dram_tensor("w_base", [32, P, H], F32, kind="ExternalInput").ap()
    acat = nc.dram_tensor("acat", [P, H], F32, kind="ExternalInput").ap()
    bcat = nc.dram_tensor("bcat", [P, H], F32, kind="ExternalInput").ap()
    wrl = nc.dram_tensor("wrl", [P, 256], F32, kind="ExternalInput").ap()
    convw = nc.dram_tensor("convw", [P, 264], F32, kind="ExternalInput").ap()
    dtbias = nc.dram_tensor("dtbias", [64, 1], F32, kind="ExternalInput").ap()
    alog = nc.dram_tensor("alog", [64, 1], F32, kind="ExternalInput").ap()
    dprep = nc.dram_tensor("dprep", [P, 64], F32, kind="ExternalInput").ap()
    gammah = nc.dram_tensor("gammah", [P, 64], F32, kind="ExternalInput").ap()
    bbase = nc.dram_tensor("bbase", [P, 32], F32, kind="ExternalInput").ap()
    e8 = nc.dram_tensor("e8", [8, P], BF16, kind="ExternalInput").ap()
    identf_d = nc.dram_tensor("identf", [P, P], F32, kind="ExternalInput").ap()
    identb_d = nc.dram_tensor("identb", [P, P], BF16, kind="ExternalInput").ap()
    amask = nc.dram_tensor("amask", [64, 12], F32, kind="ExternalInput").ap()

    out = nc.dram_tensor("out", [32, P, T], F32, kind="ExternalOutput").ap()

    projs = nc.dram_tensor("projs", [NPT, P, TH], BF16, kind="Internal").ap()
    convx = nc.dram_tensor("convx", [64, P, T], BF16, kind="Internal").ap()
    xtds = nc.dram_tensor("xtds", [NCH, 64, P, P], BF16, kind="Internal").ap()
    ynorm = nc.dram_tensor("ynorm", [64, P, T], BF16, kind="Internal").ap()
    ffnout = nc.dram_tensor("ffnout", [32, P, T], F32, kind="Internal").ap()
    rg = [[0, 1, 2, 3], [4, 5, 6, 7]]
    ag_space = maybe_share_collective_output_space("AllGather", rg)
    AGB = 64 * P + 64
    ag_in = nc.dram_tensor("ag_in", [AGB, P], F32, kind="Internal").ap()
    ag_out = nc.dram_tensor("ag_out", [4 * AGB, P], F32, kind="Internal",
                            addr_space=ag_space).ap()

    taps = {}
    if TAPS:
        for nm, shp in [("tp_xn", [P, TH]), ("tp_proj", [P, TH]),
                        ("tp_dtsp", [64, T]), ("tp_da", [64, T]),
                        ("tp_cd", [64, T]), ("tp_h0", [P, P]),
                        ("tp_hg0", [P, P]), ("tp_cvx", [P, T]),
                        ("tp_xtd", [P, P]), ("tp_yn", [P, T]),
                        ("tp_ffn", [P, T]), ("tp_rstd", [P, T]),
                        ("tp_hw", [P, T]), ("tp_d", [P, 17]),
                        ("tp_lt", [P, P]), ("tp_mt", [P, P]),
                        ("tp_mi", [P, P]), ("tp_y1", [P, P]),
                        ("tp_g", [P, P]), ("tp_ab", [P, P]),
                        ("tp_ps", [P, P]), ("tp_hf", [P, P]),
                        ("tp_xt2", [P, P])]:
            taps[nm] = nc.dram_tensor(nm, shp, F32, kind="ExternalOutput").ap()

    S = Sched()

    with ExitStack() as ctx:
        e = ctx.enter_context
        arB = e(nc.sbuf_tensor("arB", [P, ARENA_BYTES // 2], BF16))
        trD = e(nc.sbuf_tensor("trD", [P, 4352], BF16, is_transpose_dest=True))
        FV = arB[:, :].bitcast(F32)

        pb = [e(nc.psum_tensor(f"pb{i}", [P, 512], F32)) for i in range(7)]
        pbf = e(nc.psum_tensor("pbf", [P, 1024], BF16))

        semT = e(nc.semaphore("semT"))
        semV = e(nc.semaphore("semV"))
        semA = e(nc.semaphore("semA"))
        semG = e(nc.semaphore("semG"))
        dsem = {}
        for stn in ["s", "a", "g"]:
            for ep in range(3):
                dsem[(stn, ep)] = e(nc.semaphore(f"dma{stn}{ep}"))
        block = e(nc.Block())
        osem = {"t": semT, "v": semV, "a": semA, "g": semG, "s": semT}

        class Ar:
            def __init__(self):
                self.off = 0

            def al(self, nbytes, align=64):
                self.off = (self.off + align - 1) // align * align
                assert self.off + nbytes <= ARENA_BYTES, \
                    f"arena overflow {self.off}+{nbytes}"
                o = self.off
                self.off += nbytes
                return o

            def mark(self):
                return self.off

            def reset(self, m):
                self.off = m

        A_ = Ar()

        def fs(off, rows, cols):
            return FV[0:rows, off // 4: off // 4 + cols]

        def bs(off, rows, cols):
            return arB[0:rows, off // 2: off // 2 + cols]

        o_identF = A_.al(P * 4)
        o_onesF = A_.al(P * 4)
        o_gamma = A_.al(64 * 4)
        o_dprep = A_.al(64 * 4)
        o_bb = A_.al(32 * 4)
        o_dtbias = A_.al(4)
        o_alog = A_.al(4)
        o_negeA = A_.al(4)
        o_amask = A_.al(48)
        o_eps = A_.al(4)
        o_one1 = A_.al(4)
        o_convw = A_.al(264 * 4)
        o_identB = A_.al(P * 2)
        o_e8 = A_.al(P * 2)
        o_onescB = A_.al(2)
        identF = fs(o_identF, P, P)
        onesF = fs(o_onesF, P, P)
        identB = bs(o_identB, P, P)
        E8t = bs(o_e8, 8, P)
        onescB = bs(o_onescB, P, 1)
        epsA = fs(o_eps, P, 1)
        one1 = fs(o_one1, P, 1)
        convW = fs(o_convw, P, 264)
        gammaT = fs(o_gamma, P, 64)
        dprepT = fs(o_dprep, P, 64)
        bbT = fs(o_bb, P, 32)

        ld0 = []
        for dst, src in [(identF, identf_d), (identB, identb_d), (E8t, e8),
                         (gammaT, gammah), (dprepT, dprep), (bbT, bbase),
                         (fs(o_dtbias, 64, 1), dtbias),
                         (fs(o_alog, 64, 1), alog),
                         (fs(o_amask, 64, 12), amask), (convW, convw)]:
            ld0.append(S.dma("s", lambda s, dst=dst, src=src: s.dma_start(
                out=dst, in_=src)))
        S.waits("v", ld0)
        S.op("v", lambda v: v.memset(onesF[:], 1.0))
        S.op("v", lambda v: v.memset(onescB[:], 1.0))
        S.op("v", lambda v: v.memset(epsA[:], EPS))
        tk_one = S.op("v", lambda v: v.memset(one1[:], 1.0))

        o_rstdB = A_.al(T * 4)
        rstdB = fs(o_rstdB, P, T)
        M0 = A_.mark()
        o_dtres = A_.al(TH * 4)
        dtres = fs(o_dtres, 64, TH)
        M1 = A_.mark()

        o_xn = A_.al(KC * TH * 2)
        MLN = A_.mark()
        o_xbf = [A_.al(TH * 2) for _ in range(2)]
        o_xsq = A_.al(TH * 2)
        o_stat = A_.al(80 * 4)
        o_drep = A_.al(P * 2)

        def xns(kc, c0, w):
            return arB[:, o_xn // 2 + kc * TH + c0: o_xn // 2 + kc * TH + c0 + w]

        NG = 17
        gw = [128] * 16 + [3]
        gof = [128 * i for i in range(17)]
        pstats = pb[6]
        o_macc = A_.al(40 * 4)
        o_sacc = A_.al(40 * 4)
        macc = fs(o_macc, P, 17)
        sacc = fs(o_sacc, P, 17)
        S.op("v", lambda v: v.memset(macc, 0.0))
        tk_z = S.op("v", lambda v: v.memset(sacc, 0.0))
        last_ld = [None, None]
        stats_done = None
        prev_acc = tk_z
        for kc in range(KC):
            buf = kc % 2
            xb = bs(o_xbf[buf], P, TH)
            S.wait("g", last_ld[buf])
            tk_ld = S.dma("g", lambda g, kc=kc, xb=xb: g.dma_start(
                out=xb, in_=xh[kc * P:(kc + 1) * P, :]))
            S.waits("a", [tk_ld, stats_done])
            tk_sq = S.op("a", lambda a, xb=xb: a.activation(
                bs(o_xsq, P, TH), xb, AF.Square))
            S.waits("t", [tk_ld, tk_sq, tk_one, prev_acc])
            for g in range(NG):
                S.op("t", lambda t, kc=kc, g=g, xb=xb: t.matmul(
                    pstats[0:gw[g], g:g + 1],
                    xb[:, gof[g]:gof[g] + gw[g]], onescB[:],
                    start=True, stop=True))
                stats_done = S.op("t", lambda t, kc=kc, g=g: t.matmul(
                    pstats[0:gw[g], 32 + g:33 + g],
                    bs(o_xsq, P, TH)[:, gof[g]:gof[g] + gw[g]], onescB[:],
                    start=True, stop=True))
            last_ld[buf] = S.now("t")
            S.wait("v", stats_done)
            S.op("v", lambda v: v.tensor_tensor(
                macc[:, 0:16], macc[:, 0:16], pstats[:, 0:16], ALU.add))
            S.op("v", lambda v: v.tensor_tensor(
                macc[0:3, 16:17], macc[0:3, 16:17], pstats[0:3, 16:17], ALU.add))
            S.op("v", lambda v: v.tensor_tensor(
                sacc[:, 0:16], sacc[:, 0:16], pstats[:, 32:48], ALU.add))
            prev_acc = S.op("v", lambda v: v.tensor_tensor(
                sacc[0:3, 16:17], sacc[0:3, 16:17], pstats[0:3, 48:49], ALU.add))
        mmv = fs(o_stat, P, 17)
        vv = fs(o_stat + 17 * 4, P, 17)
        dd = fs(o_stat + 34 * 4, P, 17)
        md = fs(o_stat + 51 * 4, P, 17)
        S.wait("v", prev_acc)
        tk1 = S.op("v", lambda v: v.tensor_scalar(
            mmv, macc, 1.0 / H, None, op0=ALU.mult))
        S.wait("a", tk1)
        tk2 = S.op("a", lambda a: a.activation(dd, mmv, AF.Square))
        S.wait("v", tk2)
        tk3 = S.op("v", lambda v: v.scalar_tensor_tensor(
            vv, sacc, 1.0 / H, dd, op0=ALU.mult, op1=ALU.subtract))
        S.wait("a", tk3)
        S.op("a", lambda a: a.activation(dd, vv, AF.Ln, bias=epsA[:, 0:1]))
        tk_d = S.op("a", lambda a: a.activation(dd, dd, AF.Exp, scale=-0.5))
        S.wait("v", tk_d)
        tk_md = S.op("v", lambda v: v.tensor_tensor(md, mmv, dd, ALU.mult))
        drep = bs(o_drep, P, P)
        t_db_last = None
        prev_tr = None
        for g in range(NG):
            S.waits("v", [tk_md, prev_tr])
            t1 = S.op("v", lambda v, g=g: v.tensor_scalar(
                drep, onesF[:], dd[:, g:g + 1], None, op0=ALU.mult))
            S.wait("s", t1)
            t_db = S.dma("s", lambda s, g=g: s.dma_start_transpose(
                out=trD[:, g * P:g * P + P], in_=drep))
            S.wait("v", t_db)
            t2 = S.op("v", lambda v, g=g: v.tensor_scalar(
                drep, onesF[:], md[:, g:g + 1], None, op0=ALU.mult))
            S.wait("s", t2)
            t_db_last = prev_tr = S.dma("s", lambda s, g=g: s.dma_start_transpose(
                out=trD[:, 2176 + g * P:2176 + g * P + P], in_=drep))
        o_tmpF = A_.al(P * 4)
        xn_tok = None
        for kc in range(KC):
            buf = kc % 2
            xb = bs(o_xbf[buf], P, TH)
            S.wait("g", last_ld[buf])
            tk_ld = S.dma("g", lambda g, kc=kc, xb=xb: g.dma_start(
                out=xb, in_=xh[kc * P:(kc + 1) * P, :]))
            S.waits("v", [tk_ld, t_db_last])
            for g in range(NG):
                tmp = fs(o_tmpF, P, P)
                S.op("v", lambda v, g=g, xb=xb, tmp=tmp: v.tensor_tensor(
                    tmp[:, 0:gw[g]], xb[:, gof[g]:gof[g] + gw[g]],
                    trD[:, g * P:g * P + gw[g]], ALU.mult))
                xn_tok = S.op("v", lambda v, kc=kc, g=g, tmp=tmp: v.tensor_tensor(
                    xns(kc, gof[g], gw[g]), tmp[:, 0:gw[g]],
                    trD[:, 2176 + g * P:2176 + g * P + gw[g]], ALU.subtract))
            last_ld[buf] = S.now("v")
        if TAPS:
            S.wait("g", xn_tok)
            S.dma("g", lambda g: g.dma_start(
                out=taps["tp_xn"][0:P, 0:TH], in_=xns(0, 0, TH)))
            S.dma("g", lambda g: g.dma_start(
                out=taps["tp_d"], in_=fs(o_stat + 34 * 4, P, 17)))

        S.phase()
        A_.reset(MLN)
        o_wbuf = [A_.al(H * 2) for _ in range(2)]
        o_pst = [A_.al(512 * 2) for _ in range(4)]
        PG = [(0, 411), (411, 411), (822, 411), (1233, 411), (1644, 407)]
        w_ld = [None, None]
        mm_done = [None, None]
        prev_evac = None
        st_tok = [None, None, None, None]
        S.wait("g", xn_tok)
        for wt in range(NPT):
            buf = wt % 2
            wb = bs(o_wbuf[buf], P, H)
            S.wait("g", mm_done[buf])
            w_ld[buf] = S.dma("g", lambda g, wt=wt, wb=wb: g.dma_start(
                out=wb, in_=w_in[wt]))
            S.waits("t", [w_ld[buf], xn_tok, prev_evac])
            for kc in range(KC):
                for gi, (c0, wd) in enumerate(PG):
                    S.op("t", lambda t, kc=kc, gi=gi, c0=c0, wd=wd, wb=wb:
                         t.matmul(pb[gi][:, 0:wd], wb[:, kc * P:(kc + 1) * P],
                                  xns(kc, c0, wd),
                                  start=(kc == 0), stop=(kc == KC - 1)))
            mm_done[buf] = S.now("t")
            S.wait("v", mm_done[buf])
            if wt == NPT - 1:
                for gi, (c0, wd) in enumerate(PG):
                    S.op("v", lambda v, gi=gi, c0=c0, wd=wd: v.tensor_copy(
                        dtres[0:64, c0:c0 + wd], pb[gi][0:64, 0:wd]))
            else:
                for gi, (c0, wd) in enumerate(PG):
                    sb = bs(o_pst[gi % 4], P, 512)
                    S.wait("v", st_tok[gi % 4])
                    S.op("v", lambda v, gi=gi, wd=wd, sb=sb: v.tensor_copy(
                        sb[:, 0:wd], pb[gi][:, 0:wd]))
                    tkv = S.now("v")
                    S.wait("s", tkv)
                    st_tok[gi % 4] = S.dma(
                        "s", lambda s, wt=wt, c0=c0, wd=wd, sb=sb: s.dma_start(
                            out=projs[wt][:, c0:c0 + wd], in_=sb[:, 0:wd]))
            prev_evac = S.now("v")
        proj_v = S.now("v")
        proj_t = S.now("t")
        proj_s = S.now("s", "d")
        if TAPS:
            S.wait("g", proj_s)
            S.dma("g", lambda g: g.dma_start(out=taps["tp_proj"], in_=projs[64]))

        S.phase()
        A_.reset(MLN)
        o_acat = A_.al(H * 2)
        o_bcat = A_.al(H * 2)
        o_wrl = A_.al(256 * 2)
        o_hw = A_.al(T * 2)
        o_wbb = [A_.al(H * 2) for _ in range(2)]
        o_crev = A_.al(P * 2)
        o_combB = A_.al(8 * P * 2)
        o_rt = A_.al(48 * 4)
        o_fstF = [A_.al(512 * 4) for _ in range(2)]

        S.waits("g", [proj_t, proj_v, proj_s])
        tk_ac = S.dma("g", lambda g: g.dma_start(out=bs(o_acat, P, H), in_=acat))
        tk_bc = S.dma("g", lambda g: g.dma_start(out=bs(o_bcat, P, H), in_=bcat))
        tk_wr = S.dma("g", lambda g: g.dma_start(out=bs(o_wrl, P, 256), in_=wrl))
        pe = fs(o_rt, P, 8)
        zap = fs(o_rt + 32, P, 8)
        comb = fs(o_rt + 64, P, 8)
        den = fs(o_rt + 96, P, 1)
        mrep = fs(o_rt + 128, P, 8)
        hw_tok = None
        prev_pe = None
        prev_hw = None
        prev_ct = None
        prev_cr = None
        S.waits("t", ld0)
        for g16 in range(16):
            tc0 = 3 + g16 * P
            S.waits("t", [tk_wr, xn_tok, prev_pe])
            for kc in range(KC):
                S.op("t", lambda t, kc=kc, tc0=tc0: t.matmul(
                    pb[5][:, 0:8], xns(kc, tc0, P),
                    bs(o_wrl, P, 256)[:, kc * 8:kc * 8 + 8],
                    start=(kc == 0), stop=(kc == KC - 1)))
            tk_lg = S.now("t")
            S.wait("a", tk_lg)
            prev_pe = tk_pe = S.op("a", lambda a: a.activation(
                pe, pb[5][:, 0:8], AF.Exp))
            S.wait("v", tk_pe)
            S.op("v", lambda v: v.max(out=mrep, in_=pe))
            S.op("v", lambda v: v.engine_nop())
            tk_den = S.op("v", lambda v: v.tensor_scalar(
                den, mrep[:, 0:1], mrep[:, 1:2], None, op0=ALU.add))
            S.wait("a", tk_den)
            S.op("a", lambda a: a.activation(den, den, AF.Ln))
            tk_rd = S.op("a", lambda a: a.activation(den, den, AF.Exp, scale=-1.0))
            S.op("v", lambda v: v.memset(mrep[:, 2:8], 0.0))
            S.op("v", lambda v: v.tensor_copy(zap, pe))
            S.op("v", lambda v: v.match_replace(out=zap, in_to_replace=mrep,
                                                in_values=zap, imm_value=0.0))
            S.op("v", lambda v: v.engine_nop())
            S.op("v", lambda v: v.tensor_tensor(comb, pe, zap, ALU.subtract))
            S.wait("v", tk_rd)
            tk_cb = S.op("v", lambda v: v.tensor_scalar(
                comb, comb, den[:, 0:1], None, op0=ALU.mult))
            S.waits("t", [tk_cb, prev_ct])
            tk_ct = S.op("t", lambda t: t.transpose(pb[6][0:8, 0:P], comb, identF))
            S.wait("v", tk_ct)
            prev_ct = tk_ctv = S.op("v", lambda v: v.tensor_copy(
                bs(o_combB, 8, P), pb[6][0:8, 0:P]))
            S.waits("t", [tk_ctv, prev_cr])
            tk_cr = S.op("t", lambda t: t.matmul(
                pb[6][:, P:2 * P], E8t, bs(o_combB, 8, P), start=True, stop=True))
            S.wait("v", tk_cr)
            prev_cr = tk_crv = S.op("v", lambda v: v.tensor_copy(
                bs(o_crev, P, P), pb[6][:, P:2 * P]))
            S.waits("t", [tk_ac, prev_hw, tk_pe])
            for kc in range(KC):
                S.op("t", lambda t, kc=kc, tc0=tc0: t.matmul(
                    pb[5][:, P:2 * P], bs(o_acat, P, H)[:, kc * P:(kc + 1) * P],
                    xns(kc, tc0, P), start=(kc == 0), stop=(kc == KC - 1)))
            tk_h = S.now("t")
            S.waits("v", [tk_h, tk_crv])
            prev_hw = hw_tok = S.op("v", lambda v, g16=g16: v.tensor_tensor(
                bs(o_hw, P, T)[:, g16 * P:(g16 + 1) * P],
                pb[5][:, P:2 * P], bs(o_crev, P, P), ALU.mult))
        if TAPS:
            S.wait("g", hw_tok)
            S.dma("g", lambda g: g.dma_start(out=taps["tp_hw"], in_=bs(o_hw, P, T)))
        wb_ld = [None, None]
        wb_done = [None, None]
        prev_fevac = None
        fst_tok = [None, None]
        for ot in range(32):
            buf = ot % 2
            wbb = bs(o_wbb[buf], P, H)
            S.wait("g", wb_done[buf])
            wb_ld[buf] = S.dma("g", lambda g, ot=ot, wbb=wbb: g.dma_start(
                out=wbb, in_=w_base[ot]))
            S.waits("t", [wb_ld[buf], hw_tok, tk_bc, prev_fevac])
            for kc in range(KC):
                for g4 in range(4):
                    S.op("t", lambda t, kc=kc, g4=g4, wbb=wbb: t.matmul(
                        pb[g4][:, 0:512], wbb[:, kc * P:(kc + 1) * P],
                        xns(kc, 3 + g4 * 512, 512), start=(kc == 0), stop=False))
            for g4 in range(4):
                S.op("t", lambda t, ot=ot, g4=g4: t.matmul(
                    pb[g4][:, 0:512], bs(o_bcat, P, H)[:, ot * P:(ot + 1) * P],
                    bs(o_hw, P, T)[:, g4 * 512:(g4 + 1) * 512],
                    start=False, stop=True))
            wb_done[buf] = S.now("t")
            S.wait("v", wb_done[buf])
            for g4 in range(4):
                fb = g4 % 2
                fsb = fs(o_fstF[fb], P, 512)
                S.wait("v", fst_tok[fb])
                S.op("v", lambda v, ot=ot, g4=g4, fsb=fsb: v.tensor_scalar(
                    fsb, pb[g4][:, 0:512], bbT[:, ot:ot + 1], None, op0=ALU.add))
                tkv = S.now("v")
                S.wait("s", tkv)
                fst_tok[fb] = S.dma("s", lambda s, ot=ot, g4=g4, fsb=fsb:
                                    s.dma_start(out=ffnout[ot]
                                                [:, g4 * 512:(g4 + 1) * 512],
                                                in_=fsb))
            prev_fevac = S.now("v")
        ffn_s = S.now("s", "d")
        if TAPS:
            S.wait("g", ffn_s)
            S.dma("g", lambda g: g.dma_start(out=taps["tp_ffn"], in_=ffnout[0]))

        S.phase()
        A_.reset(M1)
        o_dtsp = A_.al(T * 4)
        o_dA = A_.al(T * 4)
        o_cd = A_.al(T * 4)
        o_sm = A_.al(1060 * 4)
        o_tT = A_.al(3 * NCH * 64 * 4)
        o_Hst = A_.al(NH * P * 4)
        o_Bc = A_.al(NCH * P * 2)
        o_BTc = A_.al(NCH * P * 2)
        o_Cc = A_.al(NCH * P * 2)
        o_Hg = A_.al(NH * P * 2)
        MSC = A_.mark()
        dtsp = fs(o_dtsp, 64, T)
        dAt = fs(o_dA, 64, T)
        cdt = fs(o_cd, 64, T)
        clc = fs(o_sm, 64, P)
        wsufc = fs(o_sm + 128 * 4, 64, P)
        zeros64 = fs(o_sm + 256 * 4, 64, P)
        Pg = fs(o_sm + 384 * 4, 64, 16)
        Pincl = fs(o_sm + 400 * 4, 64, 16)
        Pex = fs(o_sm + 416 * 4, 64, 16)
        o_agc = o_sm + 432 * 4
        negeA = fs(o_negeA, 64, 1)

        def Hs(h):
            return FV[:, o_Hst // 4 + h * P: o_Hst // 4 + (h + 1) * P]

        def Hgs(h):
            return arB[:, o_Hg // 2 + h * P: o_Hg // 2 + (h + 1) * P]

        def dtTs(c):
            return fs(o_tT + c * 64 * 4, P, 64)

        def wsufTs(c):
            return fs(o_tT + (NCH + c) * 64 * 4, P, 64)

        def DrepTs(c):
            return fs(o_tT + (2 * NCH + c) * 64 * 4, P, 64)

        S.wait("v", proj_v)
        S.op("v", lambda v: v.memset(zeros64[:], 0.0))
        tk0 = S.op("v", lambda v: v.tensor_scalar(
            dtsp, dtres[0:64, 3:3 + T], fs(o_dtbias, 64, 1)[:, 0:1],
            None, op0=ALU.add))
        S.wait("a", tk0)
        S.op("a", lambda a: a.activation(dAt, dtsp, AF.Abs))
        S.op("a", lambda a: a.activation(dAt, dAt, AF.Exp, scale=-1.0))
        S.op("a", lambda a: a.activation(dAt, dAt, AF.Ln, bias=one1[0:64, 0:1]))
        tk_r = S.op("a", lambda a: a.activation(cdt, dtsp, AF.Relu))
        S.wait("v", tk_r)
        S.op("v", lambda v: v.tensor_tensor(dtsp, cdt, dAt, ALU.add))
        tk_sp = S.now("v")
        S.wait("a", tk_sp)
        tk_ea = S.op("a", lambda a: a.activation(negeA, fs(o_alog, 64, 1), AF.Exp))
        S.wait("v", tk_ea)
        S.op("v", lambda v: v.tensor_scalar(negeA, negeA, -1.0, None, op0=ALU.mult))
        tk_la = S.op("v", lambda v: v.tensor_scalar(
            dAt, dtsp, negeA[:, 0:1], None, op0=ALU.mult))
        tk_lacp = S.op("v", lambda v: v.tensor_copy(cdt, dAt))
        S.wait("a", tk_lacp)
        tk_da = S.op("a", lambda a: a.activation(dAt, dAt, AF.Exp))
        S.wait("v", tk_da)
        tr_ev = None
        for c in range(NCH):
            ch = slice(c * P, (c + 1) * P)
            S.op("v", lambda v, ch=ch: v.tensor_tensor_scan(
                clc, onesF[0:64, :], cdt[:, ch], 0.0, op0=ALU.mult, op1=ALU.add))
            S.op("v", lambda v: v.engine_nop())
            tk_cl = S.now("v")
            S.wait("a", tk_cl)
            tk_ws = S.op("a", lambda a: a.activation(
                wsufc, clc, AF.Exp, bias=clc[:, 127:128], scale=-1.0))
            S.op("v", lambda v, ch=ch: v.tensor_tensor_scan(
                cdt[:, ch], dAt[:, ch], zeros64[:], 1.0, op0=ALU.mult, op1=ALU.add))
            S.op("v", lambda v: v.engine_nop())
            S.op("v", lambda v, c=c: v.tensor_copy(
                Pg[:, c:c + 1], cdt[:, c * P + 127:c * P + 128]))
            dmt = fs(o_sm + 512 * 4, 64, P)
            S.waits("v", [tk_ws])
            tk_dm = S.op("v", lambda v, c=c, dmt=dmt: v.tensor_scalar(
                dmt, onesF[0:64, :], cdt[:, c * P + 127:c * P + 128], None,
                op0=ALU.mult))
            S.waits("t", [tk_ws, tk_dm, tr_ev])
            S.op("t", lambda t, ch=ch: t.transpose(
                pb[6][:, 0:64], dtsp[:, ch], identF[0:64, 0:64]))
            S.op("t", lambda t: t.transpose(
                pb[6][:, 64:128], wsufc, identF[0:64, 0:64]))
            tk_tr = S.op("t", lambda t, dmt=dmt: t.transpose(
                pb[6][:, 128:192], dmt, identF[0:64, 0:64]))
            S.wait("v", tk_tr)
            S.op("v", lambda v, c=c: v.tensor_copy(dtTs(c), pb[6][:, 0:64]))
            S.op("v", lambda v, c=c: v.tensor_copy(wsufTs(c), pb[6][:, 64:128]))
            S.op("v", lambda v, c=c: v.tensor_copy(DrepTs(c), pb[6][:, 128:192]))
            tr_ev = S.now("v")
            S.wait("a", tk_tr)
        S.op("v", lambda v: v.tensor_tensor_scan(
            Pincl, Pg, zeros64[:, 0:16], 1.0, op0=ALU.mult, op1=ALU.add))
        S.op("v", lambda v: v.engine_nop())
        S.op("v", lambda v: v.memset(Pex[:, 0:1], 1.0))
        tk_pex = S.op("v", lambda v: v.tensor_copy(Pex[:, 1:16], Pincl[:, 0:15]))
        if TAPS:
            S.wait("g", tk_pex)
            S.dma("g", lambda g: g.dma_start(out=taps["tp_dtsp"], in_=dtsp))
            S.dma("g", lambda g: g.dma_start(out=taps["tp_da"], in_=dAt))
            S.dma("g", lambda g: g.dma_start(out=taps["tp_cd"], in_=cdt))

        o_xbcst = A_.al(66 * 131 * 2)
        o_cvx = A_.al(64 * P * 2)
        o_cacc = A_.al(3 * P * 4)
        o_xtdb = A_.al(4 * P * 2)

        def cvxs(ti):
            return arB[:, o_cvx // 2 + ti * P: o_cvx // 2 + (ti + 1) * P]

        tk_h0 = S.op("v", lambda v: v.memset(FV[:, o_Hst // 4:o_Hst // 4 + NH * P], 0.0))
        gate_v = tk_h0
        gate_t = None
        gate_a = None
        xtd_cons = defaultdict(lambda: None)
        xps_cons = defaultdict(lambda: None)
        sps_cons = defaultdict(lambda: None)
        for c in range(NCH):
            S.waits("g", [gate_v, gate_t, gate_a, proj_s])
            rd = []
            for ti in range(66):
                rd.append(S.dma("g", lambda g, c=c, ti=ti: g.dma_start(
                    out=bs(o_xbcst + ti * 131 * 2, P, 131),
                    in_=projs[64 + ti][:, c * P:c * P + 131])))
            rd_done = rd[-1]
            cvt = []
            prev_a1 = None
            for ti in range(66):
                acc = fs(o_cacc + (ti % 2) * P * 4, P, P)
                gt = fs(o_cacc + 2 * P * 4, P, P)
                S.waits("v", [rd_done, prev_a1])
                S.op("v", lambda v, c=c, ti=ti, acc=acc: v.tensor_scalar(
                    acc, bs(o_xbcst + ti * 131 * 2, P, 131)[:, 0:P],
                    convW[:, ti * 4:ti * 4 + 1], None, op0=ALU.mult))
                for k in range(1, 4):
                    S.op("v", lambda v, ti=ti, k=k, acc=acc: v.scalar_tensor_tensor(
                        acc, bs(o_xbcst + ti * 131 * 2, P, 131)[:, k:k + P],
                        convW[:, ti * 4 + k:ti * 4 + k + 1], acc,
                        op0=ALU.mult, op1=ALU.add))
                tk_acc = S.now("v")
                S.wait("a", tk_acc)
                prev_a1 = S.op("a", lambda a, acc=acc, gt=gt: a.activation(
                    gt, acc, AF.Exp, scale=-1.0))
                S.op("a", lambda a, gt=gt: a.activation(
                    gt, gt, AF.Ln, bias=one1[:, 0:1]))
                tk_g3 = S.op("a", lambda a, gt=gt: a.activation(
                    gt, gt, AF.Exp, scale=-1.0))
                if ti < 64:
                    dst = cvxs(ti)
                elif ti == 64:
                    dst = bs(o_Bc + c * P * 2, P, P)
                else:
                    dst = bs(o_Cc + c * P * 2, P, P)
                S.wait("v", tk_g3)
                tk_cv = S.op("v", lambda v, dst=dst, acc=acc, gt=gt: v.tensor_tensor(
                    dst, acc, gt, ALU.mult))
                cvt.append(tk_cv)
                if ti < 64:
                    S.wait("s", tk_cv)
                    S.dma("s", lambda s, c=c, ti=ti: s.dma_start(
                        out=convx[ti][:, c * P:(c + 1) * P], in_=cvxs(ti)))
            S.waits("t", [cvt[64], xps_cons[("bt",)]])
            tk_btp = S.op("t", lambda t, c=c: t.transpose(
                pbf[:, 512:640], bs(o_Bc + c * P * 2, P, P), identB))
            S.wait("v", tk_btp)
            tk_bt = S.op("v", lambda v, c=c: v.tensor_copy(
                bs(o_BTc + c * P * 2, P, P), pbf[:, 512:640]))
            xps_cons[("bt",)] = tk_bt
            for h in range(NH):
                sl = h % 4
                xps = pbf[:, sl * P:(sl + 1) * P]
                S.waits("t", [cvt[h], xps_cons[sl]])
                tk_xt = S.op("t", lambda t, h=h, xps=xps: t.transpose(
                    xps, cvxs(h), identB))
                xtd = bs(o_xtdb + sl * P * 2, P, P)
                S.waits("v", [tk_xt, xtd_cons[sl]])
                tk_xtd = S.op("v", lambda v, h=h, c=c, xtd=xtd, xps=xps:
                              v.tensor_scalar(xtd, xps, dtTs(c)[:, h:h + 1],
                                              None, op0=ALU.mult))
                xps_cons[sl] = tk_xtd
                S.wait("s", tk_xtd)
                tk_wxtd = S.dma("s", lambda s, c=c, h=h, xtd=xtd: s.dma_start(
                    out=xtds[c][h], in_=xtd))
                btw = bs(o_xbcst + (h % 2) * 131 * 2, P, P)
                S.waits("v", [tk_bt, sps_cons[("btw", h % 2)]])
                tk_btw = S.op("v", lambda v, c=c, h=h, btw=btw: v.tensor_scalar(
                    btw, bs(o_BTc + c * P * 2, P, P), wsufTs(c)[:, h:h + 1],
                    None, op0=ALU.mult))
                ssl = pb[h % 2][:, 0:P]
                S.waits("t", [tk_btw, tk_xtd, sps_cons[h % 2]])
                tk_s = S.op("t", lambda t, btw=btw, xtd=xtd, ssl=ssl: t.matmul(
                    ssl, btw, xtd, start=True, stop=True))
                sps_cons[("btw", h % 2)] = tk_s
                xtd_cons[sl] = (tk_s if True else None)
                S.waits("v", [tk_s])
                tk_hu = S.op("v", lambda v, h=h, c=c, ssl=ssl:
                             v.scalar_tensor_tensor(
                                 Hs(h), Hs(h), DrepTs(c)[:, h:h + 1], ssl,
                                 op0=ALU.mult, op1=ALU.add))
                sps_cons[h % 2] = tk_hu
                xtd_cons[sl] = tk_wxtd
                S.wait("v", tk_wxtd)
            gate_v = S.now("v")
            gate_t = S.now("t")
            gate_a = S.now("a")
        pass1_v = S.now("v")
        pass1_s = S.now("s", "d")
        if TAPS:
            S.waits("g", [pass1_v, pass1_s])
            S.dma("g", lambda g: g.dma_start(out=taps["tp_h0"], in_=Hs(0)))
            S.dma("g", lambda g: g.dma_start(out=taps["tp_cvx"], in_=convx[0]))
            S.dma("g", lambda g: g.dma_start(out=taps["tp_xtd"], in_=xtds[0][0]))

        S.phase()
        S.wait("g", pass1_v)
        for h in range(NH):
            S.dma("g", lambda g, h=h: g.dma_start(
                out=ag_in[h * P:(h + 1) * P, :], in_=Hs(h)))
        dmt0 = fs(o_sm + 512 * 4, 64, P)
        S.wait("v", tk_pex)
        tk_dbc = S.op("v", lambda v, dmt0=dmt0: v.tensor_scalar(
            dmt0, onesF[0:64, :], Pincl[:, 15:16], None, op0=ALU.mult))
        S.wait("g", tk_dbc)
        tk_dp = S.dma("g", lambda g, dmt0=dmt0: g.dma_start(
            out=ag_in[64 * P:64 * P + 64, :], in_=dmt0))
        S.wait("g", tk_dp)
        tk_cc = S.op("g", lambda g: g.collective_compute(
            "AllGather", ALU.bypass, replica_groups=rg,
            ins=[ag_in], outs=[ag_out]))
        S.wait("g", tk_cc)
        o_dgb = o_sm + 920 * 4
        Dg = [fs(o_agc + q * 4, 64, 1) for q in range(4)]
        d1 = fs(o_agc + 16, 64, 1)
        d2 = fs(o_agc + 20, 64, 1)
        d12 = fs(o_agc + 24, 64, 1)
        on1 = fs(o_agc + 28, 64, 1)
        cq = [fs(o_agc + 32 + q * 4, 64, 1) for q in range(4)]
        o_crepT = o_sm + 640 * 4
        dg_toks = []
        for q in range(4):
            tkb = S.dma("g", lambda g, q=q: g.dma_start(
                out=fs(o_dgb, 64, P),
                in_=ag_out[q * AGB + 64 * P:q * AGB + 64 * P + 64, :]))
            S.wait("v", tkb)
            dg_toks.append(S.op("v", lambda v, q=q: v.tensor_copy(
                Dg[q], fs(o_dgb, 64, P)[:, 0:1])))
        S.phase()
        am = fs(o_amask, 64, 12)
        S.waits("v", dg_toks)
        S.op("v", lambda v: v.memset(on1, 1.0))
        S.op("v", lambda v: v.tensor_scalar(d1, Dg[0], am[:, 0:1], None,
                                            op0=ALU.mult))
        for q in range(1, 4):
            S.op("v", lambda v, q=q: v.scalar_tensor_tensor(
                d1, Dg[q], am[:, q:q + 1], d1, op0=ALU.mult, op1=ALU.add))
        S.op("v", lambda v: v.tensor_scalar(d2, Dg[0], am[:, 4:5], None,
                                            op0=ALU.mult))
        for q in range(1, 4):
            S.op("v", lambda v, q=q: v.scalar_tensor_tensor(
                d2, Dg[q], am[:, 4 + q:5 + q], d2, op0=ALU.mult, op1=ALU.add))
        S.op("v", lambda v: v.tensor_tensor(d12, d1, d2, ALU.mult))
        prev_ctr = None
        for q in range(4):
            S.op("v", lambda v, q=q: v.tensor_scalar(
                cq[q], on1, am[:, q:q + 1], None, op0=ALU.mult))
            S.op("v", lambda v, q=q: v.scalar_tensor_tensor(
                cq[q], d1, am[:, 4 + q:5 + q], cq[q], op0=ALU.mult, op1=ALU.add))
            S.op("v", lambda v, q=q: v.scalar_tensor_tensor(
                cq[q], d12, am[:, 8 + q:9 + q], cq[q], op0=ALU.mult, op1=ALU.add))
            dmt = fs(o_sm + 512 * 4, 64, P)
            tk_cm = S.op("v", lambda v, q=q, dmt=dmt: v.tensor_scalar(
                dmt, onesF[0:64, :], cq[q][:, 0:1], None, op0=ALU.mult))
            S.waits("t", [tk_cm, prev_ctr])
            tk_ctr = S.op("t", lambda t, q=q, dmt=dmt: t.transpose(
                pb[6][:, 192 + q * 64:256 + q * 64], dmt, identF[0:64, 0:64]))
            S.wait("v", tk_ctr)
            prev_ctr = S.op("v", lambda v, q=q: v.tensor_copy(
                fs(o_crepT + q * 64 * 4, P, 64), pb[6][:, 192 + q * 64:256 + q * 64]))
        S.phase()
        o_tq = o_xbcst
        tq_cons = defaultdict(lambda: None)
        hg_done = None
        for h in range(NH):
            acc_t = None
            for q in range(4):
                slot = (h * 4 + q) % 8
                tq = bs(o_tq + slot * P * 2, P, P)
                S.wait("g", tq_cons[slot])
                tk_tq = S.dma("g", lambda g, q=q, h=h, tq=tq: g.dma_start(
                    out=tq, in_=ag_out[q * AGB + h * P:q * AGB + (h + 1) * P, :]))
                S.waits("v", [tk_tq, prev_ctr])
                if q == 0:
                    acc_t = S.op("v", lambda v, h=h, q=q, tq=tq: v.tensor_scalar(
                        Hgs(h), tq, fs(o_crepT + q * 64 * 4, P, 64)[:, h:h + 1],
                        None, op0=ALU.mult))
                else:
                    acc_t = S.op("v", lambda v, h=h, q=q, tq=tq:
                                 v.scalar_tensor_tensor(
                                     Hgs(h), tq,
                                     fs(o_crepT + q * 64 * 4, P, 64)[:, h:h + 1],
                                     Hgs(h), op0=ALU.mult, op1=ALU.add))
                tq_cons[slot] = acc_t
            hg_done = acc_t
            if h in (0, 7, 15, 31, 47):
                S.phase()
        if TAPS:
            S.wait("g", hg_done)
            S.dma("g", lambda g: g.dma_start(out=taps["tp_hg0"], in_=Hgs(0)))

        S.phase()
        A_.reset(MSC)
        o_zst = A_.al(2 * P * 2)
        o_cvx2 = A_.al(4 * P * 2)
        o_xtd2 = A_.al(4 * P * 2)
        o_ar = A_.al(32 * P * 2)
        o_cdr = A_.al(32 * P * 2)
        o_mt = A_.al(2 * P * 2)
        o_mi = A_.al(2 * P * 2)
        o_hfb = A_.al(2 * P * 2)
        o_ynst = A_.al(2 * P * 2)
        o_sqb = A_.al(2 * P * 2)
        o_Lt = A_.al(2 * P * 4)
        o_y1 = A_.al(2 * P * 4)
        o_gat = A_.al(2 * P * 4)
        o_act2 = A_.al(3 * 64 * 4)
        o_rrep = A_.al(P * 4)
        o_rsm = A_.al(8 * 4)
        o_tapst = A_.al(P * 4)

        tk_h02 = S.op("v", lambda v: v.memset(
            FV[:, o_Hst // 4:o_Hst // 4 + NH * P], 0.0))
        S.wait("v", hg_done)
        gate2 = {"v": tk_h02, "t": None, "a": None}
        cons = defaultdict(lambda: None)
        tr2_ev = None
        rms_rd = None
        for c in range(NCH):
            S.waits("g", [gate2["v"], gate2["t"], gate2["a"], pass1_s])
            aT = fs(o_act2, P, 64)
            cdT = fs(o_act2 + 64 * 4, P, 64)
            prT = fs(o_act2 + 128 * 4, P, 64)
            S.waits("t", [tk_da, tr2_ev, cons[("rtp",)]])
            S.op("t", lambda t, c=c: t.transpose(
                pb[6][:, 0:64], dAt[:, c * P:(c + 1) * P], identF[0:64, 0:64]))
            tk_t2 = S.op("t", lambda t, c=c: t.transpose(
                pb[6][:, 64:128], cdt[:, c * P:(c + 1) * P], identF[0:64, 0:64]))
            S.wait("v", tk_t2)
            S.op("v", lambda v, aT=aT: v.tensor_copy(aT, pb[6][:, 0:64]))
            tk_cdT = S.op("v", lambda v, cdT=cdT: v.tensor_copy(cdT, pb[6][:, 64:128]))
            dmt = fs(o_sm + 512 * 4, 64, P)
            S.wait("v", tk_pex)
            tk_pm = S.op("v", lambda v, c=c, dmt=dmt: v.tensor_scalar(
                dmt, onesF[0:64, :], Pex[:, c:c + 1], None, op0=ALU.mult))
            S.wait("t", tk_pm)
            tk_ptr = S.op("t", lambda t, dmt=dmt: t.transpose(
                pb[6][:, 128:192], dmt, identF[0:64, 0:64]))
            S.wait("v", tk_ptr)
            tk_prT = S.op("v", lambda v, prT=prT: v.tensor_copy(
                prT, pb[6][:, 128:192]))
            tr2_ev = S.now("v")
            S.waits("t", [pass1_v, cons[("G",)], rms_rd])
            tk_G = S.op("t", lambda t, c=c: t.matmul(
                pb[5][:, 0:P], bs(o_Bc + c * P * 2, P, P),
                bs(o_Cc + c * P * 2, P, P), start=True, stop=True))
            rms_tok = None
            for half in range(2):
                hh0 = half * 32
                ab_dma, cdb_dma, cdr_rdy = [], {}, []
                for i in range(32):
                    h = hh0 + i
                    S.waits("v", [tr2_ev, cons[("ar", i)]])
                    tk_a1 = S.op("v", lambda v, i=i, h=h, aT=aT: v.tensor_scalar(
                        bs(o_ar + i * P * 2, P, P), onesF[:], aT[:, h:h + 1],
                        None, op0=ALU.mult))
                    S.waits("v", [cons[("cdr", i)]])
                    tk_c1 = S.op("v", lambda v, i=i, h=h, cdT=cdT: v.tensor_scalar(
                        bs(o_cdr + i * P * 2, P, P), onesF[:], cdT[:, h:h + 1],
                        None, op0=ALU.mult))
                    cdr_rdy.append(tk_c1)
                    S.waits("s", [tk_a1, cons[("ab", i)]])
                    ab_dma.append(S.dma("s", lambda s, i=i: s.dma_start_transpose(
                        out=trD[:, i * P:(i + 1) * P],
                        in_=bs(o_ar + i * P * 2, P, P))))

                def issue_cdb(j):
                    if j >= 32:
                        return
                    S.waits("s", [cdr_rdy[j], cons[("cdb", j % 2)]])
                    cdb_dma[j] = S.dma("s", lambda s, j=j: s.dma_start_transpose(
                        out=trD[:, (32 + j % 2) * P:(33 + j % 2) * P],
                        in_=bs(o_cdr + j * P * 2, P, P)))

                issue_cdb(0)
                issue_cdb(1)
                for i in range(32):
                    h = hh0 + i
                    sl = i % 2
                    Lt = fs(o_Lt + sl * P * 4, P, P)
                    S.waits("v", [ab_dma[i], cons[("Lt", sl)]])
                    tk_sc = S.op("v", lambda v, i=i, Lt=Lt: v.tensor_tensor_scan(
                        Lt, trD[:, i * P:(i + 1) * P], identF, 0.0,
                        op0=ALU.mult, op1=ALU.add))
                    S.op("v", lambda v: v.engine_nop())
                    cons[("ab", i)] = tk_sc
                    mt = bs(o_mt + sl * P * 2, P, P)
                    S.waits("v", [tk_G, cons[("mt", sl)]])
                    tk_mt = S.op("v", lambda v, Lt=Lt, mt=mt: v.tensor_tensor(
                        mt, Lt, pb[5][:, 0:P], ALU.mult))
                    cons[("Lt", sl)] = tk_mt
                    mi = bs(o_mi + sl * P * 2, P, P)
                    S.waits("v", [cdb_dma[i], cons[("mi", sl)]])
                    tk_mi = S.op("v", lambda v, c=c, i=i, mi=mi: v.tensor_tensor(
                        mi, bs(o_Cc + c * P * 2, P, P),
                        trD[:, (32 + i % 2) * P:(33 + i % 2) * P], ALU.mult))
                    cons[("cdb", i % 2)] = tk_mi
                    cons[("cdr", i)] = cdb_dma[i]
                    issue_cdb(i + 2)
                    hfb = bs(o_hfb + sl * P * 2, P, P)
                    S.waits("v", [tk_prT, cons[("hf", sl)]])
                    tk_hf = S.op("v", lambda v, h=h, hfb=hfb, prT=prT:
                                 v.scalar_tensor_tensor(
                                     hfb, Hgs(h), prT[:, h:h + 1], Hs(h),
                                     op0=ALU.mult, op1=ALU.add))
                    btw = bs(o_ar + i * P * 2, P, P)
                    S.waits("v", [ab_dma[i], cons[("btwu", sl)]])
                    tk_btw = S.op("v", lambda v, c=c, h=h, btw=btw: v.tensor_scalar(
                        btw, bs(o_BTc + c * P * 2, P, P), wsufTs(c)[:, h:h + 1],
                        None, op0=ALU.mult))
                    cons[("ar", i)] = tk_btw
                    zt = bs(o_zst + sl * P * 2, P, P)
                    S.waits("g", [cons[("zt", sl)], proj_s])
                    rd_z = S.dma("g", lambda g, c=c, h=h, zt=zt: g.dma_start(
                        out=zt, in_=projs[h][:, c * P + 3:c * P + 3 + P]))
                    cvsl = i % 4
                    cvr = bs(o_cvx2 + cvsl * P * 2, P, P)
                    S.wait("g", cons[("cvr", cvsl)])
                    rd_c = S.dma("g", lambda g, c=c, h=h, cvr=cvr: g.dma_start(
                        out=cvr, in_=convx[h][:, c * P:(c + 1) * P]))
                    xsl = i % 4
                    xt2 = bs(o_xtd2 + xsl * P * 2, P, P)
                    S.wait("g", cons[("xt2", xsl)])
                    rd_x = S.dma("g", lambda g, c=c, h=h, xt2=xt2: g.dma_start(
                        out=xt2, in_=xtds[c][h]))
                    ysl = pb[2 + sl][:, 0:P]
                    S.waits("t", [tk_mt, rd_x, cons[("y", sl)]])
                    S.op("t", lambda t, mt=mt, xt2=xt2, ysl=ysl: t.matmul(
                        ysl, xt2, mt, start=True, stop=False))
                    S.waits("t", [tk_hf, tk_mi])
                    tk_y = S.op("t", lambda t, hfb=hfb, mi=mi, ysl=ysl: t.matmul(
                        ysl, hfb, mi, start=False, stop=True))
                    cons[("mt", sl)] = tk_y
                    cons[("mi", sl)] = tk_y
                    cons[("hf", sl)] = tk_y
                    ssl = pb[1 if sl == 0 else 4][:, 0:P]
                    S.waits("t", [tk_btw, cons[("S", sl)]])
                    tk_s = S.op("t", lambda t, btw=btw, xt2=xt2, ssl=ssl: t.matmul(
                        ssl, btw, xt2, start=True, stop=True))
                    cons[("btwu", sl)] = tk_s
                    S.wait("v", tk_s)
                    tk_hu = S.op("v", lambda v, h=h, c=c, ssl=ssl:
                                 v.scalar_tensor_tensor(
                                     Hs(h), Hs(h), DrepTs(c)[:, h:h + 1], ssl,
                                     op0=ALU.mult, op1=ALU.add))
                    cons[("S", sl)] = tk_hu
                    cons[("xt2", xsl)] = tk_s
                    y1 = fs(o_y1 + sl * P * 4, P, P)
                    S.waits("v", [tk_y, rd_c, cons[("y1", sl)]])
                    tk_ye = S.op("v", lambda v, y1=y1, cvr=cvr, h=h, ysl=ysl:
                                 v.scalar_tensor_tensor(
                                     y1, cvr, dprepT[:, h:h + 1], ysl,
                                     op0=ALU.mult, op1=ALU.add))
                    cons[("y", sl)] = tk_ye
                    cons[("cvr", cvsl)] = tk_ye
                    if TAPS and c == 0 and h == 0:
                        S.wait("g", tk_ye)
                        t_y1tap = S.dma("g", lambda g, y1=y1: g.dma_start(
                            out=taps["tp_y1"], in_=y1))
                        S.wait("v", tk_y)
                        tk_tps = S.op("v", lambda v, ysl=ysl: v.tensor_copy(
                            fs(o_tapst, P, P), ysl))
                        S.waits("g", [tk_tps, tk_hf])
                        S.dma("g", lambda g: g.dma_start(
                            out=taps["tp_ps"], in_=fs(o_tapst, P, P)))
                        S.dma("g", lambda g, hfb=hfb: g.dma_start(
                            out=taps["tp_hf"], in_=hfb))
                        S.dma("g", lambda g, xt2=xt2: g.dma_start(
                            out=taps["tp_xt2"], in_=xt2))
                        S.wait("v", t_y1tap)
                    gat = fs(o_gat + sl * P * 4, P, P)
                    S.waits("a", [rd_z, cons[("gat", sl)]])
                    S.op("a", lambda a, zt=zt, gat=gat: a.activation(
                        gat, zt, AF.Exp, scale=-1.0))
                    S.op("a", lambda a, gat=gat: a.activation(
                        gat, gat, AF.Ln, bias=one1[:, 0:1]))
                    tk_g3 = S.op("a", lambda a, gat=gat: a.activation(
                        gat, gat, AF.Exp, scale=-1.0))
                    S.wait("v", tk_g3)
                    S.op("v", lambda v, y1=y1, zt=zt: v.tensor_tensor(
                        y1, y1, zt, ALU.mult))
                    tk_yg = S.op("v", lambda v, y1=y1, gat=gat: v.tensor_tensor(
                        y1, y1, gat, ALU.mult))
                    cons[("gat", sl)] = tk_yg
                    cons[("zt", sl)] = tk_yg
                    if TAPS and c == 0 and h == 0:
                        S.waits("g", [tk_sc, tk_mt, tk_mi, tk_ye, tk_g3, tk_yg])
                        S.dma("g", lambda g, Lt=Lt: g.dma_start(
                            out=taps["tp_lt"], in_=Lt))
                        S.dma("g", lambda g, mt=mt: g.dma_start(
                            out=taps["tp_mt"], in_=mt))
                        S.dma("g", lambda g, mi=mi: g.dma_start(
                            out=taps["tp_mi"], in_=mi))
                        S.dma("g", lambda g, gat=gat: g.dma_start(
                            out=taps["tp_g"], in_=gat))
                        S.dma("g", lambda g, i=i: g.dma_start(
                            out=taps["tp_ab"], in_=trD[:, i * P:(i + 1) * P]))
                    sq = bs(o_sqb + sl * P * 2, P, P)
                    S.waits("a", [tk_yg, cons[("sq", sl)]])
                    tk_sq2 = S.op("a", lambda a, y1=y1, sq=sq: a.activation(
                        sq, y1, AF.Square))
                    S.waits("t", [tk_sq2, cons[("rms",)] if h == 0 else None])
                    rms_tok = S.op("t", lambda t, h=h, sq=sq: t.matmul(
                        pb[0][0:P, 0:1], sq, onescB[:],
                        start=(h == 0), stop=(h == NH - 1)))
                    cons[("sq", sl)] = rms_tok
                    yst = bs(o_ynst + sl * P * 2, P, P)
                    S.waits("v", [cons[("yst", sl)]])
                    tk_yn = S.op("v", lambda v, h=h, y1=y1, yst=yst: v.tensor_scalar(
                        yst, y1, gammaT[:, h:h + 1], None, op0=ALU.mult))
                    cons[("y1", sl)] = tk_yn
                    S.wait("s", tk_yn)
                    cons[("yst", sl)] = S.dma(
                        "s", lambda s, c=c, h=h, yst=yst: s.dma_start(
                            out=ynorm[h][:, c * P:(c + 1) * P], in_=yst))
            rsm = fs(o_rsm, P, 1)
            S.waits("a", [rms_tok, cons[("rsmu",)]])
            S.op("a", lambda a, rsm=rsm: a.activation(
                rsm, pb[0][0:P, 0:1], AF.Ln, scale=1.0 / DSSM,
                bias=epsA[:, 0:1]))
            tk_rs = S.op("a", lambda a, rsm=rsm: a.activation(
                rsm, rsm, AF.Exp, scale=-0.5))
            rms_rd = tk_rs
            cons[("G",)] = S.now("v")
            S.waits("v", [tk_rs, cons[("rrep",)]])
            rr = fs(o_rrep, P, P)
            tk_rr = S.op("v", lambda v, rsm=rsm, rr=rr: v.tensor_scalar(
                rr, onesF[:], rsm[:, 0:1], None, op0=ALU.mult))
            cons[("rsmu",)] = tk_rr
            S.waits("t", [tk_rr, cons[("rtp",)]])
            tk_rt2 = S.op("t", lambda t, rr=rr: t.transpose(
                pb[6][:, 320:448], rr, identF))
            S.wait("v", tk_rt2)
            tk_rb = S.op("v", lambda v, c=c: v.tensor_copy(
                rstdB[:, c * P:(c + 1) * P], pb[6][:, 320:448]))
            cons[("rtp",)] = tk_rb
            cons[("rrep",)] = tk_rt2
            gate2 = {"v": S.now("v"), "t": S.now("t"), "a": S.now("a")}
            if c in (0, 1, 3, 7):
                S.phase()
        pass2_v = S.now("v")
        pass2_s = S.now("s", "d")
        if TAPS:
            S.waits("g", [pass2_v, pass2_s])
            S.dma("g", lambda g: g.dma_start(out=taps["tp_yn"], in_=ynorm[0]))
            S.dma("g", lambda g: g.dma_start(out=taps["tp_rstd"], in_=rstdB))

        S.phase()
        A_.reset(M0)
        o_ynh = A_.al(64 * 1024 * 2)
        o_wob = A_.al(DSSM * 2)
        o_xst = A_.al(512 * 4)
        o_fst2 = A_.al(512 * 4)
        o_ost = [A_.al(512 * 4) for _ in range(2)]

        def ynh(k, c0, w):
            return arB[:, o_ynh // 2 + k * 1024 + c0: o_ynh // 2 + k * 1024 + c0 + w]

        wo_cons = None
        ost_cons = [None, None]
        xst_cons = None
        for half in range(2):
            hc0 = half * 1024
            ynh_toks = []
            S.waits("g", [S.now("t"), pass2_s])
            for k in range(NH):
                ynh_toks.append(S.dma("g", lambda g, k=k, hc0=hc0: g.dma_start(
                    out=ynh(k, 0, 1024), in_=ynorm[k][:, hc0:hc0 + 1024])))
            ynh_rd = ynh_toks[-1]
            for ot in range(32):
                wob = bs(o_wob, P, DSSM)
                S.waits("g", [wo_cons])
                tk_wo = S.dma("g", lambda g, ot=ot, wob=wob: g.dma_start(
                    out=wob, in_=w_out[ot]))
                S.waits("t", [tk_wo, ynh_rd, ost_cons[0], ost_cons[1]])
                for k in range(NH):
                    for g2 in range(2):
                        S.op("t", lambda t, k=k, g2=g2, wob=wob: t.matmul(
                            pb[g2][:, 0:512], wob[:, k * P:(k + 1) * P],
                            ynh(k, g2 * 512, 512),
                            start=(k == 0), stop=(k == NH - 1)))
                wo_cons = S.now("t")
                for g2 in range(2):
                    tc0 = hc0 + g2 * 512
                    sbx = fs(o_xst, P, 512)
                    sbf = fs(o_fst2, P, 512)
                    S.waits("s", [xst_cons])
                    tk_x = S.dma("s", lambda s, ot=ot, tc0=tc0, sbx=sbx: s.dma_start(
                        out=sbx, in_=xh[ot * P:(ot + 1) * P, 3 + tc0:3 + tc0 + 512]))
                    S.wait("s", ffn_s)
                    tk_f = S.dma("s", lambda s, ot=ot, tc0=tc0, sbf=sbf: s.dma_start(
                        out=sbf, in_=ffnout[ot][:, tc0:tc0 + 512]))
                    ob = fs(o_ost[g2], P, 512)
                    S.waits("v", [wo_cons, tk_x, tk_f, pass2_v, ost_cons[g2]])
                    S.op("v", lambda v, g2=g2, tc0=tc0, ob=ob: v.tensor_tensor(
                        ob, pb[g2][:, 0:512], rstdB[:, tc0:tc0 + 512], ALU.mult))
                    S.op("v", lambda v, ob=ob, sbf=sbf: v.tensor_tensor(
                        ob, ob, sbf, ALU.add))
                    tk_o = S.op("v", lambda v, ob=ob, sbx=sbx: v.tensor_tensor(
                        ob, ob, sbx, ALU.add))
                    xst_cons = tk_o
                    S.wait("s", tk_o)
                    ost_cons[g2] = S.dma(
                        "s", lambda s, ot=ot, tc0=tc0, ob=ob: s.dma_start(
                            out=out[ot][:, tc0:tc0 + 512], in_=ob))
        S.phase()
        if PHASES < len(S.marks):
            S.rollback(PHASES)
        S.wait("s", S.now("s", "d"))
        for st in ["t", "v", "a", "g"]:
            S.wait("s", S.now(st, "o"))
        S.wait("t", S.now("s", "d"))
        S.wait("g", S.now("g", "d"))
        S.wait("a", S.now("a", "d"))

        S.replay(block, {"t": "tensor", "v": "vector", "a": "scalar",
                         "g": "gpsimd", "s": "sync"}, osem, dsem)
    return nc


_CACHE = {}


def kernel(L0_attn_hidden_states, L0_attn_W_in, L0_attn_conv_w, L0_attn_dt_bias,
           L0_attn_A_log, L0_attn_D_param, L0_attn_norm_weight, L0_attn_W_out,
           L0_ffn_W_base, L0_ffn_b_base, L0_ffn_W_router, L0_ffn_A, L0_ffn_B):
    x = np.asarray(L0_attn_hidden_states, dtype=np.float32)
    W_in = np.asarray(L0_attn_W_in, dtype=np.float32)
    conv_w = np.asarray(L0_attn_conv_w, dtype=np.float32)
    dt_bias = np.asarray(L0_attn_dt_bias, dtype=np.float32)
    A_log = np.asarray(L0_attn_A_log, dtype=np.float32)
    D_param = np.asarray(L0_attn_D_param, dtype=np.float32)
    norm_w = np.asarray(L0_attn_norm_weight, dtype=np.float32)
    W_out = np.asarray(L0_attn_W_out, dtype=np.float32)
    W_base = np.asarray(L0_ffn_W_base, dtype=np.float32)
    b_base = np.asarray(L0_ffn_b_base, dtype=np.float32)
    W_router = np.asarray(L0_ffn_W_router, dtype=np.float32)
    A = np.asarray(L0_ffn_A, dtype=np.float32)
    B = np.asarray(L0_ffn_B, dtype=np.float32)

    w_in_p = np.ascontiguousarray(
        np.pad(W_in, ((0, 0), (0, PROJ_PAD - PROJ_DIM)))
        .reshape(KC, P, NPT, P).transpose(2, 1, 0, 3).reshape(NPT, P, H))
    w_out_p = np.ascontiguousarray(
        W_out.reshape(64, P, 32, P).transpose(2, 1, 0, 3).reshape(32, P, DSSM))
    w_base_p = np.ascontiguousarray(
        W_base.T.reshape(KC, P, 32, P).transpose(2, 1, 0, 3).reshape(32, P, H))
    acat_p = np.ascontiguousarray(
        A.reshape(P, H).T.reshape(KC, P, P).transpose(1, 0, 2).reshape(P, H))
    bcat_p = np.ascontiguousarray(B.transpose(0, 2, 1).reshape(P, H))
    wrl_p = np.ascontiguousarray(
        W_router.T.reshape(KC, P, 8).transpose(1, 0, 2).reshape(P, 256))
    convw_p = np.ascontiguousarray(
        conv_w.reshape(66, P, 4).transpose(1, 0, 2).reshape(P, 264))
    dtb_p = np.ascontiguousarray(dt_bias.reshape(64, 1))
    alog_p = np.ascontiguousarray(A_log.reshape(64, 1))
    dprep_p = np.ascontiguousarray(np.broadcast_to(D_param[None, :], (P, 64)))
    gamma_p = np.ascontiguousarray(norm_w.reshape(64, P).T)
    bb_p = np.ascontiguousarray(b_base.reshape(32, P).T)
    e8_p = np.zeros((8, P), dtype=ml_dtypes.bfloat16)
    for e_ in range(8):
        e8_p[e_, e_ * 16:(e_ + 1) * 16] = 2.0
    identf_p = np.eye(P, dtype=np.float32)
    identb_p = np.eye(P, dtype=ml_dtypes.bfloat16)

    if "prog" not in _CACHE:
        _CACHE["prog"] = build()
    nc = _CACHE["prog"]

    in_maps = []
    for c in range(8):
        b_i, s_i = c // 4, c % 4
        t0 = s_i * T
        if s_i == 0:
            xs = np.concatenate(
                [np.zeros((3, H), dtype=np.float32), x[b_i, 0:T]], axis=0)
        else:
            xs = x[b_i, t0 - 3:t0 + T]
        xh_p = np.ascontiguousarray(xs.T)
        am = np.zeros((64, 12), dtype=np.float32)
        for slot in range(3):
            q = s_i - 1 - slot
            if q >= 0:
                am[:, slot * 4 + q] = 1.0
        in_maps.append({
            "xh": xh_p, "w_in": w_in_p, "w_out": w_out_p, "w_base": w_base_p,
            "acat": acat_p, "bcat": bcat_p, "wrl": wrl_p, "convw": convw_p,
            "dtbias": dtb_p, "alog": alog_p, "dprep": dprep_p,
            "gammah": gamma_p, "bbase": bb_p, "e8": e8_p,
            "identf": identf_p, "identb": identb_p, "amask": am,
        })

    res = run_bass_kernel_spmd(nc, in_maps, core_ids=list(range(8)))
    outp = np.empty((2, 8192, H), dtype=np.float32)
    for c in range(8):
        b_i, s_i = c // 4, c % 4
        oc = np.asarray(res.results[c]["out"], dtype=np.float32)
        outp[b_i, s_i * T:(s_i + 1) * T] = oc.reshape(H, T).T
    if TAPS:
        kernel.last_results = res
    return outp


# version 06
# speedup vs baseline: 1.0298x; 1.0298x over previous
from collections import defaultdict
from contextlib import ExitStack

import numpy as np
import ml_dtypes
import concourse.bass as bass
import concourse.mybir as mybir
from concourse.bass_utils import run_bass_kernel_spmd
from concourse.replica_groups import maybe_share_collective_output_space

F32 = mybir.dt.float32
BF16 = mybir.dt.bfloat16
AF = mybir.ActivationFunctionType
ALU = mybir.AluOpType

P = 128
H = 4096
DSSM = 8192
NH = 64
CONV_DIM = 8448
PROJ_DIM = 16704
PROJ_PAD = 16768
NPT = PROJ_PAD // P
KC = H // P
T = 2048
TH = T + 3
NCH = 16
EPS = 1e-5
ARENA_BYTES = 196000

TAPS = False
PHASES = 99


class Sched:

    def __init__(self):
        self.q = defaultdict(list)
        self.no = defaultdict(int)
        self.nd = defaultdict(int)
        self.nde = {}
        self.waited = {}

    def op(self, st, fn):
        self.no[st] += 1
        self.q[st].append(("o", fn))
        return (st, "o", self.no[st])

    def dma(self, st, fn):
        ep = self.nde.get(st, 0)
        if self.nd[(st, ep)] + 16 > 59968:
            ep += 1
            self.nde[st] = ep
        self.nd[(st, ep)] += 16
        self.q[st].append(("d", fn, ep))
        return (st, "d", (ep, self.nd[(st, ep)]))

    def wait(self, st, tok):
        if tok is None:
            return
        tst, kind, cnt = tok
        if kind == "d":
            ep, c = cnt
            key = (st, tst, "d", ep)
            if self.waited.get(key, -1) >= c:
                return
            self.waited[key] = c
            self.q[st].append(("w", (tst, "d", ep, c)))
            return
        key = (st, tst, kind)
        if self.waited.get(key, -1) >= cnt:
            return
        self.waited[key] = cnt
        self.q[st].append(("w", (tst, kind, cnt)))

    def waits(self, st, toks):
        for tk in toks:
            self.wait(st, tk)

    def phase(self):
        self.marks = getattr(self, "marks", [])
        self.marks.append(({st: len(q) for st, q in self.q.items()},
                           dict(self.no), dict(self.nd), dict(self.nde),
                           dict(self.waited)))

    def rollback(self, k):
        lens, no, nd, nde, waited = self.marks[k]
        for st in list(self.q):
            del self.q[st][lens.get(st, 0):]
        self.no = defaultdict(int, no)
        self.nd = defaultdict(int, nd)
        self.nde = dict(nde)
        self.waited = dict(waited)

    def now(self, st, kind="o"):
        if kind == "d":
            ep = self.nde.get(st, 0)
            return (st, "d", (ep, self.nd[(st, ep)]))
        return (st, kind, self.no[st])

    def replay(self, block, engines, osem, dsem):
        for st, eng_attr in engines.items():
            items = self.q[st]
            if not items:
                continue

            def make(items=items, st=st):
                def run(eng):
                    ndma = defaultdict(int)
                    nop_ = 0
                    for item in items:
                        kind = item[0]
                        if kind == "w":
                            payload = item[1]
                            if payload[1] == "d":
                                tst, _, ep, cnt = payload
                                eng.wait_ge(dsem[(tst, ep)], cnt)
                            else:
                                tst, tkind, cnt = payload
                                eng.wait_ge(osem[tst], cnt)
                        elif kind == "o":
                            if nop_:
                                eng.wait_ge(osem[st], nop_)
                            item[1](eng).then_inc(osem[st], 1)
                            nop_ += 1
                        else:
                            _, payload, ep = item
                            payload(eng).then_inc(dsem[(st, ep)], 16)
                            ndma[ep] += 16
                            if st != "s":
                                eng.wait_ge(dsem[(st, ep)], ndma[ep])
                return run

            getattr(block, eng_attr)(make())


def build():
    nc = bass.Bass("TRN2", target_bir_lowering=False)

    xh = nc.dram_tensor("xh", [H, TH], F32, kind="ExternalInput").ap()
    w_in = nc.dram_tensor("w_in", [NPT, P, H], F32, kind="ExternalInput").ap()
    w_out = nc.dram_tensor("w_out", [32, P, DSSM], F32, kind="ExternalInput").ap()
    w_base = nc.dram_tensor("w_base", [32, P, H], F32, kind="ExternalInput").ap()
    acat = nc.dram_tensor("acat", [P, H], F32, kind="ExternalInput").ap()
    bcat = nc.dram_tensor("bcat", [P, H], F32, kind="ExternalInput").ap()
    wrl = nc.dram_tensor("wrl", [P, 256], F32, kind="ExternalInput").ap()
    convw = nc.dram_tensor("convw", [P, 264], F32, kind="ExternalInput").ap()
    dtbias = nc.dram_tensor("dtbias", [64, 1], F32, kind="ExternalInput").ap()
    alog = nc.dram_tensor("alog", [64, 1], F32, kind="ExternalInput").ap()
    dprep = nc.dram_tensor("dprep", [P, 64], F32, kind="ExternalInput").ap()
    gammah = nc.dram_tensor("gammah", [P, 64], F32, kind="ExternalInput").ap()
    bbase = nc.dram_tensor("bbase", [P, 32], F32, kind="ExternalInput").ap()
    e8 = nc.dram_tensor("e8", [8, P], BF16, kind="ExternalInput").ap()
    identf_d = nc.dram_tensor("identf", [P, P], F32, kind="ExternalInput").ap()
    identb_d = nc.dram_tensor("identb", [P, P], BF16, kind="ExternalInput").ap()
    amask = nc.dram_tensor("amask", [64, 12], F32, kind="ExternalInput").ap()

    out = nc.dram_tensor("out", [32, P, T], F32, kind="ExternalOutput").ap()

    projs = nc.dram_tensor("projs", [NPT, P, TH], BF16, kind="Internal").ap()
    convx = nc.dram_tensor("convx", [64, P, T], BF16, kind="Internal").ap()
    xtds = nc.dram_tensor("xtds", [NCH, 64, P, P], BF16, kind="Internal").ap()
    ynorm = nc.dram_tensor("ynorm", [64, P, T], BF16, kind="Internal").ap()
    ffnout = nc.dram_tensor("ffnout", [32, P, T], F32, kind="Internal").ap()
    rg = [[0, 1, 2, 3], [4, 5, 6, 7]]
    ag_space = maybe_share_collective_output_space("AllGather", rg)
    AGB = 64 * P + 64
    ag_in = nc.dram_tensor("ag_in", [AGB, P], F32, kind="Internal").ap()
    ag_out = nc.dram_tensor("ag_out", [4 * AGB, P], F32, kind="Internal",
                            addr_space=ag_space).ap()

    taps = {}
    if TAPS:
        for nm, shp in [("tp_xn", [P, TH]), ("tp_proj", [P, TH]),
                        ("tp_dtsp", [64, T]), ("tp_da", [64, T]),
                        ("tp_cd", [64, T]), ("tp_h0", [P, P]),
                        ("tp_hg0", [P, P]), ("tp_cvx", [P, T]),
                        ("tp_xtd", [P, P]), ("tp_yn", [P, T]),
                        ("tp_ffn", [P, T]), ("tp_rstd", [P, T]),
                        ("tp_hw", [P, T]), ("tp_d", [P, 17]),
                        ("tp_lt", [P, P]), ("tp_mt", [P, P]),
                        ("tp_mi", [P, P]), ("tp_y1", [P, P]),
                        ("tp_g", [P, P]), ("tp_ab", [P, P]),
                        ("tp_ps", [P, P]), ("tp_hf", [P, P]),
                        ("tp_xt2", [P, P])]:
            taps[nm] = nc.dram_tensor(nm, shp, F32, kind="ExternalOutput").ap()

    S = Sched()

    with ExitStack() as ctx:
        e = ctx.enter_context
        arB = e(nc.sbuf_tensor("arB", [P, ARENA_BYTES // 2], BF16))
        trD = e(nc.sbuf_tensor("trD", [P, 4352], BF16, is_transpose_dest=True))
        FV = arB[:, :].bitcast(F32)

        pb = [e(nc.psum_tensor(f"pb{i}", [P, 512], F32)) for i in range(7)]
        pbf = e(nc.psum_tensor("pbf", [P, 1024], BF16))

        semT = e(nc.semaphore("semT"))
        semV = e(nc.semaphore("semV"))
        semA = e(nc.semaphore("semA"))
        semG = e(nc.semaphore("semG"))
        dsem = {}
        for stn in ["s", "a", "g"]:
            for ep in range(3):
                dsem[(stn, ep)] = e(nc.semaphore(f"dma{stn}{ep}"))
        block = e(nc.Block())
        osem = {"t": semT, "v": semV, "a": semA, "g": semG, "s": semT}

        class Ar:
            def __init__(self):
                self.off = 0

            def al(self, nbytes, align=64):
                self.off = (self.off + align - 1) // align * align
                assert self.off + nbytes <= ARENA_BYTES, \
                    f"arena overflow {self.off}+{nbytes}"
                o = self.off
                self.off += nbytes
                return o

            def mark(self):
                return self.off

            def reset(self, m):
                self.off = m

        A_ = Ar()

        def fs(off, rows, cols):
            return FV[0:rows, off // 4: off // 4 + cols]

        def bs(off, rows, cols):
            return arB[0:rows, off // 2: off // 2 + cols]

        o_identF = A_.al(P * 4)
        o_onesF = A_.al(P * 4)
        o_gamma = A_.al(64 * 4)
        o_dprep = A_.al(64 * 4)
        o_bb = A_.al(32 * 4)
        o_dtbias = A_.al(4)
        o_alog = A_.al(4)
        o_negeA = A_.al(4)
        o_amask = A_.al(48)
        o_eps = A_.al(4)
        o_one1 = A_.al(4)
        o_convw = A_.al(264 * 4)
        o_identB = A_.al(P * 2)
        o_e8 = A_.al(P * 2)
        o_onescB = A_.al(2)
        identF = fs(o_identF, P, P)
        onesF = fs(o_onesF, P, P)
        identB = bs(o_identB, P, P)
        E8t = bs(o_e8, 8, P)
        onescB = bs(o_onescB, P, 1)
        epsA = fs(o_eps, P, 1)
        one1 = fs(o_one1, P, 1)
        convW = fs(o_convw, P, 264)
        gammaT = fs(o_gamma, P, 64)
        dprepT = fs(o_dprep, P, 64)
        bbT = fs(o_bb, P, 32)

        ld0 = []
        for dst, src in [(identF, identf_d), (identB, identb_d), (E8t, e8),
                         (gammaT, gammah), (dprepT, dprep), (bbT, bbase),
                         (fs(o_dtbias, 64, 1), dtbias),
                         (fs(o_alog, 64, 1), alog),
                         (fs(o_amask, 64, 12), amask), (convW, convw)]:
            ld0.append(S.dma("s", lambda s, dst=dst, src=src: s.dma_start(
                out=dst, in_=src)))
        S.waits("v", ld0)
        S.op("v", lambda v: v.memset(onesF[:], 1.0))
        S.op("v", lambda v: v.memset(onescB[:], 1.0))
        S.op("v", lambda v: v.memset(epsA[:], EPS))
        tk_one = S.op("v", lambda v: v.memset(one1[:], 1.0))

        o_rstdB = A_.al(T * 4)
        rstdB = fs(o_rstdB, P, T)
        M0 = A_.mark()
        o_dtres = A_.al(TH * 4)
        dtres = fs(o_dtres, 64, TH)
        M1 = A_.mark()

        o_xn = A_.al(KC * TH * 2)
        MLN = A_.mark()
        o_xbf = [A_.al(TH * 2) for _ in range(2)]
        o_xsq = A_.al(TH * 2)
        o_stat = A_.al(80 * 4)
        o_drep = A_.al(P * 2)

        def xns(kc, c0, w):
            return arB[:, o_xn // 2 + kc * TH + c0: o_xn // 2 + kc * TH + c0 + w]

        NG = 17
        gw = [128] * 16 + [3]
        gof = [128 * i for i in range(17)]
        pstats = pb[6]
        o_macc = A_.al(40 * 4)
        o_sacc = A_.al(40 * 4)
        macc = fs(o_macc, P, 17)
        sacc = fs(o_sacc, P, 17)
        S.op("v", lambda v: v.memset(macc, 0.0))
        tk_z = S.op("v", lambda v: v.memset(sacc, 0.0))
        last_ld = [None, None]
        stats_done = None
        prev_acc = tk_z
        for kc in range(KC):
            buf = kc % 2
            xb = bs(o_xbf[buf], P, TH)
            S.wait("g", last_ld[buf])
            tk_ld = S.dma("g", lambda g, kc=kc, xb=xb: g.dma_start(
                out=xb, in_=xh[kc * P:(kc + 1) * P, :]))
            S.waits("a", [tk_ld, stats_done])
            tk_sq = S.op("a", lambda a, xb=xb: a.activation(
                bs(o_xsq, P, TH), xb, AF.Square))
            S.waits("t", [tk_ld, tk_sq, tk_one, prev_acc])
            for g in range(NG):
                S.op("t", lambda t, kc=kc, g=g, xb=xb: t.matmul(
                    pstats[0:gw[g], g:g + 1],
                    xb[:, gof[g]:gof[g] + gw[g]], onescB[:],
                    start=True, stop=True))
                stats_done = S.op("t", lambda t, kc=kc, g=g: t.matmul(
                    pstats[0:gw[g], 32 + g:33 + g],
                    bs(o_xsq, P, TH)[:, gof[g]:gof[g] + gw[g]], onescB[:],
                    start=True, stop=True))
            last_ld[buf] = S.now("t")
            S.wait("v", stats_done)
            S.op("v", lambda v: v.tensor_tensor(
                macc[:, 0:16], macc[:, 0:16], pstats[:, 0:16], ALU.add))
            S.op("v", lambda v: v.tensor_tensor(
                macc[0:3, 16:17], macc[0:3, 16:17], pstats[0:3, 16:17], ALU.add))
            S.op("v", lambda v: v.tensor_tensor(
                sacc[:, 0:16], sacc[:, 0:16], pstats[:, 32:48], ALU.add))
            prev_acc = S.op("v", lambda v: v.tensor_tensor(
                sacc[0:3, 16:17], sacc[0:3, 16:17], pstats[0:3, 48:49], ALU.add))
        mmv = fs(o_stat, P, 17)
        vv = fs(o_stat + 17 * 4, P, 17)
        dd = fs(o_stat + 34 * 4, P, 17)
        md = fs(o_stat + 51 * 4, P, 17)
        S.wait("v", prev_acc)
        tk1 = S.op("v", lambda v: v.tensor_scalar(
            mmv, macc, 1.0 / H, None, op0=ALU.mult))
        S.wait("a", tk1)
        tk2 = S.op("a", lambda a: a.activation(dd, mmv, AF.Square))
        S.wait("v", tk2)
        tk3 = S.op("v", lambda v: v.scalar_tensor_tensor(
            vv, sacc, 1.0 / H, dd, op0=ALU.mult, op1=ALU.subtract))
        S.wait("a", tk3)
        S.op("a", lambda a: a.activation(dd, vv, AF.Ln, bias=epsA[:, 0:1]))
        tk_d = S.op("a", lambda a: a.activation(dd, dd, AF.Exp, scale=-0.5))
        S.wait("v", tk_d)
        tk_md = S.op("v", lambda v: v.tensor_tensor(md, mmv, dd, ALU.mult))
        drep = bs(o_drep, P, P)
        t_db_last = None
        prev_tr = None
        for g in range(NG):
            S.waits("v", [tk_md, prev_tr])
            t1 = S.op("v", lambda v, g=g: v.tensor_scalar(
                drep, onesF[:], dd[:, g:g + 1], None, op0=ALU.mult))
            S.wait("s", t1)
            t_db = S.dma("s", lambda s, g=g: s.dma_start_transpose(
                out=trD[:, g * P:g * P + P], in_=drep))
            S.wait("v", t_db)
            t2 = S.op("v", lambda v, g=g: v.tensor_scalar(
                drep, onesF[:], md[:, g:g + 1], None, op0=ALU.mult))
            S.wait("s", t2)
            t_db_last = prev_tr = S.dma("s", lambda s, g=g: s.dma_start_transpose(
                out=trD[:, 2176 + g * P:2176 + g * P + P], in_=drep))
        o_tmpF = A_.al(P * 4)
        xn_tok = None
        for kc in range(KC):
            buf = kc % 2
            xb = bs(o_xbf[buf], P, TH)
            S.wait("g", last_ld[buf])
            tk_ld = S.dma("g", lambda g, kc=kc, xb=xb: g.dma_start(
                out=xb, in_=xh[kc * P:(kc + 1) * P, :]))
            S.waits("v", [tk_ld, t_db_last])
            for g in range(NG):
                tmp = fs(o_tmpF, P, P)
                S.op("v", lambda v, g=g, xb=xb, tmp=tmp: v.tensor_tensor(
                    tmp[:, 0:gw[g]], xb[:, gof[g]:gof[g] + gw[g]],
                    trD[:, g * P:g * P + gw[g]], ALU.mult))
                xn_tok = S.op("v", lambda v, kc=kc, g=g, tmp=tmp: v.tensor_tensor(
                    xns(kc, gof[g], gw[g]), tmp[:, 0:gw[g]],
                    trD[:, 2176 + g * P:2176 + g * P + gw[g]], ALU.subtract))
            last_ld[buf] = S.now("v")
        if TAPS:
            S.wait("g", xn_tok)
            S.dma("g", lambda g: g.dma_start(
                out=taps["tp_xn"][0:P, 0:TH], in_=xns(0, 0, TH)))
            S.dma("g", lambda g: g.dma_start(
                out=taps["tp_d"], in_=fs(o_stat + 34 * 4, P, 17)))

        S.phase()
        A_.reset(MLN)
        o_wbuf = [A_.al(H * 2) for _ in range(2)]
        o_pst = [A_.al(512 * 2) for _ in range(4)]
        PG = [(0, 411), (411, 411), (822, 411), (1233, 411), (1644, 407)]
        w_ld = [None, None]
        mm_done = [None, None]
        prev_evac = None
        st_tok = [None, None, None, None]
        S.wait("g", xn_tok)
        for wt in range(NPT):
            buf = wt % 2
            wb = bs(o_wbuf[buf], P, H)
            S.wait("g", mm_done[buf])
            w_ld[buf] = S.dma("g", lambda g, wt=wt, wb=wb: g.dma_start(
                out=wb, in_=w_in[wt]))
            S.waits("t", [w_ld[buf], xn_tok, prev_evac])
            for kc in range(KC):
                for gi, (c0, wd) in enumerate(PG):
                    S.op("t", lambda t, kc=kc, gi=gi, c0=c0, wd=wd, wb=wb:
                         t.matmul(pb[gi][:, 0:wd], wb[:, kc * P:(kc + 1) * P],
                                  xns(kc, c0, wd),
                                  start=(kc == 0), stop=(kc == KC - 1)))
            mm_done[buf] = S.now("t")
            S.wait("v", mm_done[buf])
            if wt == NPT - 1:
                for gi, (c0, wd) in enumerate(PG):
                    S.op("v", lambda v, gi=gi, c0=c0, wd=wd: v.tensor_copy(
                        dtres[0:64, c0:c0 + wd], pb[gi][0:64, 0:wd]))
            else:
                for gi, (c0, wd) in enumerate(PG):
                    sb = bs(o_pst[gi % 4], P, 512)
                    S.wait("v", st_tok[gi % 4])
                    S.op("v", lambda v, gi=gi, wd=wd, sb=sb: v.tensor_copy(
                        sb[:, 0:wd], pb[gi][:, 0:wd]))
                    tkv = S.now("v")
                    S.wait("s", tkv)
                    st_tok[gi % 4] = S.dma(
                        "s", lambda s, wt=wt, c0=c0, wd=wd, sb=sb: s.dma_start(
                            out=projs[wt][:, c0:c0 + wd], in_=sb[:, 0:wd]))
            prev_evac = S.now("v")
        proj_v = S.now("v")
        proj_t = S.now("t")
        proj_s = S.now("s", "d")
        if TAPS:
            S.wait("g", proj_s)
            S.dma("g", lambda g: g.dma_start(out=taps["tp_proj"], in_=projs[64]))

        S.phase()
        A_.reset(MLN)
        o_acat = A_.al(H * 2)
        o_bcat = A_.al(H * 2)
        o_wrl = A_.al(256 * 2)
        o_hw = A_.al(T * 2)
        o_wbb = [A_.al(H * 2) for _ in range(2)]
        o_crev = A_.al(P * 2)
        o_combB = A_.al(8 * P * 2)
        o_rt = A_.al(48 * 4)
        o_fstF = [A_.al(512 * 4) for _ in range(2)]

        S.waits("g", [proj_t, proj_v, proj_s])
        tk_ac = S.dma("g", lambda g: g.dma_start(out=bs(o_acat, P, H), in_=acat))
        tk_bc = S.dma("g", lambda g: g.dma_start(out=bs(o_bcat, P, H), in_=bcat))
        tk_wr = S.dma("g", lambda g: g.dma_start(out=bs(o_wrl, P, 256), in_=wrl))
        pe = fs(o_rt, P, 8)
        zap = fs(o_rt + 32, P, 8)
        comb = fs(o_rt + 64, P, 8)
        den = fs(o_rt + 96, P, 1)
        mrep = fs(o_rt + 128, P, 8)
        hw_tok = None
        prev_pe = None
        prev_hw = None
        prev_ct = None
        prev_cr = None
        S.waits("t", ld0)
        for g16 in range(16):
            tc0 = 3 + g16 * P
            S.waits("t", [tk_wr, xn_tok, prev_pe])
            for kc in range(KC):
                S.op("t", lambda t, kc=kc, tc0=tc0: t.matmul(
                    pb[5][:, 0:8], xns(kc, tc0, P),
                    bs(o_wrl, P, 256)[:, kc * 8:kc * 8 + 8],
                    start=(kc == 0), stop=(kc == KC - 1)))
            tk_lg = S.now("t")
            S.wait("a", tk_lg)
            prev_pe = tk_pe = S.op("a", lambda a: a.activation(
                pe, pb[5][:, 0:8], AF.Exp))
            S.wait("v", tk_pe)
            S.op("v", lambda v: v.max(out=mrep, in_=pe))
            S.op("v", lambda v: v.engine_nop())
            tk_den = S.op("v", lambda v: v.tensor_scalar(
                den, mrep[:, 0:1], mrep[:, 1:2], None, op0=ALU.add))
            S.wait("a", tk_den)
            S.op("a", lambda a: a.activation(den, den, AF.Ln))
            tk_rd = S.op("a", lambda a: a.activation(den, den, AF.Exp, scale=-1.0))
            S.op("v", lambda v: v.memset(mrep[:, 2:8], 0.0))
            S.op("v", lambda v: v.tensor_copy(zap, pe))
            S.op("v", lambda v: v.match_replace(out=zap, in_to_replace=mrep,
                                                in_values=zap, imm_value=0.0))
            S.op("v", lambda v: v.engine_nop())
            S.op("v", lambda v: v.tensor_tensor(comb, pe, zap, ALU.subtract))
            S.wait("v", tk_rd)
            tk_cb = S.op("v", lambda v: v.tensor_scalar(
                comb, comb, den[:, 0:1], None, op0=ALU.mult))
            S.waits("t", [tk_cb, prev_ct])
            tk_ct = S.op("t", lambda t: t.transpose(pb[6][0:8, 0:P], comb, identF))
            S.wait("v", tk_ct)
            prev_ct = tk_ctv = S.op("v", lambda v: v.tensor_copy(
                bs(o_combB, 8, P), pb[6][0:8, 0:P]))
            S.waits("t", [tk_ctv, prev_cr])
            tk_cr = S.op("t", lambda t: t.matmul(
                pb[6][:, P:2 * P], E8t, bs(o_combB, 8, P), start=True, stop=True))
            S.wait("v", tk_cr)
            prev_cr = tk_crv = S.op("v", lambda v: v.tensor_copy(
                bs(o_crev, P, P), pb[6][:, P:2 * P]))
            S.waits("t", [tk_ac, prev_hw, tk_pe])
            for kc in range(KC):
                S.op("t", lambda t, kc=kc, tc0=tc0: t.matmul(
                    pb[5][:, P:2 * P], bs(o_acat, P, H)[:, kc * P:(kc + 1) * P],
                    xns(kc, tc0, P), start=(kc == 0), stop=(kc == KC - 1)))
            tk_h = S.now("t")
            S.waits("v", [tk_h, tk_crv])
            prev_hw = hw_tok = S.op("v", lambda v, g16=g16: v.tensor_tensor(
                bs(o_hw, P, T)[:, g16 * P:(g16 + 1) * P],
                pb[5][:, P:2 * P], bs(o_crev, P, P), ALU.mult))
        if TAPS:
            S.wait("g", hw_tok)
            S.dma("g", lambda g: g.dma_start(out=taps["tp_hw"], in_=bs(o_hw, P, T)))
        wb_ld = [None, None]
        wb_done = [None, None]
        prev_fevac = None
        fst_tok = [None, None]
        for ot in range(32):
            buf = ot % 2
            wbb = bs(o_wbb[buf], P, H)
            S.wait("g", wb_done[buf])
            wb_ld[buf] = S.dma("g", lambda g, ot=ot, wbb=wbb: g.dma_start(
                out=wbb, in_=w_base[ot]))
            S.waits("t", [wb_ld[buf], hw_tok, tk_bc, prev_fevac])
            for kc in range(KC):
                for g4 in range(4):
                    S.op("t", lambda t, kc=kc, g4=g4, wbb=wbb: t.matmul(
                        pb[g4][:, 0:512], wbb[:, kc * P:(kc + 1) * P],
                        xns(kc, 3 + g4 * 512, 512), start=(kc == 0), stop=False))
            for g4 in range(4):
                S.op("t", lambda t, ot=ot, g4=g4: t.matmul(
                    pb[g4][:, 0:512], bs(o_bcat, P, H)[:, ot * P:(ot + 1) * P],
                    bs(o_hw, P, T)[:, g4 * 512:(g4 + 1) * 512],
                    start=False, stop=True))
            wb_done[buf] = S.now("t")
            S.wait("v", wb_done[buf])
            for g4 in range(4):
                fb = g4 % 2
                fsb = fs(o_fstF[fb], P, 512)
                S.wait("v", fst_tok[fb])
                S.op("v", lambda v, ot=ot, g4=g4, fsb=fsb: v.tensor_scalar(
                    fsb, pb[g4][:, 0:512], bbT[:, ot:ot + 1], None, op0=ALU.add))
                tkv = S.now("v")
                S.wait("s", tkv)
                fst_tok[fb] = S.dma("s", lambda s, ot=ot, g4=g4, fsb=fsb:
                                    s.dma_start(out=ffnout[ot]
                                                [:, g4 * 512:(g4 + 1) * 512],
                                                in_=fsb))
            prev_fevac = S.now("v")
        ffn_s = S.now("s", "d")
        if TAPS:
            S.wait("g", ffn_s)
            S.dma("g", lambda g: g.dma_start(out=taps["tp_ffn"], in_=ffnout[0]))

        S.phase()
        A_.reset(M1)
        o_dtsp = A_.al(T * 4)
        o_dA = A_.al(T * 4)
        o_cd = A_.al(T * 4)
        o_sm = A_.al(1060 * 4)
        o_tT = A_.al(3 * NCH * 64 * 4)
        o_Hst = A_.al(NH * P * 4)
        o_Bc = A_.al(NCH * P * 2)
        o_BTc = A_.al(NCH * P * 2)
        o_Cc = A_.al(NCH * P * 2)
        o_Hg = A_.al(NH * P * 2)
        MSC = A_.mark()
        dtsp = fs(o_dtsp, 64, T)
        dAt = fs(o_dA, 64, T)
        cdt = fs(o_cd, 64, T)
        clc = fs(o_sm, 64, P)
        wsufc = fs(o_sm + 128 * 4, 64, P)
        zeros64 = fs(o_sm + 256 * 4, 64, P)
        Pg = fs(o_sm + 384 * 4, 64, 16)
        Pincl = fs(o_sm + 400 * 4, 64, 16)
        Pex = fs(o_sm + 416 * 4, 64, 16)
        o_agc = o_sm + 432 * 4
        negeA = fs(o_negeA, 64, 1)

        def Hs(h):
            return FV[:, o_Hst // 4 + h * P: o_Hst // 4 + (h + 1) * P]

        def Hgs(h):
            return arB[:, o_Hg // 2 + h * P: o_Hg // 2 + (h + 1) * P]

        def dtTs(c):
            return fs(o_tT + c * 64 * 4, P, 64)

        def wsufTs(c):
            return fs(o_tT + (NCH + c) * 64 * 4, P, 64)

        def DrepTs(c):
            return fs(o_tT + (2 * NCH + c) * 64 * 4, P, 64)

        S.wait("v", proj_v)
        S.op("v", lambda v: v.memset(zeros64[:], 0.0))
        tk0 = S.op("v", lambda v: v.tensor_scalar(
            dtsp, dtres[0:64, 3:3 + T], fs(o_dtbias, 64, 1)[:, 0:1],
            None, op0=ALU.add))
        S.wait("a", tk0)
        S.op("a", lambda a: a.activation(dAt, dtsp, AF.Abs))
        S.op("a", lambda a: a.activation(dAt, dAt, AF.Exp, scale=-1.0))
        S.op("a", lambda a: a.activation(dAt, dAt, AF.Ln, bias=one1[0:64, 0:1]))
        tk_r = S.op("a", lambda a: a.activation(cdt, dtsp, AF.Relu))
        S.wait("v", tk_r)
        S.op("v", lambda v: v.tensor_tensor(dtsp, cdt, dAt, ALU.add))
        tk_sp = S.now("v")
        S.wait("a", tk_sp)
        tk_ea = S.op("a", lambda a: a.activation(negeA, fs(o_alog, 64, 1), AF.Exp))
        S.wait("v", tk_ea)
        S.op("v", lambda v: v.tensor_scalar(negeA, negeA, -1.0, None, op0=ALU.mult))
        tk_la = S.op("v", lambda v: v.tensor_scalar(
            dAt, dtsp, negeA[:, 0:1], None, op0=ALU.mult))
        tk_lacp = S.op("v", lambda v: v.tensor_copy(cdt, dAt))
        S.wait("a", tk_lacp)
        tk_da = S.op("a", lambda a: a.activation(dAt, dAt, AF.Exp))
        S.wait("v", tk_da)
        tr_ev = None
        for c in range(NCH):
            ch = slice(c * P, (c + 1) * P)
            S.op("v", lambda v, ch=ch: v.tensor_tensor_scan(
                clc, onesF[0:64, :], cdt[:, ch], 0.0, op0=ALU.mult, op1=ALU.add))
            S.op("v", lambda v: v.engine_nop())
            tk_cl = S.now("v")
            S.wait("a", tk_cl)
            tk_ws = S.op("a", lambda a: a.activation(
                wsufc, clc, AF.Exp, bias=clc[:, 127:128], scale=-1.0))
            S.op("v", lambda v, ch=ch: v.tensor_tensor_scan(
                cdt[:, ch], dAt[:, ch], zeros64[:], 1.0, op0=ALU.mult, op1=ALU.add))
            S.op("v", lambda v: v.engine_nop())
            S.op("v", lambda v, c=c: v.tensor_copy(
                Pg[:, c:c + 1], cdt[:, c * P + 127:c * P + 128]))
            dmt = fs(o_sm + 512 * 4, 64, P)
            S.waits("v", [tk_ws])
            tk_dm = S.op("v", lambda v, c=c, dmt=dmt: v.tensor_scalar(
                dmt, onesF[0:64, :], cdt[:, c * P + 127:c * P + 128], None,
                op0=ALU.mult))
            S.waits("t", [tk_ws, tk_dm, tr_ev])
            S.op("t", lambda t, ch=ch: t.transpose(
                pb[6][:, 0:64], dtsp[:, ch], identF[0:64, 0:64]))
            S.op("t", lambda t: t.transpose(
                pb[6][:, 64:128], wsufc, identF[0:64, 0:64]))
            tk_tr = S.op("t", lambda t, dmt=dmt: t.transpose(
                pb[6][:, 128:192], dmt, identF[0:64, 0:64]))
            S.wait("v", tk_tr)
            S.op("v", lambda v, c=c: v.tensor_copy(dtTs(c), pb[6][:, 0:64]))
            S.op("v", lambda v, c=c: v.tensor_copy(wsufTs(c), pb[6][:, 64:128]))
            S.op("v", lambda v, c=c: v.tensor_copy(DrepTs(c), pb[6][:, 128:192]))
            tr_ev = S.now("v")
            S.wait("a", tk_tr)
        S.op("v", lambda v: v.tensor_tensor_scan(
            Pincl, Pg, zeros64[:, 0:16], 1.0, op0=ALU.mult, op1=ALU.add))
        S.op("v", lambda v: v.engine_nop())
        S.op("v", lambda v: v.memset(Pex[:, 0:1], 1.0))
        tk_pex = S.op("v", lambda v: v.tensor_copy(Pex[:, 1:16], Pincl[:, 0:15]))
        if TAPS:
            S.wait("g", tk_pex)
            S.dma("g", lambda g: g.dma_start(out=taps["tp_dtsp"], in_=dtsp))
            S.dma("g", lambda g: g.dma_start(out=taps["tp_da"], in_=dAt))
            S.dma("g", lambda g: g.dma_start(out=taps["tp_cd"], in_=cdt))

        o_xbcst = A_.al(66 * 131 * 2)
        o_cvx = A_.al(64 * P * 2)
        o_cacc = A_.al(3 * P * 4)
        o_xtdb = A_.al(4 * P * 2)

        def cvxs(ti):
            return arB[:, o_cvx // 2 + ti * P: o_cvx // 2 + (ti + 1) * P]

        tk_h0 = S.op("v", lambda v: v.memset(FV[:, o_Hst // 4:o_Hst // 4 + NH * P], 0.0))
        gate_v = tk_h0
        gate_t = None
        gate_a = None
        xtd_cons = defaultdict(lambda: None)
        xps_cons = defaultdict(lambda: None)
        sps_cons = defaultdict(lambda: None)
        for c in range(NCH):
            S.waits("g", [gate_v, gate_t, gate_a, proj_s])
            rd = []
            for ti in range(66):
                rd.append(S.dma("g", lambda g, c=c, ti=ti: g.dma_start(
                    out=bs(o_xbcst + ti * 131 * 2, P, 131),
                    in_=projs[64 + ti][:, c * P:c * P + 131])))
            rd_done = rd[-1]
            cvt = []
            prev_a1 = None
            for ti in range(66):
                acc = fs(o_cacc + (ti % 2) * P * 4, P, P)
                gt = fs(o_cacc + 2 * P * 4, P, P)
                S.waits("v", [rd_done, prev_a1])
                S.op("v", lambda v, c=c, ti=ti, acc=acc: v.tensor_scalar(
                    acc, bs(o_xbcst + ti * 131 * 2, P, 131)[:, 0:P],
                    convW[:, ti * 4:ti * 4 + 1], None, op0=ALU.mult))
                for k in range(1, 4):
                    S.op("v", lambda v, ti=ti, k=k, acc=acc: v.scalar_tensor_tensor(
                        acc, bs(o_xbcst + ti * 131 * 2, P, 131)[:, k:k + P],
                        convW[:, ti * 4 + k:ti * 4 + k + 1], acc,
                        op0=ALU.mult, op1=ALU.add))
                tk_acc = S.now("v")
                S.wait("a", tk_acc)
                prev_a1 = S.op("a", lambda a, acc=acc, gt=gt: a.activation(
                    gt, acc, AF.Exp, scale=-1.0))
                S.op("a", lambda a, gt=gt: a.activation(
                    gt, gt, AF.Ln, bias=one1[:, 0:1]))
                tk_g3 = S.op("a", lambda a, gt=gt: a.activation(
                    gt, gt, AF.Exp, scale=-1.0))
                if ti < 64:
                    dst = cvxs(ti)
                elif ti == 64:
                    dst = bs(o_Bc + c * P * 2, P, P)
                else:
                    dst = bs(o_Cc + c * P * 2, P, P)
                S.wait("v", tk_g3)
                tk_cv = S.op("v", lambda v, dst=dst, acc=acc, gt=gt: v.tensor_tensor(
                    dst, acc, gt, ALU.mult))
                cvt.append(tk_cv)
                if ti < 64:
                    S.wait("s", tk_cv)
                    S.dma("s", lambda s, c=c, ti=ti: s.dma_start(
                        out=convx[ti][:, c * P:(c + 1) * P], in_=cvxs(ti)))
            S.waits("t", [cvt[64], xps_cons[("bt",)]])
            tk_btp = S.op("t", lambda t, c=c: t.transpose(
                pbf[:, 512:640], bs(o_Bc + c * P * 2, P, P), identB))
            S.wait("v", tk_btp)
            tk_bt = S.op("v", lambda v, c=c: v.tensor_copy(
                bs(o_BTc + c * P * 2, P, P), pbf[:, 512:640]))
            xps_cons[("bt",)] = tk_bt
            for h in range(NH):
                sl = h % 4
                xps = pbf[:, sl * P:(sl + 1) * P]
                S.waits("t", [cvt[h], xps_cons[sl]])
                tk_xt = S.op("t", lambda t, h=h, xps=xps: t.transpose(
                    xps, cvxs(h), identB))
                xtd = bs(o_xtdb + sl * P * 2, P, P)
                S.waits("v", [tk_xt, xtd_cons[sl]])
                tk_xtd = S.op("v", lambda v, h=h, c=c, xtd=xtd, xps=xps:
                              v.tensor_scalar(xtd, xps, dtTs(c)[:, h:h + 1],
                                              None, op0=ALU.mult))
                xps_cons[sl] = tk_xtd
                S.wait("s", tk_xtd)
                tk_wxtd = S.dma("s", lambda s, c=c, h=h, xtd=xtd: s.dma_start(
                    out=xtds[c][h], in_=xtd))
                btw = bs(o_xbcst + (h % 2) * 131 * 2, P, P)
                S.waits("v", [tk_bt, sps_cons[("btw", h % 2)]])
                tk_btw = S.op("v", lambda v, c=c, h=h, btw=btw: v.tensor_scalar(
                    btw, bs(o_BTc + c * P * 2, P, P), wsufTs(c)[:, h:h + 1],
                    None, op0=ALU.mult))
                ssl = pb[h % 2][:, 0:P]
                S.waits("t", [tk_btw, tk_xtd, sps_cons[h % 2]])
                tk_s = S.op("t", lambda t, btw=btw, xtd=xtd, ssl=ssl: t.matmul(
                    ssl, btw, xtd, start=True, stop=True))
                sps_cons[("btw", h % 2)] = tk_s
                xtd_cons[sl] = (tk_s if True else None)
                S.waits("v", [tk_s])
                tk_hu = S.op("v", lambda v, h=h, c=c, ssl=ssl:
                             v.scalar_tensor_tensor(
                                 Hs(h), Hs(h), DrepTs(c)[:, h:h + 1], ssl,
                                 op0=ALU.mult, op1=ALU.add))
                sps_cons[h % 2] = tk_hu
                xtd_cons[sl] = tk_wxtd
                S.wait("v", tk_wxtd)
            gate_v = S.now("v")
            gate_t = S.now("t")
            gate_a = S.now("a")
        pass1_v = S.now("v")
        pass1_s = S.now("s", "d")
        if TAPS:
            S.waits("g", [pass1_v, pass1_s])
            S.dma("g", lambda g: g.dma_start(out=taps["tp_h0"], in_=Hs(0)))
            S.dma("g", lambda g: g.dma_start(out=taps["tp_cvx"], in_=convx[0]))
            S.dma("g", lambda g: g.dma_start(out=taps["tp_xtd"], in_=xtds[0][0]))

        S.phase()
        S.wait("g", pass1_v)
        for h in range(NH):
            S.dma("g", lambda g, h=h: g.dma_start(
                out=ag_in[h * P:(h + 1) * P, :], in_=Hs(h)))
        dmt0 = fs(o_sm + 512 * 4, 64, P)
        S.wait("v", tk_pex)
        tk_dbc = S.op("v", lambda v, dmt0=dmt0: v.tensor_scalar(
            dmt0, onesF[0:64, :], Pincl[:, 15:16], None, op0=ALU.mult))
        S.wait("g", tk_dbc)
        tk_dp = S.dma("g", lambda g, dmt0=dmt0: g.dma_start(
            out=ag_in[64 * P:64 * P + 64, :], in_=dmt0))
        S.wait("g", tk_dp)
        tk_cc = S.op("g", lambda g: g.collective_compute(
            "AllGather", ALU.bypass, replica_groups=rg,
            ins=[ag_in], outs=[ag_out]))
        S.wait("g", tk_cc)
        o_dgb = o_sm + 920 * 4
        Dg = [fs(o_agc + q * 4, 64, 1) for q in range(4)]
        d1 = fs(o_agc + 16, 64, 1)
        d2 = fs(o_agc + 20, 64, 1)
        d12 = fs(o_agc + 24, 64, 1)
        on1 = fs(o_agc + 28, 64, 1)
        cq = [fs(o_agc + 32 + q * 4, 64, 1) for q in range(4)]
        o_crepT = o_sm + 640 * 4
        dg_toks = []
        for q in range(4):
            tkb = S.dma("g", lambda g, q=q: g.dma_start(
                out=fs(o_dgb, 64, P),
                in_=ag_out[q * AGB + 64 * P:q * AGB + 64 * P + 64, :]))
            S.wait("v", tkb)
            dg_toks.append(S.op("v", lambda v, q=q: v.tensor_copy(
                Dg[q], fs(o_dgb, 64, P)[:, 0:1])))
        S.phase()
        am = fs(o_amask, 64, 12)
        S.waits("v", dg_toks)
        S.op("v", lambda v: v.memset(on1, 1.0))
        S.op("v", lambda v: v.tensor_scalar(d1, Dg[0], am[:, 0:1], None,
                                            op0=ALU.mult))
        for q in range(1, 4):
            S.op("v", lambda v, q=q: v.scalar_tensor_tensor(
                d1, Dg[q], am[:, q:q + 1], d1, op0=ALU.mult, op1=ALU.add))
        S.op("v", lambda v: v.tensor_scalar(d2, Dg[0], am[:, 4:5], None,
                                            op0=ALU.mult))
        for q in range(1, 4):
            S.op("v", lambda v, q=q: v.scalar_tensor_tensor(
                d2, Dg[q], am[:, 4 + q:5 + q], d2, op0=ALU.mult, op1=ALU.add))
        S.op("v", lambda v: v.tensor_tensor(d12, d1, d2, ALU.mult))
        prev_ctr = None
        for q in range(4):
            S.op("v", lambda v, q=q: v.tensor_scalar(
                cq[q], on1, am[:, q:q + 1], None, op0=ALU.mult))
            S.op("v", lambda v, q=q: v.scalar_tensor_tensor(
                cq[q], d1, am[:, 4 + q:5 + q], cq[q], op0=ALU.mult, op1=ALU.add))
            S.op("v", lambda v, q=q: v.scalar_tensor_tensor(
                cq[q], d12, am[:, 8 + q:9 + q], cq[q], op0=ALU.mult, op1=ALU.add))
            dmt = fs(o_sm + 512 * 4, 64, P)
            tk_cm = S.op("v", lambda v, q=q, dmt=dmt: v.tensor_scalar(
                dmt, onesF[0:64, :], cq[q][:, 0:1], None, op0=ALU.mult))
            S.waits("t", [tk_cm, prev_ctr])
            tk_ctr = S.op("t", lambda t, q=q, dmt=dmt: t.transpose(
                pb[6][:, 192 + q * 64:256 + q * 64], dmt, identF[0:64, 0:64]))
            S.wait("v", tk_ctr)
            prev_ctr = S.op("v", lambda v, q=q: v.tensor_copy(
                fs(o_crepT + q * 64 * 4, P, 64), pb[6][:, 192 + q * 64:256 + q * 64]))
        S.phase()
        o_tq = o_xbcst
        tq_cons = defaultdict(lambda: None)
        hg_done = None
        for h in range(NH):
            acc_t = None
            for q in range(4):
                slot = (h * 4 + q) % 8
                tq = bs(o_tq + slot * P * 2, P, P)
                S.wait("g", tq_cons[slot])
                tk_tq = S.dma("g", lambda g, q=q, h=h, tq=tq: g.dma_start(
                    out=tq, in_=ag_out[q * AGB + h * P:q * AGB + (h + 1) * P, :]))
                S.waits("v", [tk_tq, prev_ctr])
                if q == 0:
                    acc_t = S.op("v", lambda v, h=h, q=q, tq=tq: v.tensor_scalar(
                        Hgs(h), tq, fs(o_crepT + q * 64 * 4, P, 64)[:, h:h + 1],
                        None, op0=ALU.mult))
                else:
                    acc_t = S.op("v", lambda v, h=h, q=q, tq=tq:
                                 v.scalar_tensor_tensor(
                                     Hgs(h), tq,
                                     fs(o_crepT + q * 64 * 4, P, 64)[:, h:h + 1],
                                     Hgs(h), op0=ALU.mult, op1=ALU.add))
                tq_cons[slot] = acc_t
            hg_done = acc_t
            if h in (0, 7, 15, 31, 47):
                S.phase()
        if TAPS:
            S.wait("g", hg_done)
            S.dma("g", lambda g: g.dma_start(out=taps["tp_hg0"], in_=Hgs(0)))

        S.phase()
        A_.reset(MSC)
        o_zst = A_.al(2 * P * 2)
        o_cvx2 = A_.al(4 * P * 2)
        o_xtd2 = A_.al(4 * P * 2)
        o_ar = A_.al(32 * P * 2)
        o_cdr = A_.al(32 * P * 2)
        o_mt = A_.al(2 * P * 2)
        o_mi = A_.al(2 * P * 2)
        o_hfb = A_.al(2 * P * 2)
        o_ynst = A_.al(2 * P * 2)
        o_sqb = A_.al(2 * P * 2)
        o_Lt = A_.al(2 * P * 4)
        o_y1 = A_.al(2 * P * 4)
        o_gat = A_.al(2 * P * 4)
        o_act2 = A_.al(3 * 64 * 4)
        o_rrep = A_.al(P * 4)
        o_rsm = A_.al(8 * 4)
        o_tapst = A_.al(P * 4)

        tk_h02 = S.op("v", lambda v: v.memset(
            FV[:, o_Hst // 4:o_Hst // 4 + NH * P], 0.0))
        S.wait("v", hg_done)
        gate2 = {"v": tk_h02, "t": None, "a": None}
        cons = defaultdict(lambda: None)
        tr2_ev = None
        rms_rd = None
        for c in range(NCH):
            S.waits("g", [gate2["v"], gate2["t"], gate2["a"], pass1_s])
            aT = fs(o_act2, P, 64)
            cdT = fs(o_act2 + 64 * 4, P, 64)
            prT = fs(o_act2 + 128 * 4, P, 64)
            S.waits("t", [tk_da, tr2_ev, cons[("rtp",)]])
            S.op("t", lambda t, c=c: t.transpose(
                pb[6][:, 0:64], dAt[:, c * P:(c + 1) * P], identF[0:64, 0:64]))
            tk_t2 = S.op("t", lambda t, c=c: t.transpose(
                pb[6][:, 64:128], cdt[:, c * P:(c + 1) * P], identF[0:64, 0:64]))
            S.wait("v", tk_t2)
            S.op("v", lambda v, aT=aT: v.tensor_copy(aT, pb[6][:, 0:64]))
            tk_cdT = S.op("v", lambda v, cdT=cdT: v.tensor_copy(cdT, pb[6][:, 64:128]))
            dmt = fs(o_sm + 512 * 4, 64, P)
            S.wait("v", tk_pex)
            tk_pm = S.op("v", lambda v, c=c, dmt=dmt: v.tensor_scalar(
                dmt, onesF[0:64, :], Pex[:, c:c + 1], None, op0=ALU.mult))
            S.wait("t", tk_pm)
            tk_ptr = S.op("t", lambda t, dmt=dmt: t.transpose(
                pb[6][:, 128:192], dmt, identF[0:64, 0:64]))
            S.wait("v", tk_ptr)
            tk_prT = S.op("v", lambda v, prT=prT: v.tensor_copy(
                prT, pb[6][:, 128:192]))
            tr2_ev = S.now("v")
            S.waits("t", [pass1_v, cons[("G",)], rms_rd])
            tk_G = S.op("t", lambda t, c=c: t.matmul(
                pb[5][:, 0:P], bs(o_Bc + c * P * 2, P, P),
                bs(o_Cc + c * P * 2, P, P), start=True, stop=True))
            rms_tok = None
            for half in range(2):
                hh0 = half * 32
                ab_dma, cdb_dma, cdr_rdy = [], {}, []
                for i in range(32):
                    h = hh0 + i
                    S.waits("v", [tr2_ev, cons[("ar", i)]])
                    tk_a1 = S.op("v", lambda v, i=i, h=h, aT=aT: v.tensor_scalar(
                        bs(o_ar + i * P * 2, P, P), onesF[:], aT[:, h:h + 1],
                        None, op0=ALU.mult))
                    S.waits("v", [cons[("cdr", i)]])
                    tk_c1 = S.op("v", lambda v, i=i, h=h, cdT=cdT: v.tensor_scalar(
                        bs(o_cdr + i * P * 2, P, P), onesF[:], cdT[:, h:h + 1],
                        None, op0=ALU.mult))
                    cdr_rdy.append(tk_c1)
                    S.waits("s", [tk_a1, cons[("ab", i)]])
                    ab_dma.append(S.dma("s", lambda s, i=i: s.dma_start_transpose(
                        out=trD[:, i * P:(i + 1) * P],
                        in_=bs(o_ar + i * P * 2, P, P))))

                def issue_cdb(j):
                    if j >= 32:
                        return
                    S.waits("s", [cdr_rdy[j], cons[("cdb", j % 2)]])
                    cdb_dma[j] = S.dma("s", lambda s, j=j: s.dma_start_transpose(
                        out=trD[:, (32 + j % 2) * P:(33 + j % 2) * P],
                        in_=bs(o_cdr + j * P * 2, P, P)))

                issue_cdb(0)
                issue_cdb(1)
                for i in range(32):
                    h = hh0 + i
                    sl = i % 2
                    Lt = fs(o_Lt + sl * P * 4, P, P)
                    S.waits("v", [ab_dma[i], cons[("Lt", sl)]])
                    tk_sc = S.op("v", lambda v, i=i, Lt=Lt: v.tensor_tensor_scan(
                        Lt, trD[:, i * P:(i + 1) * P], identF, 0.0,
                        op0=ALU.mult, op1=ALU.add))
                    S.op("v", lambda v: v.engine_nop())
                    cons[("ab", i)] = tk_sc
                    mt = bs(o_mt + sl * P * 2, P, P)
                    S.waits("v", [tk_G, cons[("mt", sl)]])
                    tk_mt = S.op("v", lambda v, Lt=Lt, mt=mt: v.tensor_tensor(
                        mt, Lt, pb[5][:, 0:P], ALU.mult))
                    cons[("Lt", sl)] = tk_mt
                    mi = bs(o_mi + sl * P * 2, P, P)
                    S.waits("v", [cdb_dma[i], cons[("mi", sl)]])
                    tk_mi = S.op("v", lambda v, c=c, i=i, mi=mi: v.tensor_tensor(
                        mi, bs(o_Cc + c * P * 2, P, P),
                        trD[:, (32 + i % 2) * P:(33 + i % 2) * P], ALU.mult))
                    cons[("cdb", i % 2)] = tk_mi
                    cons[("cdr", i)] = cdb_dma[i]
                    issue_cdb(i + 2)
                    hfb = bs(o_hfb + sl * P * 2, P, P)
                    S.waits("v", [tk_prT, cons[("hf", sl)]])
                    tk_hf = S.op("v", lambda v, h=h, hfb=hfb, prT=prT:
                                 v.scalar_tensor_tensor(
                                     hfb, Hgs(h), prT[:, h:h + 1], Hs(h),
                                     op0=ALU.mult, op1=ALU.add))
                    btw = bs(o_ar + i * P * 2, P, P)
                    S.waits("v", [ab_dma[i], cons[("btwu", sl)]])
                    tk_btw = S.op("v", lambda v, c=c, h=h, btw=btw: v.tensor_scalar(
                        btw, bs(o_BTc + c * P * 2, P, P), wsufTs(c)[:, h:h + 1],
                        None, op0=ALU.mult))
                    cons[("ar", i)] = tk_btw
                    zt = bs(o_zst + sl * P * 2, P, P)
                    S.waits("g", [cons[("zt", sl)], proj_s])
                    rd_z = S.dma("g", lambda g, c=c, h=h, zt=zt: g.dma_start(
                        out=zt, in_=projs[h][:, c * P + 3:c * P + 3 + P]))
                    cvsl = i % 4
                    cvr = bs(o_cvx2 + cvsl * P * 2, P, P)
                    S.wait("g", cons[("cvr", cvsl)])
                    rd_c = S.dma("g", lambda g, c=c, h=h, cvr=cvr: g.dma_start(
                        out=cvr, in_=convx[h][:, c * P:(c + 1) * P]))
                    xsl = i % 4
                    xt2 = bs(o_xtd2 + xsl * P * 2, P, P)
                    S.wait("g", cons[("xt2", xsl)])
                    rd_x = S.dma("g", lambda g, c=c, h=h, xt2=xt2: g.dma_start(
                        out=xt2, in_=xtds[c][h]))
                    ysl = pb[2 + sl][:, 0:P]
                    S.waits("t", [tk_mt, rd_x, cons[("y", sl)]])
                    S.op("t", lambda t, mt=mt, xt2=xt2, ysl=ysl: t.matmul(
                        ysl, xt2, mt, start=True, stop=False))
                    S.waits("t", [tk_hf, tk_mi])
                    tk_y = S.op("t", lambda t, hfb=hfb, mi=mi, ysl=ysl: t.matmul(
                        ysl, hfb, mi, start=False, stop=True))
                    cons[("mt", sl)] = tk_y
                    cons[("mi", sl)] = tk_y
                    cons[("hf", sl)] = tk_y
                    ssl = pb[1 if sl == 0 else 4][:, 0:P]
                    S.waits("t", [tk_btw, cons[("S", sl)]])
                    tk_s = S.op("t", lambda t, btw=btw, xt2=xt2, ssl=ssl: t.matmul(
                        ssl, btw, xt2, start=True, stop=True))
                    cons[("btwu", sl)] = tk_s
                    S.wait("v", tk_s)
                    tk_hu = S.op("v", lambda v, h=h, c=c, ssl=ssl:
                                 v.scalar_tensor_tensor(
                                     Hs(h), Hs(h), DrepTs(c)[:, h:h + 1], ssl,
                                     op0=ALU.mult, op1=ALU.add))
                    cons[("S", sl)] = tk_hu
                    cons[("xt2", xsl)] = tk_s
                    y1 = fs(o_y1 + sl * P * 4, P, P)
                    S.waits("v", [tk_y, rd_c, cons[("y1", sl)]])
                    tk_ye = S.op("v", lambda v, y1=y1, cvr=cvr, h=h, ysl=ysl:
                                 v.scalar_tensor_tensor(
                                     y1, cvr, dprepT[:, h:h + 1], ysl,
                                     op0=ALU.mult, op1=ALU.add))
                    cons[("y", sl)] = tk_ye
                    cons[("cvr", cvsl)] = tk_ye
                    if TAPS and c == 0 and h == 0:
                        S.wait("g", tk_ye)
                        t_y1tap = S.dma("g", lambda g, y1=y1: g.dma_start(
                            out=taps["tp_y1"], in_=y1))
                        S.wait("v", tk_y)
                        tk_tps = S.op("v", lambda v, ysl=ysl: v.tensor_copy(
                            fs(o_tapst, P, P), ysl))
                        S.waits("g", [tk_tps, tk_hf])
                        S.dma("g", lambda g: g.dma_start(
                            out=taps["tp_ps"], in_=fs(o_tapst, P, P)))
                        S.dma("g", lambda g, hfb=hfb: g.dma_start(
                            out=taps["tp_hf"], in_=hfb))
                        S.dma("g", lambda g, xt2=xt2: g.dma_start(
                            out=taps["tp_xt2"], in_=xt2))
                        S.wait("v", t_y1tap)
                    gat = fs(o_gat + sl * P * 4, P, P)
                    S.waits("a", [rd_z, cons[("gat", sl)]])
                    S.op("a", lambda a, zt=zt, gat=gat: a.activation(
                        gat, zt, AF.Exp, scale=-1.0))
                    S.op("a", lambda a, gat=gat: a.activation(
                        gat, gat, AF.Ln, bias=one1[:, 0:1]))
                    tk_g3 = S.op("a", lambda a, gat=gat: a.activation(
                        gat, gat, AF.Exp, scale=-1.0))
                    S.wait("v", tk_g3)
                    S.op("v", lambda v, y1=y1, zt=zt: v.tensor_tensor(
                        y1, y1, zt, ALU.mult))
                    tk_yg = S.op("v", lambda v, y1=y1, gat=gat: v.tensor_tensor(
                        y1, y1, gat, ALU.mult))
                    cons[("gat", sl)] = tk_yg
                    cons[("zt", sl)] = tk_yg
                    if TAPS and c == 0 and h == 0:
                        S.waits("g", [tk_sc, tk_mt, tk_mi, tk_ye, tk_g3, tk_yg])
                        S.dma("g", lambda g, Lt=Lt: g.dma_start(
                            out=taps["tp_lt"], in_=Lt))
                        S.dma("g", lambda g, mt=mt: g.dma_start(
                            out=taps["tp_mt"], in_=mt))
                        S.dma("g", lambda g, mi=mi: g.dma_start(
                            out=taps["tp_mi"], in_=mi))
                        S.dma("g", lambda g, gat=gat: g.dma_start(
                            out=taps["tp_g"], in_=gat))
                        S.dma("g", lambda g, i=i: g.dma_start(
                            out=taps["tp_ab"], in_=trD[:, i * P:(i + 1) * P]))
                    sq = bs(o_sqb + sl * P * 2, P, P)
                    S.waits("a", [tk_yg, cons[("sq", sl)]])
                    tk_sq2 = S.op("a", lambda a, y1=y1, sq=sq: a.activation(
                        sq, y1, AF.Square))
                    S.waits("t", [tk_sq2, cons[("rms",)] if h == 0 else None])
                    rms_tok = S.op("t", lambda t, h=h, sq=sq: t.matmul(
                        pb[0][0:P, 0:1], sq, onescB[:],
                        start=(h == 0), stop=(h == NH - 1)))
                    cons[("sq", sl)] = rms_tok
                    yst = bs(o_ynst + sl * P * 2, P, P)
                    S.waits("v", [cons[("yst", sl)]])
                    tk_yn = S.op("v", lambda v, h=h, y1=y1, yst=yst: v.tensor_scalar(
                        yst, y1, gammaT[:, h:h + 1], None, op0=ALU.mult))
                    cons[("y1", sl)] = tk_yn
                    S.wait("s", tk_yn)
                    cons[("yst", sl)] = S.dma(
                        "s", lambda s, c=c, h=h, yst=yst: s.dma_start(
                            out=ynorm[h][:, c * P:(c + 1) * P], in_=yst))
            rsm = fs(o_rsm, P, 1)
            S.waits("a", [rms_tok, cons[("rsmu",)]])
            S.op("a", lambda a, rsm=rsm: a.activation(
                rsm, pb[0][0:P, 0:1], AF.Ln, scale=1.0 / DSSM,
                bias=epsA[:, 0:1]))
            tk_rs = S.op("a", lambda a, rsm=rsm: a.activation(
                rsm, rsm, AF.Exp, scale=-0.5))
            rms_rd = tk_rs
            cons[("G",)] = S.now("v")
            S.waits("v", [tk_rs, cons[("rrep",)]])
            rr = fs(o_rrep, P, P)
            tk_rr = S.op("v", lambda v, rsm=rsm, rr=rr: v.tensor_scalar(
                rr, onesF[:], rsm[:, 0:1], None, op0=ALU.mult))
            cons[("rsmu",)] = tk_rr
            S.waits("t", [tk_rr, cons[("rtp",)]])
            tk_rt2 = S.op("t", lambda t, rr=rr: t.transpose(
                pb[6][:, 320:448], rr, identF))
            S.wait("v", tk_rt2)
            tk_rb = S.op("v", lambda v, c=c: v.tensor_copy(
                rstdB[:, c * P:(c + 1) * P], pb[6][:, 320:448]))
            cons[("rtp",)] = tk_rb
            cons[("rrep",)] = tk_rt2
            gate2 = {"v": S.now("v"), "t": S.now("t"), "a": S.now("a")}
            if c in (0, 1, 3, 7):
                S.phase()
        pass2_v = S.now("v")
        pass2_s = S.now("s", "d")
        if TAPS:
            S.waits("g", [pass2_v, pass2_s])
            S.dma("g", lambda g: g.dma_start(out=taps["tp_yn"], in_=ynorm[0]))
            S.dma("g", lambda g: g.dma_start(out=taps["tp_rstd"], in_=rstdB))

        S.phase()
        A_.reset(M0)
        o_ynh = A_.al(64 * 1024 * 2)
        o_wob = A_.al(DSSM * 2)
        o_xst = A_.al(512 * 4)
        o_fst2 = A_.al(512 * 4)
        o_ost = [A_.al(512 * 4) for _ in range(2)]

        def ynh(k, c0, w):
            return arB[:, o_ynh // 2 + k * 1024 + c0: o_ynh // 2 + k * 1024 + c0 + w]

        wo_cons = None
        ost_cons = [None, None]
        xst_cons = None
        for half in range(2):
            hc0 = half * 1024
            ynh_toks = []
            S.waits("g", [S.now("t"), pass2_s])
            for k in range(NH):
                ynh_toks.append(S.dma("g", lambda g, k=k, hc0=hc0: g.dma_start(
                    out=ynh(k, 0, 1024), in_=ynorm[k][:, hc0:hc0 + 1024])))
            ynh_rd = ynh_toks[-1]
            for ot in range(32):
                wob = bs(o_wob, P, DSSM)
                S.waits("g", [wo_cons])
                tk_wo = S.dma("g", lambda g, ot=ot, wob=wob: g.dma_start(
                    out=wob, in_=w_out[ot]))
                S.waits("t", [tk_wo, ynh_rd, ost_cons[0], ost_cons[1]])
                for k in range(NH):
                    for g2 in range(2):
                        S.op("t", lambda t, k=k, g2=g2, wob=wob: t.matmul(
                            pb[g2][:, 0:512], wob[:, k * P:(k + 1) * P],
                            ynh(k, g2 * 512, 512),
                            start=(k == 0), stop=(k == NH - 1)))
                wo_cons = S.now("t")
                for g2 in range(2):
                    tc0 = hc0 + g2 * 512
                    sbx = fs(o_xst, P, 512)
                    sbf = fs(o_fst2, P, 512)
                    S.waits("s", [xst_cons])
                    tk_x = S.dma("s", lambda s, ot=ot, tc0=tc0, sbx=sbx: s.dma_start(
                        out=sbx, in_=xh[ot * P:(ot + 1) * P, 3 + tc0:3 + tc0 + 512]))
                    S.wait("s", ffn_s)
                    tk_f = S.dma("s", lambda s, ot=ot, tc0=tc0, sbf=sbf: s.dma_start(
                        out=sbf, in_=ffnout[ot][:, tc0:tc0 + 512]))
                    ob = fs(o_ost[g2], P, 512)
                    S.waits("v", [wo_cons, tk_x, tk_f, pass2_v, ost_cons[g2]])
                    S.op("v", lambda v, g2=g2, tc0=tc0, ob=ob: v.tensor_tensor(
                        ob, pb[g2][:, 0:512], rstdB[:, tc0:tc0 + 512], ALU.mult))
                    S.op("v", lambda v, ob=ob, sbf=sbf: v.tensor_tensor(
                        ob, ob, sbf, ALU.add))
                    tk_o = S.op("v", lambda v, ob=ob, sbx=sbx: v.tensor_tensor(
                        ob, ob, sbx, ALU.add))
                    xst_cons = tk_o
                    S.wait("s", tk_o)
                    ost_cons[g2] = S.dma(
                        "s", lambda s, ot=ot, tc0=tc0, ob=ob: s.dma_start(
                            out=out[ot][:, tc0:tc0 + 512], in_=ob))
        S.phase()
        if PHASES < len(S.marks):
            S.rollback(PHASES)
        S.wait("s", S.now("s", "d"))
        for st in ["t", "v", "a", "g"]:
            S.wait("s", S.now(st, "o"))
        S.wait("t", S.now("s", "d"))
        S.wait("g", S.now("g", "d"))
        S.wait("a", S.now("a", "d"))

        S.replay(block, {"t": "tensor", "v": "vector", "a": "scalar",
                         "g": "gpsimd", "s": "sync"}, osem, dsem)
    return nc


_CACHE = {}


def kernel(L0_attn_hidden_states, L0_attn_W_in, L0_attn_conv_w, L0_attn_dt_bias,
           L0_attn_A_log, L0_attn_D_param, L0_attn_norm_weight, L0_attn_W_out,
           L0_ffn_W_base, L0_ffn_b_base, L0_ffn_W_router, L0_ffn_A, L0_ffn_B):
    x = np.asarray(L0_attn_hidden_states, dtype=np.float32)
    W_in = np.asarray(L0_attn_W_in, dtype=np.float32)
    conv_w = np.asarray(L0_attn_conv_w, dtype=np.float32)
    dt_bias = np.asarray(L0_attn_dt_bias, dtype=np.float32)
    A_log = np.asarray(L0_attn_A_log, dtype=np.float32)
    D_param = np.asarray(L0_attn_D_param, dtype=np.float32)
    norm_w = np.asarray(L0_attn_norm_weight, dtype=np.float32)
    W_out = np.asarray(L0_attn_W_out, dtype=np.float32)
    W_base = np.asarray(L0_ffn_W_base, dtype=np.float32)
    b_base = np.asarray(L0_ffn_b_base, dtype=np.float32)
    W_router = np.asarray(L0_ffn_W_router, dtype=np.float32)
    A = np.asarray(L0_ffn_A, dtype=np.float32)
    B = np.asarray(L0_ffn_B, dtype=np.float32)

    w_in_p = np.ascontiguousarray(
        np.pad(W_in, ((0, 0), (0, PROJ_PAD - PROJ_DIM)))
        .reshape(KC, P, NPT, P).transpose(2, 1, 0, 3).reshape(NPT, P, H))
    w_out_p = np.ascontiguousarray(
        W_out.reshape(64, P, 32, P).transpose(2, 1, 0, 3).reshape(32, P, DSSM))
    w_base_p = np.ascontiguousarray(
        W_base.T.reshape(KC, P, 32, P).transpose(2, 1, 0, 3).reshape(32, P, H))
    acat_p = np.ascontiguousarray(
        A.reshape(P, H).T.reshape(KC, P, P).transpose(1, 0, 2).reshape(P, H))
    bcat_p = np.ascontiguousarray(B.transpose(0, 2, 1).reshape(P, H))
    wrl_p = np.ascontiguousarray(
        W_router.T.reshape(KC, P, 8).transpose(1, 0, 2).reshape(P, 256))
    convw_p = np.ascontiguousarray(
        conv_w.reshape(66, P, 4).transpose(1, 0, 2).reshape(P, 264))
    dtb_p = np.ascontiguousarray(dt_bias.reshape(64, 1))
    alog_p = np.ascontiguousarray(A_log.reshape(64, 1))
    dprep_p = np.ascontiguousarray(np.broadcast_to(D_param[None, :], (P, 64)))
    gamma_p = np.ascontiguousarray(norm_w.reshape(64, P).T)
    bb_p = np.ascontiguousarray(b_base.reshape(32, P).T)
    e8_p = np.zeros((8, P), dtype=ml_dtypes.bfloat16)
    for e_ in range(8):
        e8_p[e_, e_ * 16:(e_ + 1) * 16] = 2.0
    identf_p = np.eye(P, dtype=np.float32)
    identb_p = np.eye(P, dtype=ml_dtypes.bfloat16)

    if "prog" not in _CACHE:
        _CACHE["prog"] = build()
    nc = _CACHE["prog"]

    in_maps = []
    for c in range(8):
        b_i, s_i = c // 4, c % 4
        t0 = s_i * T
        if s_i == 0:
            xs = np.concatenate(
                [np.zeros((3, H), dtype=np.float32), x[b_i, 0:T]], axis=0)
        else:
            xs = x[b_i, t0 - 3:t0 + T]
        xh_p = np.ascontiguousarray(xs.T)
        am = np.zeros((64, 12), dtype=np.float32)
        for slot in range(3):
            q = s_i - 1 - slot
            if q >= 0:
                am[:, slot * 4 + q] = 1.0
        in_maps.append({
            "xh": xh_p, "w_in": w_in_p, "w_out": w_out_p, "w_base": w_base_p,
            "acat": acat_p, "bcat": bcat_p, "wrl": wrl_p, "convw": convw_p,
            "dtbias": dtb_p, "alog": alog_p, "dprep": dprep_p,
            "gammah": gamma_p, "bbase": bb_p, "e8": e8_p,
            "identf": identf_p, "identb": identb_p, "amask": am,
        })

    res = run_bass_kernel_spmd(nc, in_maps, core_ids=list(range(8)))
    outp = np.empty((2, 8192, H), dtype=np.float32)
    for c in range(8):
        b_i, s_i = c // 4, c % 4
        oc = np.asarray(res.results[c]["out"], dtype=np.float32)
        outp[b_i, s_i * T:(s_i + 1) * T] = oc.reshape(H, T).T
    if TAPS:
        kernel.last_results = res
    return outp


# version 07 (final)
# speedup vs baseline: 1.0896x; 1.0581x over previous
from collections import defaultdict
from contextlib import ExitStack

import numpy as np
import ml_dtypes
import concourse.bass as bass
import concourse.mybir as mybir
from concourse.bass_utils import run_bass_kernel_spmd
from concourse.replica_groups import maybe_share_collective_output_space

F32 = mybir.dt.float32
BF16 = mybir.dt.bfloat16
AF = mybir.ActivationFunctionType
ALU = mybir.AluOpType

P = 128
H = 4096
DSSM = 8192
NH = 64
CONV_DIM = 8448
PROJ_DIM = 16704
PROJ_PAD = 16768
NPT = PROJ_PAD // P
KC = H // P
T = 2048
TH = T + 3
NCH = 16
EPS = 1e-5
ARENA_BYTES = 196000

TAPS = False
PHASES = 99


class Sched:

    def __init__(self):
        self.q = defaultdict(list)
        self.no = defaultdict(int)
        self.nd = defaultdict(int)
        self.nde = {}
        self.waited = {}

    def op(self, st, fn):
        self.no[st] += 1
        self.q[st].append(("o", fn))
        return (st, "o", self.no[st])

    def dma(self, st, fn):
        ep = self.nde.get(st, 0)
        if self.nd[(st, ep)] + 16 > 59968:
            ep += 1
            self.nde[st] = ep
        self.nd[(st, ep)] += 16
        self.q[st].append(("d", fn, ep))
        return (st, "d", (ep, self.nd[(st, ep)]))

    def wait(self, st, tok):
        if tok is None:
            return
        tst, kind, cnt = tok
        if kind == "d":
            ep, c = cnt
            key = (st, tst, "d", ep)
            if self.waited.get(key, -1) >= c:
                return
            self.waited[key] = c
            self.q[st].append(("w", (tst, "d", ep, c)))
            return
        key = (st, tst, kind)
        if self.waited.get(key, -1) >= cnt:
            return
        self.waited[key] = cnt
        self.q[st].append(("w", (tst, kind, cnt)))

    def waits(self, st, toks):
        for tk in toks:
            self.wait(st, tk)

    def phase(self):
        self.marks = getattr(self, "marks", [])
        self.marks.append(({st: len(q) for st, q in self.q.items()},
                           dict(self.no), dict(self.nd), dict(self.nde),
                           dict(self.waited)))

    def rollback(self, k):
        lens, no, nd, nde, waited = self.marks[k]
        for st in list(self.q):
            del self.q[st][lens.get(st, 0):]
        self.no = defaultdict(int, no)
        self.nd = defaultdict(int, nd)
        self.nde = dict(nde)
        self.waited = dict(waited)

    def now(self, st, kind="o"):
        if kind == "d":
            ep = self.nde.get(st, 0)
            return (st, "d", (ep, self.nd[(st, ep)]))
        return (st, kind, self.no[st])

    def replay(self, block, engines, osem, dsem):
        gw = set()
        for q in self.q.values():
            for item in q:
                if item[0] == "w" and item[1][0] == "g" and item[1][1] == "d":
                    gw.add((item[1][2], item[1][3]))
        for st, eng_attr in engines.items():
            items = self.q[st]
            if not items:
                continue

            def make(items=items, st=st):
                def run(eng):
                    ndma = defaultdict(int)
                    nop_ = 0
                    for item in items:
                        kind = item[0]
                        if kind == "w":
                            payload = item[1]
                            if payload[1] == "d":
                                tst, _, ep, cnt = payload
                                eng.wait_ge(dsem[(tst, ep)], cnt)
                            else:
                                tst, tkind, cnt = payload
                                eng.wait_ge(osem[tst], cnt)
                        elif kind == "o":
                            if nop_:
                                eng.wait_ge(osem[st], nop_)
                            item[1](eng).then_inc(osem[st], 1)
                            nop_ += 1
                        else:
                            _, payload, ep = item
                            payload(eng).then_inc(dsem[(st, ep)], 16)
                            ndma[ep] += 16
                            if st == "g" and (ep, ndma[ep]) in gw:
                                eng.wait_ge(dsem[(st, ep)], ndma[ep])
                return run

            getattr(block, eng_attr)(make())


def build():
    nc = bass.Bass("TRN2", target_bir_lowering=False)

    xh = nc.dram_tensor("xh", [H, TH], F32, kind="ExternalInput").ap()
    w_in = nc.dram_tensor("w_in", [NPT, P, H], F32, kind="ExternalInput").ap()
    w_out = nc.dram_tensor("w_out", [32, P, DSSM], F32, kind="ExternalInput").ap()
    w_base = nc.dram_tensor("w_base", [32, P, H], F32, kind="ExternalInput").ap()
    acat = nc.dram_tensor("acat", [P, H], F32, kind="ExternalInput").ap()
    bcat = nc.dram_tensor("bcat", [P, H], F32, kind="ExternalInput").ap()
    wrl = nc.dram_tensor("wrl", [P, 256], F32, kind="ExternalInput").ap()
    convw = nc.dram_tensor("convw", [P, 264], F32, kind="ExternalInput").ap()
    dtbias = nc.dram_tensor("dtbias", [64, 1], F32, kind="ExternalInput").ap()
    alog = nc.dram_tensor("alog", [64, 1], F32, kind="ExternalInput").ap()
    dprep = nc.dram_tensor("dprep", [P, 64], F32, kind="ExternalInput").ap()
    gammah = nc.dram_tensor("gammah", [P, 64], F32, kind="ExternalInput").ap()
    bbase = nc.dram_tensor("bbase", [P, 32], F32, kind="ExternalInput").ap()
    e8 = nc.dram_tensor("e8", [8, P], BF16, kind="ExternalInput").ap()
    identf_d = nc.dram_tensor("identf", [P, P], F32, kind="ExternalInput").ap()
    identb_d = nc.dram_tensor("identb", [P, P], BF16, kind="ExternalInput").ap()
    amask = nc.dram_tensor("amask", [64, 12], F32, kind="ExternalInput").ap()

    out = nc.dram_tensor("out", [32, P, T], F32, kind="ExternalOutput").ap()

    projs = nc.dram_tensor("projs", [NPT, P, TH], BF16, kind="Internal").ap()
    convx = nc.dram_tensor("convx", [64, P, T], BF16, kind="Internal").ap()
    xtds = nc.dram_tensor("xtds", [NCH, 64, P, P], BF16, kind="Internal").ap()
    ynorm = nc.dram_tensor("ynorm", [64, P, T], BF16, kind="Internal").ap()
    ffnout = nc.dram_tensor("ffnout", [32, P, T], F32, kind="Internal").ap()
    rg = [[0, 1, 2, 3], [4, 5, 6, 7]]
    ag_space = maybe_share_collective_output_space("AllGather", rg)
    AGB = 64 * P + 64
    ag_in = nc.dram_tensor("ag_in", [AGB, P], F32, kind="Internal").ap()
    ag_out = nc.dram_tensor("ag_out", [4 * AGB, P], F32, kind="Internal",
                            addr_space=ag_space).ap()

    taps = {}
    if TAPS:
        for nm, shp in [("tp_xn", [P, TH]), ("tp_proj", [P, TH]),
                        ("tp_dtsp", [64, T]), ("tp_da", [64, T]),
                        ("tp_cd", [64, T]), ("tp_h0", [P, P]),
                        ("tp_hg0", [P, P]), ("tp_cvx", [P, T]),
                        ("tp_xtd", [P, P]), ("tp_yn", [P, T]),
                        ("tp_ffn", [P, T]), ("tp_rstd", [P, T]),
                        ("tp_hw", [P, T]), ("tp_d", [P, 17]),
                        ("tp_lt", [P, P]), ("tp_mt", [P, P]),
                        ("tp_mi", [P, P]), ("tp_y1", [P, P]),
                        ("tp_g", [P, P]), ("tp_ab", [P, P]),
                        ("tp_ps", [P, P]), ("tp_hf", [P, P]),
                        ("tp_xt2", [P, P])]:
            taps[nm] = nc.dram_tensor(nm, shp, F32, kind="ExternalOutput").ap()

    S = Sched()

    with ExitStack() as ctx:
        e = ctx.enter_context
        arB = e(nc.sbuf_tensor("arB", [P, ARENA_BYTES // 2], BF16))
        trD = e(nc.sbuf_tensor("trD", [P, 4352], BF16, is_transpose_dest=True))
        FV = arB[:, :].bitcast(F32)

        pb = [e(nc.psum_tensor(f"pb{i}", [P, 512], F32)) for i in range(7)]
        pbf = e(nc.psum_tensor("pbf", [P, 1024], BF16))

        semT = e(nc.semaphore("semT"))
        semV = e(nc.semaphore("semV"))
        semA = e(nc.semaphore("semA"))
        semG = e(nc.semaphore("semG"))
        dsem = {}
        for stn in ["s", "a", "g"]:
            for ep in range(3):
                dsem[(stn, ep)] = e(nc.semaphore(f"dma{stn}{ep}"))
        block = e(nc.Block())
        osem = {"t": semT, "v": semV, "a": semA, "g": semG, "s": semT}

        class Ar:
            def __init__(self):
                self.off = 0

            def al(self, nbytes, align=64):
                self.off = (self.off + align - 1) // align * align
                assert self.off + nbytes <= ARENA_BYTES, \
                    f"arena overflow {self.off}+{nbytes}"
                o = self.off
                self.off += nbytes
                return o

            def mark(self):
                return self.off

            def reset(self, m):
                self.off = m

        A_ = Ar()

        def fs(off, rows, cols):
            return FV[0:rows, off // 4: off // 4 + cols]

        def bs(off, rows, cols):
            return arB[0:rows, off // 2: off // 2 + cols]

        o_identF = A_.al(P * 4)
        o_onesF = A_.al(P * 4)
        o_gamma = A_.al(64 * 4)
        o_dprep = A_.al(64 * 4)
        o_bb = A_.al(32 * 4)
        o_dtbias = A_.al(4)
        o_alog = A_.al(4)
        o_negeA = A_.al(4)
        o_amask = A_.al(48)
        o_eps = A_.al(4)
        o_one1 = A_.al(4)
        o_convw = A_.al(264 * 4)
        o_identB = A_.al(P * 2)
        o_e8 = A_.al(P * 2)
        o_onescB = A_.al(2)
        identF = fs(o_identF, P, P)
        onesF = fs(o_onesF, P, P)
        identB = bs(o_identB, P, P)
        E8t = bs(o_e8, 8, P)
        onescB = bs(o_onescB, P, 1)
        epsA = fs(o_eps, P, 1)
        one1 = fs(o_one1, P, 1)
        convW = fs(o_convw, P, 264)
        gammaT = fs(o_gamma, P, 64)
        dprepT = fs(o_dprep, P, 64)
        bbT = fs(o_bb, P, 32)

        ld0 = []
        for dst, src in [(identF, identf_d), (identB, identb_d), (E8t, e8),
                         (gammaT, gammah), (dprepT, dprep), (bbT, bbase),
                         (fs(o_dtbias, 64, 1), dtbias),
                         (fs(o_alog, 64, 1), alog),
                         (fs(o_amask, 64, 12), amask), (convW, convw)]:
            ld0.append(S.dma("s", lambda s, dst=dst, src=src: s.dma_start(
                out=dst, in_=src)))
        S.waits("v", ld0)
        S.op("v", lambda v: v.memset(onesF[:], 1.0))
        S.op("v", lambda v: v.memset(onescB[:], 1.0))
        S.op("v", lambda v: v.memset(epsA[:], EPS))
        tk_one = S.op("v", lambda v: v.memset(one1[:], 1.0))

        o_rstdB = A_.al(T * 4)
        rstdB = fs(o_rstdB, P, T)
        M0 = A_.mark()
        o_dtres = A_.al(TH * 4)
        dtres = fs(o_dtres, 64, TH)
        M1 = A_.mark()

        o_xn = A_.al(KC * TH * 2)
        MLN = A_.mark()
        o_xbf = [A_.al(TH * 2) for _ in range(2)]
        o_xsq = A_.al(TH * 2)
        o_stat = A_.al(80 * 4)
        o_drep = A_.al(P * 2)

        def xns(kc, c0, w):
            return arB[:, o_xn // 2 + kc * TH + c0: o_xn // 2 + kc * TH + c0 + w]

        NG = 17
        gw = [128] * 16 + [3]
        gof = [128 * i for i in range(17)]
        pstats = pb[6]
        o_macc = A_.al(40 * 4)
        o_sacc = A_.al(40 * 4)
        macc = fs(o_macc, P, 17)
        sacc = fs(o_sacc, P, 17)
        S.op("v", lambda v: v.memset(macc, 0.0))
        tk_z = S.op("v", lambda v: v.memset(sacc, 0.0))
        last_ld = [None, None]
        stats_done = None
        prev_acc = tk_z
        for kc in range(KC):
            buf = kc % 2
            xb = bs(o_xbf[buf], P, TH)
            S.wait("g", last_ld[buf])
            tk_ld = S.dma("g", lambda g, kc=kc, xb=xb: g.dma_start(
                out=xb, in_=xh[kc * P:(kc + 1) * P, :]))
            S.waits("a", [tk_ld, stats_done])
            tk_sq = S.op("a", lambda a, xb=xb: a.activation(
                bs(o_xsq, P, TH), xb, AF.Square))
            S.waits("t", [tk_ld, tk_sq, tk_one, prev_acc])
            for g in range(NG):
                S.op("t", lambda t, kc=kc, g=g, xb=xb: t.matmul(
                    pstats[0:gw[g], g:g + 1],
                    xb[:, gof[g]:gof[g] + gw[g]], onescB[:],
                    start=True, stop=True))
                stats_done = S.op("t", lambda t, kc=kc, g=g: t.matmul(
                    pstats[0:gw[g], 32 + g:33 + g],
                    bs(o_xsq, P, TH)[:, gof[g]:gof[g] + gw[g]], onescB[:],
                    start=True, stop=True))
            last_ld[buf] = S.now("t")
            S.wait("v", stats_done)
            S.op("v", lambda v: v.tensor_tensor(
                macc[:, 0:16], macc[:, 0:16], pstats[:, 0:16], ALU.add))
            S.op("v", lambda v: v.tensor_tensor(
                macc[0:3, 16:17], macc[0:3, 16:17], pstats[0:3, 16:17], ALU.add))
            S.op("v", lambda v: v.tensor_tensor(
                sacc[:, 0:16], sacc[:, 0:16], pstats[:, 32:48], ALU.add))
            prev_acc = S.op("v", lambda v: v.tensor_tensor(
                sacc[0:3, 16:17], sacc[0:3, 16:17], pstats[0:3, 48:49], ALU.add))
        mmv = fs(o_stat, P, 17)
        vv = fs(o_stat + 17 * 4, P, 17)
        dd = fs(o_stat + 34 * 4, P, 17)
        md = fs(o_stat + 51 * 4, P, 17)
        S.wait("v", prev_acc)
        tk1 = S.op("v", lambda v: v.tensor_scalar(
            mmv, macc, 1.0 / H, None, op0=ALU.mult))
        S.wait("a", tk1)
        tk2 = S.op("a", lambda a: a.activation(dd, mmv, AF.Square))
        S.wait("v", tk2)
        tk3 = S.op("v", lambda v: v.scalar_tensor_tensor(
            vv, sacc, 1.0 / H, dd, op0=ALU.mult, op1=ALU.subtract))
        S.wait("a", tk3)
        S.op("a", lambda a: a.activation(dd, vv, AF.Ln, bias=epsA[:, 0:1]))
        tk_d = S.op("a", lambda a: a.activation(dd, dd, AF.Exp, scale=-0.5))
        S.wait("v", tk_d)
        tk_md = S.op("v", lambda v: v.tensor_tensor(md, mmv, dd, ALU.mult))
        drep = bs(o_drep, P, P)
        t_db_last = None
        prev_tr = None
        for g in range(NG):
            S.waits("v", [tk_md, prev_tr])
            t1 = S.op("v", lambda v, g=g: v.tensor_scalar(
                drep, onesF[:], dd[:, g:g + 1], None, op0=ALU.mult))
            S.wait("s", t1)
            t_db = S.dma("s", lambda s, g=g: s.dma_start_transpose(
                out=trD[:, g * P:g * P + P], in_=drep))
            S.wait("v", t_db)
            t2 = S.op("v", lambda v, g=g: v.tensor_scalar(
                drep, onesF[:], md[:, g:g + 1], None, op0=ALU.mult))
            S.wait("s", t2)
            t_db_last = prev_tr = S.dma("s", lambda s, g=g: s.dma_start_transpose(
                out=trD[:, 2176 + g * P:2176 + g * P + P], in_=drep))
        o_tmpF = A_.al(P * 4)
        xn_tok = None
        for kc in range(KC):
            buf = kc % 2
            xb = bs(o_xbf[buf], P, TH)
            S.wait("g", last_ld[buf])
            tk_ld = S.dma("g", lambda g, kc=kc, xb=xb: g.dma_start(
                out=xb, in_=xh[kc * P:(kc + 1) * P, :]))
            S.waits("v", [tk_ld, t_db_last])
            for g in range(NG):
                tmp = fs(o_tmpF, P, P)
                S.op("v", lambda v, g=g, xb=xb, tmp=tmp: v.tensor_tensor(
                    tmp[:, 0:gw[g]], xb[:, gof[g]:gof[g] + gw[g]],
                    trD[:, g * P:g * P + gw[g]], ALU.mult))
                xn_tok = S.op("v", lambda v, kc=kc, g=g, tmp=tmp: v.tensor_tensor(
                    xns(kc, gof[g], gw[g]), tmp[:, 0:gw[g]],
                    trD[:, 2176 + g * P:2176 + g * P + gw[g]], ALU.subtract))
            last_ld[buf] = S.now("v")
        if TAPS:
            S.wait("g", xn_tok)
            S.dma("g", lambda g: g.dma_start(
                out=taps["tp_xn"][0:P, 0:TH], in_=xns(0, 0, TH)))
            S.dma("g", lambda g: g.dma_start(
                out=taps["tp_d"], in_=fs(o_stat + 34 * 4, P, 17)))

        S.phase()
        A_.reset(MLN)
        o_wbuf = [A_.al(H * 2) for _ in range(2)]
        o_pst = [A_.al(512 * 2) for _ in range(4)]
        PG = [(0, 411), (411, 411), (822, 411), (1233, 411), (1644, 407)]
        w_ld = [None, None]
        mm_done = [None, None]
        prev_evac = None
        st_tok = [None, None, None, None]
        S.wait("g", xn_tok)
        for wt in range(NPT):
            buf = wt % 2
            wb = bs(o_wbuf[buf], P, H)
            S.wait("g", mm_done[buf])
            w_ld[buf] = S.dma("g", lambda g, wt=wt, wb=wb: g.dma_start(
                out=wb, in_=w_in[wt]))
            S.waits("t", [w_ld[buf], xn_tok, prev_evac])
            for kc in range(KC):
                for gi, (c0, wd) in enumerate(PG):
                    S.op("t", lambda t, kc=kc, gi=gi, c0=c0, wd=wd, wb=wb:
                         t.matmul(pb[gi][:, 0:wd], wb[:, kc * P:(kc + 1) * P],
                                  xns(kc, c0, wd),
                                  start=(kc == 0), stop=(kc == KC - 1)))
            mm_done[buf] = S.now("t")
            S.wait("v", mm_done[buf])
            if wt == NPT - 1:
                for gi, (c0, wd) in enumerate(PG):
                    S.op("v", lambda v, gi=gi, c0=c0, wd=wd: v.tensor_copy(
                        dtres[0:64, c0:c0 + wd], pb[gi][0:64, 0:wd]))
            else:
                for gi, (c0, wd) in enumerate(PG):
                    sb = bs(o_pst[gi % 4], P, 512)
                    S.wait("v", st_tok[gi % 4])
                    S.op("v", lambda v, gi=gi, wd=wd, sb=sb: v.tensor_copy(
                        sb[:, 0:wd], pb[gi][:, 0:wd]))
                    tkv = S.now("v")
                    S.wait("s", tkv)
                    st_tok[gi % 4] = S.dma(
                        "s", lambda s, wt=wt, c0=c0, wd=wd, sb=sb: s.dma_start(
                            out=projs[wt][:, c0:c0 + wd], in_=sb[:, 0:wd]))
            prev_evac = S.now("v")
        proj_v = S.now("v")
        proj_t = S.now("t")
        proj_s = S.now("s", "d")
        if TAPS:
            S.wait("g", proj_s)
            S.dma("g", lambda g: g.dma_start(out=taps["tp_proj"], in_=projs[64]))

        S.phase()
        A_.reset(MLN)
        o_acat = A_.al(H * 2)
        o_bcat = A_.al(H * 2)
        o_wrl = A_.al(256 * 2)
        o_hw = A_.al(T * 2)
        o_wbb = [A_.al(H * 2) for _ in range(2)]
        o_crev = A_.al(P * 2)
        o_combB = A_.al(8 * P * 2)
        o_rt = A_.al(48 * 4)
        o_fstF = [A_.al(512 * 4) for _ in range(2)]

        S.waits("g", [proj_t, proj_v, proj_s])
        tk_ac = S.dma("g", lambda g: g.dma_start(out=bs(o_acat, P, H), in_=acat))
        tk_bc = S.dma("g", lambda g: g.dma_start(out=bs(o_bcat, P, H), in_=bcat))
        tk_wr = S.dma("g", lambda g: g.dma_start(out=bs(o_wrl, P, 256), in_=wrl))
        pe = fs(o_rt, P, 8)
        zap = fs(o_rt + 32, P, 8)
        comb = fs(o_rt + 64, P, 8)
        den = fs(o_rt + 96, P, 1)
        mrep = fs(o_rt + 128, P, 8)
        hw_tok = None
        prev_pe = None
        prev_hw = None
        prev_ct = None
        prev_cr = None
        S.waits("t", ld0)
        for g16 in range(16):
            tc0 = 3 + g16 * P
            S.waits("t", [tk_wr, xn_tok, prev_pe])
            for kc in range(KC):
                S.op("t", lambda t, kc=kc, tc0=tc0: t.matmul(
                    pb[5][:, 0:8], xns(kc, tc0, P),
                    bs(o_wrl, P, 256)[:, kc * 8:kc * 8 + 8],
                    start=(kc == 0), stop=(kc == KC - 1)))
            tk_lg = S.now("t")
            S.wait("a", tk_lg)
            prev_pe = tk_pe = S.op("a", lambda a: a.activation(
                pe, pb[5][:, 0:8], AF.Exp))
            S.wait("v", tk_pe)
            S.op("v", lambda v: v.max(out=mrep, in_=pe))
            S.op("v", lambda v: v.engine_nop())
            tk_den = S.op("v", lambda v: v.tensor_scalar(
                den, mrep[:, 0:1], mrep[:, 1:2], None, op0=ALU.add))
            S.wait("a", tk_den)
            S.op("a", lambda a: a.activation(den, den, AF.Ln))
            tk_rd = S.op("a", lambda a: a.activation(den, den, AF.Exp, scale=-1.0))
            S.op("v", lambda v: v.memset(mrep[:, 2:8], 0.0))
            S.op("v", lambda v: v.tensor_copy(zap, pe))
            S.op("v", lambda v: v.match_replace(out=zap, in_to_replace=mrep,
                                                in_values=zap, imm_value=0.0))
            S.op("v", lambda v: v.engine_nop())
            S.op("v", lambda v: v.tensor_tensor(comb, pe, zap, ALU.subtract))
            S.wait("v", tk_rd)
            tk_cb = S.op("v", lambda v: v.tensor_scalar(
                comb, comb, den[:, 0:1], None, op0=ALU.mult))
            S.waits("t", [tk_cb, prev_ct])
            tk_ct = S.op("t", lambda t: t.transpose(pb[6][0:8, 0:P], comb, identF))
            S.wait("v", tk_ct)
            prev_ct = tk_ctv = S.op("v", lambda v: v.tensor_copy(
                bs(o_combB, 8, P), pb[6][0:8, 0:P]))
            S.waits("t", [tk_ctv, prev_cr])
            tk_cr = S.op("t", lambda t: t.matmul(
                pb[6][:, P:2 * P], E8t, bs(o_combB, 8, P), start=True, stop=True))
            S.wait("v", tk_cr)
            prev_cr = tk_crv = S.op("v", lambda v: v.tensor_copy(
                bs(o_crev, P, P), pb[6][:, P:2 * P]))
            S.waits("t", [tk_ac, prev_hw, tk_pe])
            for kc in range(KC):
                S.op("t", lambda t, kc=kc, tc0=tc0: t.matmul(
                    pb[5][:, P:2 * P], bs(o_acat, P, H)[:, kc * P:(kc + 1) * P],
                    xns(kc, tc0, P), start=(kc == 0), stop=(kc == KC - 1)))
            tk_h = S.now("t")
            S.waits("v", [tk_h, tk_crv])
            prev_hw = hw_tok = S.op("v", lambda v, g16=g16: v.tensor_tensor(
                bs(o_hw, P, T)[:, g16 * P:(g16 + 1) * P],
                pb[5][:, P:2 * P], bs(o_crev, P, P), ALU.mult))
        if TAPS:
            S.wait("g", hw_tok)
            S.dma("g", lambda g: g.dma_start(out=taps["tp_hw"], in_=bs(o_hw, P, T)))
        wb_ld = [None, None]
        wb_done = [None, None]
        prev_fevac = None
        fst_tok = [None, None]
        for ot in range(32):
            buf = ot % 2
            wbb = bs(o_wbb[buf], P, H)
            S.wait("g", wb_done[buf])
            wb_ld[buf] = S.dma("g", lambda g, ot=ot, wbb=wbb: g.dma_start(
                out=wbb, in_=w_base[ot]))
            S.waits("t", [wb_ld[buf], hw_tok, tk_bc, prev_fevac])
            for kc in range(KC):
                for g4 in range(4):
                    S.op("t", lambda t, kc=kc, g4=g4, wbb=wbb: t.matmul(
                        pb[g4][:, 0:512], wbb[:, kc * P:(kc + 1) * P],
                        xns(kc, 3 + g4 * 512, 512), start=(kc == 0), stop=False))
            for g4 in range(4):
                S.op("t", lambda t, ot=ot, g4=g4: t.matmul(
                    pb[g4][:, 0:512], bs(o_bcat, P, H)[:, ot * P:(ot + 1) * P],
                    bs(o_hw, P, T)[:, g4 * 512:(g4 + 1) * 512],
                    start=False, stop=True))
            wb_done[buf] = S.now("t")
            S.wait("v", wb_done[buf])
            for g4 in range(4):
                fb = g4 % 2
                fsb = fs(o_fstF[fb], P, 512)
                S.wait("v", fst_tok[fb])
                S.op("v", lambda v, ot=ot, g4=g4, fsb=fsb: v.tensor_scalar(
                    fsb, pb[g4][:, 0:512], bbT[:, ot:ot + 1], None, op0=ALU.add))
                tkv = S.now("v")
                S.wait("s", tkv)
                fst_tok[fb] = S.dma("s", lambda s, ot=ot, g4=g4, fsb=fsb:
                                    s.dma_start(out=ffnout[ot]
                                                [:, g4 * 512:(g4 + 1) * 512],
                                                in_=fsb))
            prev_fevac = S.now("v")
        ffn_s = S.now("s", "d")
        if TAPS:
            S.wait("g", ffn_s)
            S.dma("g", lambda g: g.dma_start(out=taps["tp_ffn"], in_=ffnout[0]))

        S.phase()
        A_.reset(M1)
        o_dtsp = A_.al(T * 4)
        o_dA = A_.al(T * 4)
        o_cd = A_.al(T * 4)
        o_sm = A_.al(1060 * 4)
        o_tT = A_.al(3 * NCH * 64 * 4)
        o_Hst = A_.al(NH * P * 4)
        o_Bc = A_.al(NCH * P * 2)
        o_BTc = A_.al(NCH * P * 2)
        o_Cc = A_.al(NCH * P * 2)
        o_Hg = A_.al(NH * P * 2)
        MSC = A_.mark()
        dtsp = fs(o_dtsp, 64, T)
        dAt = fs(o_dA, 64, T)
        cdt = fs(o_cd, 64, T)
        clc = fs(o_sm, 64, P)
        wsufc = fs(o_sm + 128 * 4, 64, P)
        zeros64 = fs(o_sm + 256 * 4, 64, P)
        Pg = fs(o_sm + 384 * 4, 64, 16)
        Pincl = fs(o_sm + 400 * 4, 64, 16)
        Pex = fs(o_sm + 416 * 4, 64, 16)
        o_agc = o_sm + 432 * 4
        negeA = fs(o_negeA, 64, 1)

        def Hs(h):
            return FV[:, o_Hst // 4 + h * P: o_Hst // 4 + (h + 1) * P]

        def Hgs(h):
            return arB[:, o_Hg // 2 + h * P: o_Hg // 2 + (h + 1) * P]

        def dtTs(c):
            return fs(o_tT + c * 64 * 4, P, 64)

        def wsufTs(c):
            return fs(o_tT + (NCH + c) * 64 * 4, P, 64)

        def DrepTs(c):
            return fs(o_tT + (2 * NCH + c) * 64 * 4, P, 64)

        S.wait("v", proj_v)
        S.op("v", lambda v: v.memset(zeros64[:], 0.0))
        tk0 = S.op("v", lambda v: v.tensor_scalar(
            dtsp, dtres[0:64, 3:3 + T], fs(o_dtbias, 64, 1)[:, 0:1],
            None, op0=ALU.add))
        S.wait("a", tk0)
        S.op("a", lambda a: a.activation(dAt, dtsp, AF.Abs))
        S.op("a", lambda a: a.activation(dAt, dAt, AF.Exp, scale=-1.0))
        S.op("a", lambda a: a.activation(dAt, dAt, AF.Ln, bias=one1[0:64, 0:1]))
        tk_r = S.op("a", lambda a: a.activation(cdt, dtsp, AF.Relu))
        S.wait("v", tk_r)
        S.op("v", lambda v: v.tensor_tensor(dtsp, cdt, dAt, ALU.add))
        tk_sp = S.now("v")
        S.wait("a", tk_sp)
        tk_ea = S.op("a", lambda a: a.activation(negeA, fs(o_alog, 64, 1), AF.Exp))
        S.wait("v", tk_ea)
        S.op("v", lambda v: v.tensor_scalar(negeA, negeA, -1.0, None, op0=ALU.mult))
        tk_la = S.op("v", lambda v: v.tensor_scalar(
            dAt, dtsp, negeA[:, 0:1], None, op0=ALU.mult))
        tk_lacp = S.op("v", lambda v: v.tensor_copy(cdt, dAt))
        S.wait("a", tk_lacp)
        tk_da = S.op("a", lambda a: a.activation(dAt, dAt, AF.Exp))
        S.wait("v", tk_da)
        tr_ev = None
        for c in range(NCH):
            ch = slice(c * P, (c + 1) * P)
            S.op("v", lambda v, ch=ch: v.tensor_tensor_scan(
                clc, onesF[0:64, :], cdt[:, ch], 0.0, op0=ALU.mult, op1=ALU.add))
            S.op("v", lambda v: v.engine_nop())
            tk_cl = S.now("v")
            S.wait("a", tk_cl)
            tk_ws = S.op("a", lambda a: a.activation(
                wsufc, clc, AF.Exp, bias=clc[:, 127:128], scale=-1.0))
            S.op("v", lambda v, ch=ch: v.tensor_tensor_scan(
                cdt[:, ch], dAt[:, ch], zeros64[:], 1.0, op0=ALU.mult, op1=ALU.add))
            S.op("v", lambda v: v.engine_nop())
            S.op("v", lambda v, c=c: v.tensor_copy(
                Pg[:, c:c + 1], cdt[:, c * P + 127:c * P + 128]))
            dmt = fs(o_sm + 512 * 4, 64, P)
            S.waits("v", [tk_ws])
            tk_dm = S.op("v", lambda v, c=c, dmt=dmt: v.tensor_scalar(
                dmt, onesF[0:64, :], cdt[:, c * P + 127:c * P + 128], None,
                op0=ALU.mult))
            S.waits("t", [tk_ws, tk_dm, tr_ev])
            S.op("t", lambda t, ch=ch: t.transpose(
                pb[6][:, 0:64], dtsp[:, ch], identF[0:64, 0:64]))
            S.op("t", lambda t: t.transpose(
                pb[6][:, 64:128], wsufc, identF[0:64, 0:64]))
            tk_tr = S.op("t", lambda t, dmt=dmt: t.transpose(
                pb[6][:, 128:192], dmt, identF[0:64, 0:64]))
            S.wait("v", tk_tr)
            S.op("v", lambda v, c=c: v.tensor_copy(dtTs(c), pb[6][:, 0:64]))
            S.op("v", lambda v, c=c: v.tensor_copy(wsufTs(c), pb[6][:, 64:128]))
            S.op("v", lambda v, c=c: v.tensor_copy(DrepTs(c), pb[6][:, 128:192]))
            tr_ev = S.now("v")
            S.wait("a", tk_tr)
        S.op("v", lambda v: v.tensor_tensor_scan(
            Pincl, Pg, zeros64[:, 0:16], 1.0, op0=ALU.mult, op1=ALU.add))
        S.op("v", lambda v: v.engine_nop())
        S.op("v", lambda v: v.memset(Pex[:, 0:1], 1.0))
        tk_pex = S.op("v", lambda v: v.tensor_copy(Pex[:, 1:16], Pincl[:, 0:15]))
        if TAPS:
            S.wait("g", tk_pex)
            S.dma("g", lambda g: g.dma_start(out=taps["tp_dtsp"], in_=dtsp))
            S.dma("g", lambda g: g.dma_start(out=taps["tp_da"], in_=dAt))
            S.dma("g", lambda g: g.dma_start(out=taps["tp_cd"], in_=cdt))

        o_xbcst = A_.al(66 * 131 * 2)
        o_cvx = A_.al(64 * P * 2)
        o_cacc = A_.al(3 * P * 4)
        o_xtdb = A_.al(4 * P * 2)

        def cvxs(ti):
            return arB[:, o_cvx // 2 + ti * P: o_cvx // 2 + (ti + 1) * P]

        tk_h0 = S.op("v", lambda v: v.memset(FV[:, o_Hst // 4:o_Hst // 4 + NH * P], 0.0))
        gate_v = tk_h0
        gate_t = None
        gate_a = None
        xtd_cons = defaultdict(lambda: None)
        xps_cons = defaultdict(lambda: None)
        sps_cons = defaultdict(lambda: None)
        for c in range(NCH):
            S.waits("g", [gate_v, gate_t, gate_a, proj_s])
            rd = []
            for ti in range(66):
                rd.append(S.dma("g", lambda g, c=c, ti=ti: g.dma_start(
                    out=bs(o_xbcst + ti * 131 * 2, P, 131),
                    in_=projs[64 + ti][:, c * P:c * P + 131])))
            rd_done = rd[-1]
            cvt = []
            prev_a1 = None
            for ti in range(66):
                acc = fs(o_cacc + (ti % 2) * P * 4, P, P)
                gt = fs(o_cacc + 2 * P * 4, P, P)
                S.waits("v", [rd_done, prev_a1])
                S.op("v", lambda v, c=c, ti=ti, acc=acc: v.tensor_scalar(
                    acc, bs(o_xbcst + ti * 131 * 2, P, 131)[:, 0:P],
                    convW[:, ti * 4:ti * 4 + 1], None, op0=ALU.mult))
                for k in range(1, 4):
                    S.op("v", lambda v, ti=ti, k=k, acc=acc: v.scalar_tensor_tensor(
                        acc, bs(o_xbcst + ti * 131 * 2, P, 131)[:, k:k + P],
                        convW[:, ti * 4 + k:ti * 4 + k + 1], acc,
                        op0=ALU.mult, op1=ALU.add))
                tk_acc = S.now("v")
                S.wait("a", tk_acc)
                prev_a1 = S.op("a", lambda a, acc=acc, gt=gt: a.activation(
                    gt, acc, AF.Exp, scale=-1.0))
                S.op("a", lambda a, gt=gt: a.activation(
                    gt, gt, AF.Ln, bias=one1[:, 0:1]))
                tk_g3 = S.op("a", lambda a, gt=gt: a.activation(
                    gt, gt, AF.Exp, scale=-1.0))
                if ti < 64:
                    dst = cvxs(ti)
                elif ti == 64:
                    dst = bs(o_Bc + c * P * 2, P, P)
                else:
                    dst = bs(o_Cc + c * P * 2, P, P)
                S.wait("v", tk_g3)
                tk_cv = S.op("v", lambda v, dst=dst, acc=acc, gt=gt: v.tensor_tensor(
                    dst, acc, gt, ALU.mult))
                cvt.append(tk_cv)
                if ti < 64:
                    S.wait("s", tk_cv)
                    S.dma("s", lambda s, c=c, ti=ti: s.dma_start(
                        out=convx[ti][:, c * P:(c + 1) * P], in_=cvxs(ti)))
            S.waits("t", [cvt[64], xps_cons[("bt",)]])
            tk_btp = S.op("t", lambda t, c=c: t.transpose(
                pbf[:, 512:640], bs(o_Bc + c * P * 2, P, P), identB))
            S.wait("v", tk_btp)
            tk_bt = S.op("v", lambda v, c=c: v.tensor_copy(
                bs(o_BTc + c * P * 2, P, P), pbf[:, 512:640]))
            xps_cons[("bt",)] = tk_bt
            for h in range(NH):
                sl = h % 4
                xps = pbf[:, sl * P:(sl + 1) * P]
                S.waits("t", [cvt[h], xps_cons[sl]])
                tk_xt = S.op("t", lambda t, h=h, xps=xps: t.transpose(
                    xps, cvxs(h), identB))
                xtd = bs(o_xtdb + sl * P * 2, P, P)
                S.waits("v", [tk_xt, xtd_cons[sl]])
                tk_xtd = S.op("v", lambda v, h=h, c=c, xtd=xtd, xps=xps:
                              v.tensor_scalar(xtd, xps, dtTs(c)[:, h:h + 1],
                                              None, op0=ALU.mult))
                xps_cons[sl] = tk_xtd
                S.wait("s", tk_xtd)
                tk_wxtd = S.dma("s", lambda s, c=c, h=h, xtd=xtd: s.dma_start(
                    out=xtds[c][h], in_=xtd))
                btw = bs(o_xbcst + (h % 2) * 131 * 2, P, P)
                S.waits("v", [tk_bt, sps_cons[("btw", h % 2)]])
                tk_btw = S.op("v", lambda v, c=c, h=h, btw=btw: v.tensor_scalar(
                    btw, bs(o_BTc + c * P * 2, P, P), wsufTs(c)[:, h:h + 1],
                    None, op0=ALU.mult))
                ssl = pb[h % 2][:, 0:P]
                S.waits("t", [tk_btw, tk_xtd, sps_cons[h % 2]])
                tk_s = S.op("t", lambda t, btw=btw, xtd=xtd, ssl=ssl: t.matmul(
                    ssl, btw, xtd, start=True, stop=True))
                sps_cons[("btw", h % 2)] = tk_s
                xtd_cons[sl] = (tk_s if True else None)
                S.waits("v", [tk_s])
                tk_hu = S.op("v", lambda v, h=h, c=c, ssl=ssl:
                             v.scalar_tensor_tensor(
                                 Hs(h), Hs(h), DrepTs(c)[:, h:h + 1], ssl,
                                 op0=ALU.mult, op1=ALU.add))
                sps_cons[h % 2] = tk_hu
                xtd_cons[sl] = tk_wxtd
                S.wait("v", tk_wxtd)
            gate_v = S.now("v")
            gate_t = S.now("t")
            gate_a = S.now("a")
        pass1_v = S.now("v")
        pass1_s = S.now("s", "d")
        if TAPS:
            S.waits("g", [pass1_v, pass1_s])
            S.dma("g", lambda g: g.dma_start(out=taps["tp_h0"], in_=Hs(0)))
            S.dma("g", lambda g: g.dma_start(out=taps["tp_cvx"], in_=convx[0]))
            S.dma("g", lambda g: g.dma_start(out=taps["tp_xtd"], in_=xtds[0][0]))

        S.phase()
        S.wait("g", pass1_v)
        for h in range(NH):
            S.dma("g", lambda g, h=h: g.dma_start(
                out=ag_in[h * P:(h + 1) * P, :], in_=Hs(h)))
        dmt0 = fs(o_sm + 512 * 4, 64, P)
        S.wait("v", tk_pex)
        tk_dbc = S.op("v", lambda v, dmt0=dmt0: v.tensor_scalar(
            dmt0, onesF[0:64, :], Pincl[:, 15:16], None, op0=ALU.mult))
        S.wait("g", tk_dbc)
        tk_dp = S.dma("g", lambda g, dmt0=dmt0: g.dma_start(
            out=ag_in[64 * P:64 * P + 64, :], in_=dmt0))
        S.wait("g", tk_dp)
        tk_cc = S.op("g", lambda g: g.collective_compute(
            "AllGather", ALU.bypass, replica_groups=rg,
            ins=[ag_in], outs=[ag_out]))
        S.wait("g", tk_cc)
        o_dgb = o_sm + 920 * 4
        Dg = [fs(o_agc + q * 4, 64, 1) for q in range(4)]
        d1 = fs(o_agc + 16, 64, 1)
        d2 = fs(o_agc + 20, 64, 1)
        d12 = fs(o_agc + 24, 64, 1)
        on1 = fs(o_agc + 28, 64, 1)
        cq = [fs(o_agc + 32 + q * 4, 64, 1) for q in range(4)]
        o_crepT = o_sm + 640 * 4
        dg_toks = []
        for q in range(4):
            tkb = S.dma("g", lambda g, q=q: g.dma_start(
                out=fs(o_dgb, 64, P),
                in_=ag_out[q * AGB + 64 * P:q * AGB + 64 * P + 64, :]))
            S.wait("v", tkb)
            dg_toks.append(S.op("v", lambda v, q=q: v.tensor_copy(
                Dg[q], fs(o_dgb, 64, P)[:, 0:1])))
        S.phase()
        am = fs(o_amask, 64, 12)
        S.waits("v", dg_toks)
        S.op("v", lambda v: v.memset(on1, 1.0))
        S.op("v", lambda v: v.tensor_scalar(d1, Dg[0], am[:, 0:1], None,
                                            op0=ALU.mult))
        for q in range(1, 4):
            S.op("v", lambda v, q=q: v.scalar_tensor_tensor(
                d1, Dg[q], am[:, q:q + 1], d1, op0=ALU.mult, op1=ALU.add))
        S.op("v", lambda v: v.tensor_scalar(d2, Dg[0], am[:, 4:5], None,
                                            op0=ALU.mult))
        for q in range(1, 4):
            S.op("v", lambda v, q=q: v.scalar_tensor_tensor(
                d2, Dg[q], am[:, 4 + q:5 + q], d2, op0=ALU.mult, op1=ALU.add))
        S.op("v", lambda v: v.tensor_tensor(d12, d1, d2, ALU.mult))
        prev_ctr = None
        for q in range(4):
            S.op("v", lambda v, q=q: v.tensor_scalar(
                cq[q], on1, am[:, q:q + 1], None, op0=ALU.mult))
            S.op("v", lambda v, q=q: v.scalar_tensor_tensor(
                cq[q], d1, am[:, 4 + q:5 + q], cq[q], op0=ALU.mult, op1=ALU.add))
            S.op("v", lambda v, q=q: v.scalar_tensor_tensor(
                cq[q], d12, am[:, 8 + q:9 + q], cq[q], op0=ALU.mult, op1=ALU.add))
            dmt = fs(o_sm + 512 * 4, 64, P)
            tk_cm = S.op("v", lambda v, q=q, dmt=dmt: v.tensor_scalar(
                dmt, onesF[0:64, :], cq[q][:, 0:1], None, op0=ALU.mult))
            S.waits("t", [tk_cm, prev_ctr])
            tk_ctr = S.op("t", lambda t, q=q, dmt=dmt: t.transpose(
                pb[6][:, 192 + q * 64:256 + q * 64], dmt, identF[0:64, 0:64]))
            S.wait("v", tk_ctr)
            prev_ctr = S.op("v", lambda v, q=q: v.tensor_copy(
                fs(o_crepT + q * 64 * 4, P, 64), pb[6][:, 192 + q * 64:256 + q * 64]))
        S.phase()
        o_tq = o_xbcst
        tq_cons = defaultdict(lambda: None)
        hg_done = None
        for h in range(NH):
            acc_t = None
            for q in range(4):
                slot = (h * 4 + q) % 8
                tq = bs(o_tq + slot * P * 2, P, P)
                S.wait("g", tq_cons[slot])
                tk_tq = S.dma("g", lambda g, q=q, h=h, tq=tq: g.dma_start(
                    out=tq, in_=ag_out[q * AGB + h * P:q * AGB + (h + 1) * P, :]))
                S.waits("v", [tk_tq, prev_ctr])
                if q == 0:
                    acc_t = S.op("v", lambda v, h=h, q=q, tq=tq: v.tensor_scalar(
                        Hgs(h), tq, fs(o_crepT + q * 64 * 4, P, 64)[:, h:h + 1],
                        None, op0=ALU.mult))
                else:
                    acc_t = S.op("v", lambda v, h=h, q=q, tq=tq:
                                 v.scalar_tensor_tensor(
                                     Hgs(h), tq,
                                     fs(o_crepT + q * 64 * 4, P, 64)[:, h:h + 1],
                                     Hgs(h), op0=ALU.mult, op1=ALU.add))
                tq_cons[slot] = acc_t
            hg_done = acc_t
            if h in (0, 7, 15, 31, 47):
                S.phase()
        if TAPS:
            S.wait("g", hg_done)
            S.dma("g", lambda g: g.dma_start(out=taps["tp_hg0"], in_=Hgs(0)))

        S.phase()
        A_.reset(MSC)
        o_zst = A_.al(2 * P * 2)
        o_cvx2 = A_.al(4 * P * 2)
        o_xtd2 = A_.al(4 * P * 2)
        o_ar = A_.al(32 * P * 2)
        o_cdr = A_.al(32 * P * 2)
        o_mt = A_.al(2 * P * 2)
        o_mi = A_.al(2 * P * 2)
        o_hfb = A_.al(2 * P * 2)
        o_ynst = A_.al(2 * P * 2)
        o_sqb = A_.al(2 * P * 2)
        o_Lt = A_.al(2 * P * 4)
        o_y1 = A_.al(2 * P * 4)
        o_gat = A_.al(2 * P * 4)
        o_act2 = A_.al(3 * 64 * 4)
        o_rrep = A_.al(P * 4)
        o_rsm = A_.al(8 * 4)
        o_tapst = A_.al(P * 4)

        tk_h02 = S.op("v", lambda v: v.memset(
            FV[:, o_Hst // 4:o_Hst // 4 + NH * P], 0.0))
        S.wait("v", hg_done)
        gate2 = {"v": tk_h02, "t": None, "a": None}
        cons = defaultdict(lambda: None)
        tr2_ev = None
        rms_rd = None
        for c in range(NCH):
            S.waits("g", [gate2["v"], gate2["t"], gate2["a"], pass1_s])
            aT = fs(o_act2, P, 64)
            cdT = fs(o_act2 + 64 * 4, P, 64)
            prT = fs(o_act2 + 128 * 4, P, 64)
            S.waits("t", [tk_da, tr2_ev, cons[("rtp",)]])
            S.op("t", lambda t, c=c: t.transpose(
                pb[6][:, 0:64], dAt[:, c * P:(c + 1) * P], identF[0:64, 0:64]))
            tk_t2 = S.op("t", lambda t, c=c: t.transpose(
                pb[6][:, 64:128], cdt[:, c * P:(c + 1) * P], identF[0:64, 0:64]))
            S.wait("v", tk_t2)
            S.op("v", lambda v, aT=aT: v.tensor_copy(aT, pb[6][:, 0:64]))
            tk_cdT = S.op("v", lambda v, cdT=cdT: v.tensor_copy(cdT, pb[6][:, 64:128]))
            dmt = fs(o_sm + 512 * 4, 64, P)
            S.wait("v", tk_pex)
            tk_pm = S.op("v", lambda v, c=c, dmt=dmt: v.tensor_scalar(
                dmt, onesF[0:64, :], Pex[:, c:c + 1], None, op0=ALU.mult))
            S.wait("t", tk_pm)
            tk_ptr = S.op("t", lambda t, dmt=dmt: t.transpose(
                pb[6][:, 128:192], dmt, identF[0:64, 0:64]))
            S.wait("v", tk_ptr)
            tk_prT = S.op("v", lambda v, prT=prT: v.tensor_copy(
                prT, pb[6][:, 128:192]))
            tr2_ev = S.now("v")
            S.waits("t", [pass1_v, cons[("G",)], rms_rd])
            tk_G = S.op("t", lambda t, c=c: t.matmul(
                pb[5][:, 0:P], bs(o_Bc + c * P * 2, P, P),
                bs(o_Cc + c * P * 2, P, P), start=True, stop=True))
            rms_tok = None
            for half in range(2):
                hh0 = half * 32
                ab_dma, cdb_dma, cdr_rdy = [], {}, []
                for i in range(32):
                    h = hh0 + i
                    S.waits("v", [tr2_ev, cons[("ar", i)]])
                    tk_a1 = S.op("v", lambda v, i=i, h=h, aT=aT: v.tensor_scalar(
                        bs(o_ar + i * P * 2, P, P), onesF[:], aT[:, h:h + 1],
                        None, op0=ALU.mult))
                    S.waits("v", [cons[("cdr", i)]])
                    tk_c1 = S.op("v", lambda v, i=i, h=h, cdT=cdT: v.tensor_scalar(
                        bs(o_cdr + i * P * 2, P, P), onesF[:], cdT[:, h:h + 1],
                        None, op0=ALU.mult))
                    cdr_rdy.append(tk_c1)
                    S.waits("s", [tk_a1, cons[("ab", i)]])
                    ab_dma.append(S.dma("s", lambda s, i=i: s.dma_start_transpose(
                        out=trD[:, i * P:(i + 1) * P],
                        in_=bs(o_ar + i * P * 2, P, P))))

                def issue_cdb(j):
                    if j >= 32:
                        return
                    S.waits("s", [cdr_rdy[j], cons[("cdb", j % 2)]])
                    cdb_dma[j] = S.dma("s", lambda s, j=j: s.dma_start_transpose(
                        out=trD[:, (32 + j % 2) * P:(33 + j % 2) * P],
                        in_=bs(o_cdr + j * P * 2, P, P)))

                issue_cdb(0)
                issue_cdb(1)
                for i in range(32):
                    h = hh0 + i
                    sl = i % 2
                    Lt = fs(o_Lt + sl * P * 4, P, P)
                    S.waits("v", [ab_dma[i], cons[("Lt", sl)]])
                    tk_sc = S.op("v", lambda v, i=i, Lt=Lt: v.tensor_tensor_scan(
                        Lt, trD[:, i * P:(i + 1) * P], identF, 0.0,
                        op0=ALU.mult, op1=ALU.add))
                    S.op("v", lambda v: v.engine_nop())
                    cons[("ab", i)] = tk_sc
                    mt = bs(o_mt + sl * P * 2, P, P)
                    S.waits("v", [tk_G, cons[("mt", sl)]])
                    tk_mt = S.op("v", lambda v, Lt=Lt, mt=mt: v.tensor_tensor(
                        mt, Lt, pb[5][:, 0:P], ALU.mult))
                    cons[("Lt", sl)] = tk_mt
                    mi = bs(o_mi + sl * P * 2, P, P)
                    S.waits("v", [cdb_dma[i], cons[("mi", sl)]])
                    tk_mi = S.op("v", lambda v, c=c, i=i, mi=mi: v.tensor_tensor(
                        mi, bs(o_Cc + c * P * 2, P, P),
                        trD[:, (32 + i % 2) * P:(33 + i % 2) * P], ALU.mult))
                    cons[("cdb", i % 2)] = tk_mi
                    cons[("cdr", i)] = cdb_dma[i]
                    issue_cdb(i + 2)
                    hfb = bs(o_hfb + sl * P * 2, P, P)
                    S.waits("v", [tk_prT, cons[("hf", sl)]])
                    tk_hf = S.op("v", lambda v, h=h, hfb=hfb, prT=prT:
                                 v.scalar_tensor_tensor(
                                     hfb, Hgs(h), prT[:, h:h + 1], Hs(h),
                                     op0=ALU.mult, op1=ALU.add))
                    btw = bs(o_ar + i * P * 2, P, P)
                    S.waits("v", [ab_dma[i], cons[("btwu", sl)]])
                    tk_btw = S.op("v", lambda v, c=c, h=h, btw=btw: v.tensor_scalar(
                        btw, bs(o_BTc + c * P * 2, P, P), wsufTs(c)[:, h:h + 1],
                        None, op0=ALU.mult))
                    cons[("ar", i)] = tk_btw
                    zt = bs(o_zst + sl * P * 2, P, P)
                    S.waits("g", [cons[("zt", sl)], proj_s])
                    rd_z = S.dma("g", lambda g, c=c, h=h, zt=zt: g.dma_start(
                        out=zt, in_=projs[h][:, c * P + 3:c * P + 3 + P]))
                    cvsl = i % 4
                    cvr = bs(o_cvx2 + cvsl * P * 2, P, P)
                    S.wait("g", cons[("cvr", cvsl)])
                    rd_c = S.dma("g", lambda g, c=c, h=h, cvr=cvr: g.dma_start(
                        out=cvr, in_=convx[h][:, c * P:(c + 1) * P]))
                    xsl = i % 4
                    xt2 = bs(o_xtd2 + xsl * P * 2, P, P)
                    S.wait("g", cons[("xt2", xsl)])
                    rd_x = S.dma("g", lambda g, c=c, h=h, xt2=xt2: g.dma_start(
                        out=xt2, in_=xtds[c][h]))
                    ysl = pb[2 + sl][:, 0:P]
                    S.waits("t", [tk_mt, rd_x, cons[("y", sl)]])
                    S.op("t", lambda t, mt=mt, xt2=xt2, ysl=ysl: t.matmul(
                        ysl, xt2, mt, start=True, stop=False))
                    S.waits("t", [tk_hf, tk_mi])
                    tk_y = S.op("t", lambda t, hfb=hfb, mi=mi, ysl=ysl: t.matmul(
                        ysl, hfb, mi, start=False, stop=True))
                    cons[("mt", sl)] = tk_y
                    cons[("mi", sl)] = tk_y
                    cons[("hf", sl)] = tk_y
                    ssl = pb[1 if sl == 0 else 4][:, 0:P]
                    S.waits("t", [tk_btw, cons[("S", sl)]])
                    tk_s = S.op("t", lambda t, btw=btw, xt2=xt2, ssl=ssl: t.matmul(
                        ssl, btw, xt2, start=True, stop=True))
                    cons[("btwu", sl)] = tk_s
                    S.wait("v", tk_s)
                    tk_hu = S.op("v", lambda v, h=h, c=c, ssl=ssl:
                                 v.scalar_tensor_tensor(
                                     Hs(h), Hs(h), DrepTs(c)[:, h:h + 1], ssl,
                                     op0=ALU.mult, op1=ALU.add))
                    cons[("S", sl)] = tk_hu
                    cons[("xt2", xsl)] = tk_s
                    y1 = fs(o_y1 + sl * P * 4, P, P)
                    S.waits("v", [tk_y, rd_c, cons[("y1", sl)]])
                    tk_ye = S.op("v", lambda v, y1=y1, cvr=cvr, h=h, ysl=ysl:
                                 v.scalar_tensor_tensor(
                                     y1, cvr, dprepT[:, h:h + 1], ysl,
                                     op0=ALU.mult, op1=ALU.add))
                    cons[("y", sl)] = tk_ye
                    cons[("cvr", cvsl)] = tk_ye
                    if TAPS and c == 0 and h == 0:
                        S.wait("g", tk_ye)
                        t_y1tap = S.dma("g", lambda g, y1=y1: g.dma_start(
                            out=taps["tp_y1"], in_=y1))
                        S.wait("v", tk_y)
                        tk_tps = S.op("v", lambda v, ysl=ysl: v.tensor_copy(
                            fs(o_tapst, P, P), ysl))
                        S.waits("g", [tk_tps, tk_hf])
                        S.dma("g", lambda g: g.dma_start(
                            out=taps["tp_ps"], in_=fs(o_tapst, P, P)))
                        S.dma("g", lambda g, hfb=hfb: g.dma_start(
                            out=taps["tp_hf"], in_=hfb))
                        S.dma("g", lambda g, xt2=xt2: g.dma_start(
                            out=taps["tp_xt2"], in_=xt2))
                        S.wait("v", t_y1tap)
                    gat = fs(o_gat + sl * P * 4, P, P)
                    S.waits("a", [rd_z, cons[("gat", sl)]])
                    S.op("a", lambda a, zt=zt, gat=gat: a.activation(
                        gat, zt, AF.Exp, scale=-1.0))
                    S.op("a", lambda a, gat=gat: a.activation(
                        gat, gat, AF.Ln, bias=one1[:, 0:1]))
                    tk_g3 = S.op("a", lambda a, gat=gat: a.activation(
                        gat, gat, AF.Exp, scale=-1.0))
                    S.wait("v", tk_g3)
                    S.op("v", lambda v, y1=y1, zt=zt: v.tensor_tensor(
                        y1, y1, zt, ALU.mult))
                    tk_yg = S.op("v", lambda v, y1=y1, gat=gat: v.tensor_tensor(
                        y1, y1, gat, ALU.mult))
                    cons[("gat", sl)] = tk_yg
                    cons[("zt", sl)] = tk_yg
                    if TAPS and c == 0 and h == 0:
                        S.waits("g", [tk_sc, tk_mt, tk_mi, tk_ye, tk_g3, tk_yg])
                        S.dma("g", lambda g, Lt=Lt: g.dma_start(
                            out=taps["tp_lt"], in_=Lt))
                        S.dma("g", lambda g, mt=mt: g.dma_start(
                            out=taps["tp_mt"], in_=mt))
                        S.dma("g", lambda g, mi=mi: g.dma_start(
                            out=taps["tp_mi"], in_=mi))
                        S.dma("g", lambda g, gat=gat: g.dma_start(
                            out=taps["tp_g"], in_=gat))
                        S.dma("g", lambda g, i=i: g.dma_start(
                            out=taps["tp_ab"], in_=trD[:, i * P:(i + 1) * P]))
                    sq = bs(o_sqb + sl * P * 2, P, P)
                    S.waits("a", [tk_yg, cons[("sq", sl)]])
                    tk_sq2 = S.op("a", lambda a, y1=y1, sq=sq: a.activation(
                        sq, y1, AF.Square))
                    S.waits("t", [tk_sq2, cons[("rms",)] if h == 0 else None])
                    rms_tok = S.op("t", lambda t, h=h, sq=sq: t.matmul(
                        pb[0][0:P, 0:1], sq, onescB[:],
                        start=(h == 0), stop=(h == NH - 1)))
                    cons[("sq", sl)] = rms_tok
                    yst = bs(o_ynst + sl * P * 2, P, P)
                    S.waits("v", [cons[("yst", sl)]])
                    tk_yn = S.op("v", lambda v, h=h, y1=y1, yst=yst: v.tensor_scalar(
                        yst, y1, gammaT[:, h:h + 1], None, op0=ALU.mult))
                    cons[("y1", sl)] = tk_yn
                    S.wait("s", tk_yn)
                    cons[("yst", sl)] = S.dma(
                        "s", lambda s, c=c, h=h, yst=yst: s.dma_start(
                            out=ynorm[h][:, c * P:(c + 1) * P], in_=yst))
            rsm = fs(o_rsm, P, 1)
            S.waits("a", [rms_tok, cons[("rsmu",)]])
            S.op("a", lambda a, rsm=rsm: a.activation(
                rsm, pb[0][0:P, 0:1], AF.Ln, scale=1.0 / DSSM,
                bias=epsA[:, 0:1]))
            tk_rs = S.op("a", lambda a, rsm=rsm: a.activation(
                rsm, rsm, AF.Exp, scale=-0.5))
            rms_rd = tk_rs
            cons[("G",)] = S.now("v")
            S.waits("v", [tk_rs, cons[("rrep",)]])
            rr = fs(o_rrep, P, P)
            tk_rr = S.op("v", lambda v, rsm=rsm, rr=rr: v.tensor_scalar(
                rr, onesF[:], rsm[:, 0:1], None, op0=ALU.mult))
            cons[("rsmu",)] = tk_rr
            S.waits("t", [tk_rr, cons[("rtp",)]])
            tk_rt2 = S.op("t", lambda t, rr=rr: t.transpose(
                pb[6][:, 320:448], rr, identF))
            S.wait("v", tk_rt2)
            tk_rb = S.op("v", lambda v, c=c: v.tensor_copy(
                rstdB[:, c * P:(c + 1) * P], pb[6][:, 320:448]))
            cons[("rtp",)] = tk_rb
            cons[("rrep",)] = tk_rt2
            gate2 = {"v": S.now("v"), "t": S.now("t"), "a": S.now("a")}
            if c in (0, 1, 3, 7):
                S.phase()
        pass2_v = S.now("v")
        pass2_s = S.now("s", "d")
        if TAPS:
            S.waits("g", [pass2_v, pass2_s])
            S.dma("g", lambda g: g.dma_start(out=taps["tp_yn"], in_=ynorm[0]))
            S.dma("g", lambda g: g.dma_start(out=taps["tp_rstd"], in_=rstdB))

        S.phase()
        A_.reset(M0)
        o_ynh = A_.al(64 * 1024 * 2)
        o_wob = A_.al(DSSM * 2)
        o_xst = A_.al(512 * 4)
        o_fst2 = A_.al(512 * 4)
        o_ost = [A_.al(512 * 4) for _ in range(2)]

        def ynh(k, c0, w):
            return arB[:, o_ynh // 2 + k * 1024 + c0: o_ynh // 2 + k * 1024 + c0 + w]

        wo_cons = None
        ost_cons = [None, None]
        xst_cons = None
        for half in range(2):
            hc0 = half * 1024
            ynh_toks = []
            S.waits("g", [S.now("t"), pass2_s])
            for k in range(NH):
                ynh_toks.append(S.dma("g", lambda g, k=k, hc0=hc0: g.dma_start(
                    out=ynh(k, 0, 1024), in_=ynorm[k][:, hc0:hc0 + 1024])))
            ynh_rd = ynh_toks[-1]
            for ot in range(32):
                wob = bs(o_wob, P, DSSM)
                S.waits("g", [wo_cons])
                tk_wo = S.dma("g", lambda g, ot=ot, wob=wob: g.dma_start(
                    out=wob, in_=w_out[ot]))
                S.waits("t", [tk_wo, ynh_rd, ost_cons[0], ost_cons[1]])
                for k in range(NH):
                    for g2 in range(2):
                        S.op("t", lambda t, k=k, g2=g2, wob=wob: t.matmul(
                            pb[g2][:, 0:512], wob[:, k * P:(k + 1) * P],
                            ynh(k, g2 * 512, 512),
                            start=(k == 0), stop=(k == NH - 1)))
                wo_cons = S.now("t")
                for g2 in range(2):
                    tc0 = hc0 + g2 * 512
                    sbx = fs(o_xst, P, 512)
                    sbf = fs(o_fst2, P, 512)
                    S.waits("s", [xst_cons])
                    tk_x = S.dma("s", lambda s, ot=ot, tc0=tc0, sbx=sbx: s.dma_start(
                        out=sbx, in_=xh[ot * P:(ot + 1) * P, 3 + tc0:3 + tc0 + 512]))
                    S.wait("s", ffn_s)
                    tk_f = S.dma("s", lambda s, ot=ot, tc0=tc0, sbf=sbf: s.dma_start(
                        out=sbf, in_=ffnout[ot][:, tc0:tc0 + 512]))
                    ob = fs(o_ost[g2], P, 512)
                    S.waits("v", [wo_cons, tk_x, tk_f, pass2_v, ost_cons[g2]])
                    S.op("v", lambda v, g2=g2, tc0=tc0, ob=ob: v.tensor_tensor(
                        ob, pb[g2][:, 0:512], rstdB[:, tc0:tc0 + 512], ALU.mult))
                    S.op("v", lambda v, ob=ob, sbf=sbf: v.tensor_tensor(
                        ob, ob, sbf, ALU.add))
                    tk_o = S.op("v", lambda v, ob=ob, sbx=sbx: v.tensor_tensor(
                        ob, ob, sbx, ALU.add))
                    xst_cons = tk_o
                    S.wait("s", tk_o)
                    ost_cons[g2] = S.dma(
                        "s", lambda s, ot=ot, tc0=tc0, ob=ob: s.dma_start(
                            out=out[ot][:, tc0:tc0 + 512], in_=ob))
        S.phase()
        if PHASES < len(S.marks):
            S.rollback(PHASES)
        S.wait("s", S.now("s", "d"))
        for st in ["t", "v", "a", "g"]:
            S.wait("s", S.now(st, "o"))
        S.wait("t", S.now("s", "d"))
        S.wait("g", S.now("g", "d"))
        S.wait("a", S.now("a", "d"))

        S.replay(block, {"t": "tensor", "v": "vector", "a": "scalar",
                         "g": "gpsimd", "s": "sync"}, osem, dsem)
    return nc


_CACHE = {}


def kernel(L0_attn_hidden_states, L0_attn_W_in, L0_attn_conv_w, L0_attn_dt_bias,
           L0_attn_A_log, L0_attn_D_param, L0_attn_norm_weight, L0_attn_W_out,
           L0_ffn_W_base, L0_ffn_b_base, L0_ffn_W_router, L0_ffn_A, L0_ffn_B):
    x = np.asarray(L0_attn_hidden_states, dtype=np.float32)
    W_in = np.asarray(L0_attn_W_in, dtype=np.float32)
    conv_w = np.asarray(L0_attn_conv_w, dtype=np.float32)
    dt_bias = np.asarray(L0_attn_dt_bias, dtype=np.float32)
    A_log = np.asarray(L0_attn_A_log, dtype=np.float32)
    D_param = np.asarray(L0_attn_D_param, dtype=np.float32)
    norm_w = np.asarray(L0_attn_norm_weight, dtype=np.float32)
    W_out = np.asarray(L0_attn_W_out, dtype=np.float32)
    W_base = np.asarray(L0_ffn_W_base, dtype=np.float32)
    b_base = np.asarray(L0_ffn_b_base, dtype=np.float32)
    W_router = np.asarray(L0_ffn_W_router, dtype=np.float32)
    A = np.asarray(L0_ffn_A, dtype=np.float32)
    B = np.asarray(L0_ffn_B, dtype=np.float32)

    w_in_p = np.ascontiguousarray(
        np.pad(W_in, ((0, 0), (0, PROJ_PAD - PROJ_DIM)))
        .reshape(KC, P, NPT, P).transpose(2, 1, 0, 3).reshape(NPT, P, H))
    w_out_p = np.ascontiguousarray(
        W_out.reshape(64, P, 32, P).transpose(2, 1, 0, 3).reshape(32, P, DSSM))
    w_base_p = np.ascontiguousarray(
        W_base.T.reshape(KC, P, 32, P).transpose(2, 1, 0, 3).reshape(32, P, H))
    acat_p = np.ascontiguousarray(
        A.reshape(P, H).T.reshape(KC, P, P).transpose(1, 0, 2).reshape(P, H))
    bcat_p = np.ascontiguousarray(B.transpose(0, 2, 1).reshape(P, H))
    wrl_p = np.ascontiguousarray(
        W_router.T.reshape(KC, P, 8).transpose(1, 0, 2).reshape(P, 256))
    convw_p = np.ascontiguousarray(
        conv_w.reshape(66, P, 4).transpose(1, 0, 2).reshape(P, 264))
    dtb_p = np.ascontiguousarray(dt_bias.reshape(64, 1))
    alog_p = np.ascontiguousarray(A_log.reshape(64, 1))
    dprep_p = np.ascontiguousarray(np.broadcast_to(D_param[None, :], (P, 64)))
    gamma_p = np.ascontiguousarray(norm_w.reshape(64, P).T)
    bb_p = np.ascontiguousarray(b_base.reshape(32, P).T)
    e8_p = np.zeros((8, P), dtype=ml_dtypes.bfloat16)
    for e_ in range(8):
        e8_p[e_, e_ * 16:(e_ + 1) * 16] = 2.0
    identf_p = np.eye(P, dtype=np.float32)
    identb_p = np.eye(P, dtype=ml_dtypes.bfloat16)

    if "prog" not in _CACHE:
        _CACHE["prog"] = build()
    nc = _CACHE["prog"]

    in_maps = []
    for c in range(8):
        b_i, s_i = c // 4, c % 4
        t0 = s_i * T
        if s_i == 0:
            xs = np.concatenate(
                [np.zeros((3, H), dtype=np.float32), x[b_i, 0:T]], axis=0)
        else:
            xs = x[b_i, t0 - 3:t0 + T]
        xh_p = np.ascontiguousarray(xs.T)
        am = np.zeros((64, 12), dtype=np.float32)
        for slot in range(3):
            q = s_i - 1 - slot
            if q >= 0:
                am[:, slot * 4 + q] = 1.0
        in_maps.append({
            "xh": xh_p, "w_in": w_in_p, "w_out": w_out_p, "w_base": w_base_p,
            "acat": acat_p, "bcat": bcat_p, "wrl": wrl_p, "convw": convw_p,
            "dtbias": dtb_p, "alog": alog_p, "dprep": dprep_p,
            "gammah": gamma_p, "bbase": bb_p, "e8": e8_p,
            "identf": identf_p, "identb": identb_p, "amask": am,
        })

    res = run_bass_kernel_spmd(nc, in_maps, core_ids=list(range(8)))
    outp = np.empty((2, 8192, H), dtype=np.float32)
    for c in range(8):
        b_i, s_i = c // 4, c % 4
        oc = np.asarray(res.results[c]["out"], dtype=np.float32)
        outp[b_i, s_i * T:(s_i + 1) * T] = oc.reshape(H, T).T
    if TAPS:
        kernel.last_results = res
    return outp
